# version 01
# baseline (speedup 1.0000x reference)
import contextlib
import numpy as np
import ml_dtypes
import concourse.bass as bass
import concourse.mybir as mybir
from concourse.bass_utils import run_bass_kernel_spmd

F32 = mybir.dt.float32
BF16 = mybir.dt.bfloat16
U8 = mybir.dt.uint8
AF = mybir.ActivationFunctionType
ALU = mybir.AluOpType
AX = mybir.AxisListType
BF = ml_dtypes.bfloat16

NC = 8
B = 2
T = 2048
TT = B * T
D = 1024
DI = 2048
NHEAD = 16
HD = 64
DS = 16
DTR = 64
NE = 8
HID = 2048


class Prog:
    ENGINES = ("sync", "vector", "scalar", "tensor", "gpsimd")

    def __init__(self, nc):
        self.nc = nc
        self.ndma = 0
        self.ncmp = 0
        self.items = {e: [] for e in self.ENGINES}

    def op(self, eng, fn):
        self.items[eng].append((self.ndma, self.ncmp, fn, "cmp"))
        self.ncmp += 1

    def dma(self, eng, out, in_, **kw):
        def fn(E, out=out, in_=in_, kw=kw):
            return E.dma_start(out=out, in_=in_, **kw)
        self.items[eng].append((self.ndma, self.ncmp, fn, "dma"))
        self.ndma += 16

    def finalize(self, block, dmasem, cmpsem):
        tot_d, tot_c = self.ndma, self.ncmp
        for eng in self.ENGINES:
            lst = self.items[eng]

            def make_section(lst=lst):
                def section(E):
                    last_d = last_c = -1
                    for (wd, wc, fn, kind) in lst:
                        if wd > 0 and wd != last_d:
                            E.wait_ge(dmasem, wd)
                            last_d = wd
                        if wc > 0 and wc != last_c:
                            E.wait_ge(cmpsem, wc)
                            last_c = wc
                        r = fn(E)
                        if kind == "dma":
                            r.then_inc(dmasem, 16)
                        else:
                            r.then_inc(cmpsem, 1)
                    if tot_d > 0:
                        E.wait_ge(dmasem, tot_d)
                    if tot_c > 0:
                        E.wait_ge(cmpsem, tot_c)
                return section
            getattr(block, eng)(make_section())


def new_nc():
    return bass.Bass("TRN2", target_bir_lowering=False)


def _run(build_fn, per_core_maps, cache={}):
    key = build_fn.__name__
    if key not in cache:
        cache[key] = build_fn()
    nc = cache[key]
    res = run_bass_kernel_spmd(nc, per_core_maps, core_ids=list(range(len(per_core_maps))))
    return res.results


def finish(nc, prog, es):
    dmasem = es.enter_context(nc.semaphore("dmas"))
    cmpsem = es.enter_context(nc.semaphore("cmps"))
    block = es.enter_context(nc.Block())
    prog.finalize(block, dmasem, cmpsem)
    es.close()


def build_s1():
    nc = new_nc()
    XTM = nc.dram_tensor("XTM", [TT, D], F32, kind="ExternalInput").ap()
    identf = nc.dram_tensor("identf", [128, 128], F32, kind="ExternalInput").ap()
    XF = nc.dram_tensor("XF", [D, TT], F32, kind="ExternalOutput").ap()
    XNF = nc.dram_tensor("XNF", [D, TT], F32, kind="ExternalOutput").ap()
    XNB = nc.dram_tensor("XNB", [D, TT], BF16, kind="ExternalOutput").ap()

    es = contextlib.ExitStack()
    sb = lambda n, s, d: es.enter_context(nc.sbuf_tensor(n, s, d))
    ps = lambda n, s, d: es.enter_context(nc.psum_tensor(n, s, d))
    xt = sb("xt", [128, D], F32)
    xsq = sb("xsq", [128, D], F32)
    xn = sb("xn", [128, D], F32)
    idf = sb("idf", [128, 128], F32)
    s1 = sb("s1", [128, 1], F32)
    s2 = sb("s2", [128, 1], F32)
    mean = sb("mean", [128, 1], F32)
    ex2 = sb("ex2", [128, 1], F32)
    msq = sb("msq", [128, 1], F32)
    var = sb("var", [128, 1], F32)
    rstd = sb("rstd", [128, 1], F32)
    epsc = sb("epsc", [128, 1], F32)
    mr = sb("mr", [128, 1], F32)
    ev = sb("ev", [128, 128], F32)
    evb = sb("evb", [128, 128], BF16)
    pT = ps("pT", [128, 128], F32)

    p = Prog(nc)
    p.op("vector", lambda v: v.memset(epsc[:], 1e-5))
    p.dma("sync", idf[:], identf)
    for tt in range(TT // 128):
        p.dma("sync", xt[:], XTM[tt * 128:(tt + 1) * 128, :])
        p.op("vector", lambda v: v.tensor_reduce(s1[:], xt[:], AX.X, ALU.add))
        p.op("scalar", lambda s: s.activation(xsq[:], xt[:], AF.Square))
        p.op("vector", lambda v: v.tensor_reduce(s2[:], xsq[:], AX.X, ALU.add))
        p.op("vector", lambda v: v.tensor_scalar(mean[:], s1[:], 1.0 / D, None, ALU.mult))
        p.op("vector", lambda v: v.tensor_scalar(ex2[:], s2[:], 1.0 / D, None, ALU.mult))
        p.op("scalar", lambda s: s.activation(msq[:], mean[:], AF.Square))
        p.op("vector", lambda v: v.tensor_tensor(var[:], ex2[:], msq[:], ALU.subtract))
        p.op("scalar", lambda s: s.activation(var[:], var[:], AF.Ln, bias=epsc[:]))
        p.op("scalar", lambda s: s.activation(rstd[:], var[:], AF.Exp, scale=-0.5))
        p.op("vector", lambda v: v.tensor_tensor(mr[:], mean[:], rstd[:], ALU.mult))
        p.op("vector", lambda v: v.tensor_scalar(xn[:], xt[:], rstd[:], mr[:], ALU.mult, ALU.subtract))
        for dc in range(D // 128):
            csl = slice(dc * 128, (dc + 1) * 128)
            p.op("tensor", lambda t, csl=csl: t.transpose(pT[:], xn[:, csl], idf[:]))
            p.op("vector", lambda v: v.tensor_copy(ev[:], pT[:]))
            p.op("scalar", lambda s: s.copy(evb[:], pT[:]))
            p.dma("sync", XNF[dc * 128:(dc + 1) * 128, tt * 128:(tt + 1) * 128], ev[:])
            p.dma("sync", XNB[dc * 128:(dc + 1) * 128, tt * 128:(tt + 1) * 128], evb[:])
            p.op("tensor", lambda t, csl=csl: t.transpose(pT[:], xt[:, csl], idf[:]))
            p.op("vector", lambda v: v.tensor_copy(ev[:], pT[:]))
            p.dma("sync", XF[dc * 128:(dc + 1) * 128, tt * 128:(tt + 1) * 128], ev[:])
    finish(nc, p, es)
    return nc


def build_s2():
    nc = new_nc()
    XNB = nc.dram_tensor("XNB", [D, TT], BF16, kind="ExternalInput").ap()
    WQKV = nc.dram_tensor("WQKV", [D, 1152], F32, kind="ExternalInput").ap()
    BQ = nc.dram_tensor("BQ", [9, 128, 1], F32, kind="ExternalInput").ap()
    identb = nc.dram_tensor("identb", [128, 128], BF16, kind="ExternalInput").ap()
    QH = nc.dram_tensor("QH", [B, 64, NHEAD * T], BF16, kind="ExternalOutput").ap()
    KB = nc.dram_tensor("KB", [B, 64, T], BF16, kind="ExternalOutput").ap()
    VA = nc.dram_tensor("VA", [B, 128, 16 * 65], BF16, kind="ExternalOutput").ap()

    es = contextlib.ExitStack()
    sb = lambda n, s, d: es.enter_context(nc.sbuf_tensor(n, s, d))
    ps = lambda n, s, d: es.enter_context(nc.psum_tensor(n, s, d))
    wq = [sb(f"wq{i}", [128, 1152], BF16) for i in range(8)]
    bq = sb("bq", [128, 9], F32)
    xnb = [sb(f"xnb{i}", [128, 512], BF16) for i in range(8)]
    idb = sb("idb", [128, 128], BF16)
    qtmp = sb("qtmp", [64, 512], BF16)
    ktmp = sb("ktmp", [64, 512], BF16)
    vtmp = sb("vtmp", [64, 512], BF16)
    vat = sb("vat", [128, 64], BF16)
    onec = sb("onec", [128, 1], BF16)
    P1 = ps("P1", [128, 512], F32)
    PT = ps("PT", [128, 128], BF16)

    p = Prog(nc)
    p.dma("sync", idb[:], identb)
    p.op("vector", lambda v: v.memset(onec[:], 1.0))
    for i in range(8):
        p.dma("gpsimd", wq[i][:], WQKV[i * 128:(i + 1) * 128, :])
    for i in range(9):
        p.dma("sync", bq[:, i:i + 1], BQ[i])
    for b in range(B):
        for kt in range(16):
            p.dma("sync", VA[b, :, kt * 65 + 64:kt * 65 + 65], onec[:], allow_slow_non_contiguous=True)
    for tc in range(8):
        b, tloc = tc // 4, (tc % 4) * 512
        csl = slice(tc * 512, (tc + 1) * 512)
        for i in range(8):
            p.dma("sync", xnb[i][:], XNB[i * 128:(i + 1) * 128, csl])
        for oc in range(9):
            for kc in range(8):
                p.op("tensor", lambda t, oc=oc, kc=kc: t.matmul(
                    P1[:], wq[kc][:, oc * 128:(oc + 1) * 128], xnb[kc][:],
                    start=(kc == 0), stop=(kc == 7)))
            if oc < 8:
                for half in range(2):
                    h = 2 * oc + half
                    hs = slice(half * 64, half * 64 + 64)
                    p.op("vector", lambda v, hs=hs, oc=oc: v.tensor_scalar(
                        qtmp[:], P1[hs, :], bq[hs, oc:oc + 1], 0.125, ALU.add, ALU.mult))
                    p.dma("sync", QH[b, :, h * T + tloc:h * T + tloc + 512], qtmp[:])
            else:
                p.op("vector", lambda v: v.tensor_scalar(
                    ktmp[:], P1[0:64, :], bq[0:64, 8:9], None, ALU.add))
                p.dma("sync", KB[b, :, tloc:tloc + 512], ktmp[:])
                p.op("vector", lambda v: v.tensor_scalar(
                    vtmp[:], P1[64:128, :], bq[64:128, 8:9], None, ALU.add))
                for i in range(4):
                    kt = (tloc // 128) + i
                    p.op("tensor", lambda t, i=i: t.transpose(PT[:, 0:64], vtmp[:, i * 128:(i + 1) * 128], idb[0:64, 0:64]))
                    p.op("vector", lambda v: v.tensor_copy(vat[:], PT[:, 0:64]))
                    p.dma("sync", VA[b, :, kt * 65:kt * 65 + 64], vat[:])
    finish(nc, p, es)
    return nc


def build_s3():
    nc = new_nc()
    QH = nc.dram_tensor("QH", [B, 64, NHEAD * T], BF16, kind="ExternalInput").ap()
    KB = nc.dram_tensor("KB", [B, 64, T], BF16, kind="ExternalInput").ap()
    VA = nc.dram_tensor("VA", [B, 128, 16 * 65], BF16, kind="ExternalInput").ap()
    WAO = nc.dram_tensor("WAO", [D, D], F32, kind="ExternalInput").ap()
    BAO = nc.dram_tensor("BAO", [8, 128, 1], F32, kind="ExternalInput").ap()
    XNF = nc.dram_tensor("XNF", [D, TT], F32, kind="ExternalInput").ap()
    MASKC = nc.dram_tensor("MASKC", [128, 4 * 512], BF16, kind="ExternalInput").ap()
    ONESF = nc.dram_tensor("ONESF", [1, 128], F32, kind="ExternalInput").ap()
    X1F = nc.dram_tensor("X1F", [D, TT], F32, kind="ExternalOutput").ap()
    X1B = nc.dram_tensor("X1B", [D, TT], BF16, kind="ExternalOutput").ap()

    es = contextlib.ExitStack()
    sb = lambda n, s, d: es.enter_context(nc.sbuf_tensor(n, s, d))
    ps = lambda n, s, d: es.enter_context(nc.psum_tensor(n, s, d))
    qh = sb("qh", [64, NHEAD * T], BF16)
    kb = sb("kb", [64, T], BF16)
    va = sb("va", [128, 16 * 65], BF16)
    wao = [sb(f"wao{i}", [128, D], BF16) for i in range(8)]
    bao = sb("bao", [128, 8], F32)
    maskc = sb("maskc", [128, 4 * 512], BF16)
    onesf = sb("onesf", [1, 128], F32)
    onesb64 = sb("onesb64", [64, 1], BF16)
    sqb = sb("sqb", [64, 2048], BF16)
    sums = sb("sums", [1, 2048], F32)
    red = sb("red", [1, 1], F32)
    qmax2 = sb("qmax2", [1, 1], F32)
    kmax2 = sb("kmax2", [1, 1], F32)
    bmag = sb("bmag", [1, 1], F32)
    negb = sb("negb", [1, 1], F32)
    negbc = sb("negbc", [128, 1], F32)
    et = sb("et", [128, 512], BF16)
    et2 = sb("et2", [128, 512], BF16)
    rc = sb("rc", [1, 512], F32)
    rcb = sb("rcb", [64, 512], F32)
    ohx = sb("ohx", [64, 512], BF16)
    ofm = sb("ofm", [128, 8 * T], BF16)
    xattn = sb("xattn", [128, 512], F32)
    xnt = sb("xnt", [128, 512], F32)
    x1t = sb("x1t", [128, 512], F32)
    x1b = sb("x1b", [128, 512], BF16)
    PS = ps("PS", [128, 512], F32)
    PO = ps("PO", [65, 512], F32)
    PB = ps("PB", [64, 512], F32)
    PN = ps("PN", [1, 2048], F32)

    p = Prog(nc)
    p.dma("sync", maskc[:], MASKC)
    p.dma("sync", onesf[:], ONESF)
    p.op("vector", lambda v: v.memset(onesb64[:], 1.0))
    for i in range(8):
        p.dma("gpsimd", wao[i][:], WAO[i * 128:(i + 1) * 128, :])
    for i in range(8):
        p.dma("sync", bao[:, i:i + 1], BAO[i])
    for b in range(B):
        p.dma("sync", qh[:], QH[b])
        p.dma("sync", kb[:], KB[b])
        p.dma("sync", va[:], VA[b])
        p.op("scalar", lambda s: s.activation(sqb[:], kb[:], AF.Square))
        for c in range(4):
            p.op("tensor", lambda t, c=c: t.matmul(PN[:, c * 512:(c + 1) * 512], onesb64[:], sqb[:, c * 512:(c + 1) * 512], start=True, stop=True))
        p.op("vector", lambda v: v.tensor_copy(sums[:], PN[:]))
        p.op("vector", lambda v: v.tensor_reduce(kmax2[:], sums[:], AX.X, ALU.max))
        p.op("vector", lambda v: v.memset(qmax2[:], 0.0))
        for blk in range(16):
            p.op("scalar", lambda s, blk=blk: s.activation(sqb[:], qh[:, blk * 2048:(blk + 1) * 2048], AF.Square))
            for c in range(4):
                p.op("tensor", lambda t, c=c: t.matmul(
                    PN[:, c * 512:(c + 1) * 512], onesb64[:],
                    sqb[:, c * 512:(c + 1) * 512], start=True, stop=True))
            p.op("vector", lambda v: v.tensor_copy(sums[:], PN[:]))
            p.op("vector", lambda v: v.tensor_reduce(red[:], sums[:], AX.X, ALU.max))
            p.op("vector", lambda v: v.tensor_tensor(qmax2[:], qmax2[:], red[:], ALU.max))
        p.op("vector", lambda v: v.tensor_tensor(bmag[:], qmax2[:], kmax2[:], ALU.mult))
        p.op("scalar", lambda s: s.activation(bmag[:], bmag[:], AF.Sqrt))
        p.op("vector", lambda v: v.tensor_scalar(negb[:], bmag[:], -0.1300, None, ALU.mult))
        p.op("tensor", lambda t: t.matmul(PS[:, 0:1], onesf[:], negb[:], start=True, stop=True))
        p.op("vector", lambda v: v.tensor_copy(negbc[:], PS[:, 0:1]))
        for h in range(NHEAD):
            for qc in range(4):
                q512 = qh[:, h * T + qc * 512: h * T + (qc + 1) * 512]
                nkt = 4 * qc + 4
                for kt in range(nkt):
                    p.op("tensor", lambda t, kt=kt, q512=q512: t.matmul(
                        PS[:], kb[:, kt * 128:(kt + 1) * 128], q512, start=True, stop=True))
                    p.op("scalar", lambda s: s.activation(et[:], PS[:], AF.Exp, bias=negbc[:]))
                    if kt >= 4 * qc:
                        rr = kt - 4 * qc
                        p.op("vector", lambda v, rr=rr: v.tensor_tensor(
                            et2[:], et[:], maskc[:, rr * 512:(rr + 1) * 512], ALU.mult))
                        src = et2
                    else:
                        src = et
                    p.op("tensor", lambda t, kt=kt, src=src, nkt=nkt: t.matmul(
                        PO[:], va[:, kt * 65:kt * 65 + 65], src[:],
                        start=(kt == 0), stop=(kt == nkt - 1)))
                p.op("vector", lambda v: v.reciprocal(rc[:], PO[64:65, :]))
                p.op("tensor", lambda t: t.matmul(PB[:], onesf[:, 0:64], rc[:], start=True, stop=True))
                p.op("vector", lambda v: v.tensor_copy(rcb[:], PB[:]))
                p.op("vector", lambda v: v.tensor_tensor(ohx[:], PO[0:64, :], rcb[:], ALU.mult))
                dc, half = h // 2, h % 2
                p.op("vector", lambda v, dc=dc, half=half, qc=qc: v.tensor_copy(
                    ofm[half * 64:half * 64 + 64, dc * T + qc * 512:dc * T + (qc + 1) * 512], ohx[:]))
        for oc in range(8):
            for tc4 in range(4):
                for dc in range(8):
                    p.op("tensor", lambda t, oc=oc, tc4=tc4, dc=dc: t.matmul(
                        PS[:], wao[dc][:, oc * 128:(oc + 1) * 128],
                        ofm[:, dc * T + tc4 * 512:dc * T + (tc4 + 1) * 512],
                        start=(dc == 0), stop=(dc == 7)))
                gcols = slice(b * T + tc4 * 512, b * T + (tc4 + 1) * 512)
                p.op("vector", lambda v, oc=oc: v.tensor_scalar(
                    xattn[:], PS[:], bao[:, oc:oc + 1], None, ALU.add))
                p.dma("sync", xnt[:], XNF[oc * 128:(oc + 1) * 128, gcols])
                p.op("vector", lambda v: v.tensor_tensor(x1t[:], xattn[:], xnt[:], ALU.add))
                p.op("vector", lambda v: v.tensor_copy(x1b[:], x1t[:]))
                p.dma("sync", X1F[oc * 128:(oc + 1) * 128, gcols], x1t[:])
                p.dma("sync", X1B[oc * 128:(oc + 1) * 128, gcols], x1b[:])
    finish(nc, p, es)
    return nc


def build_s4():
    nc = new_nc()
    X1B = nc.dram_tensor("X1B", [D, TT], BF16, kind="ExternalInput").ap()
    WIN = nc.dram_tensor("WIN", [D, 2 * DI], F32, kind="ExternalInput").ap()
    CW = nc.dram_tensor("CW", [16, 128, 4], F32, kind="ExternalInput").ap()
    CB = nc.dram_tensor("CB", [16, 128, 1], F32, kind="ExternalInput").ap()
    WXP = nc.dram_tensor("WXP", [DI, 96], F32, kind="ExternalInput").ap()
    WDT = nc.dram_tensor("WDT", [DTR, DI], F32, kind="ExternalInput").ap()
    BDT = nc.dram_tensor("BDT", [16, 128, 1], F32, kind="ExternalInput").ap()
    UB = nc.dram_tensor("UB", [DI, TT], BF16, kind="ExternalOutput").ap()
    UF = nc.dram_tensor("UF", [DI, TT], F32, kind="ExternalOutput").ap()
    SRES = nc.dram_tensor("SRES", [DI, TT], F32, kind="ExternalOutput").ap()
    DLT = nc.dram_tensor("DLT", [DI, TT], F32, kind="ExternalOutput").ap()
    BM = nc.dram_tensor("BM", [B, DS, T], F32, kind="ExternalOutput").ap()
    CM = nc.dram_tensor("CM", [B, DS, T], F32, kind="ExternalOutput").ap()
    U0D = nc.dram_tensor("U0D", [DI, TT], F32, kind="Internal").ap()
    XDB = nc.dram_tensor("XDB", [DTR, TT], BF16, kind="Internal").ap()

    es = contextlib.ExitStack()
    sb = lambda n, s, d: es.enter_context(nc.sbuf_tensor(n, s, d))
    ps = lambda n, s, d: es.enter_context(nc.psum_tensor(n, s, d))
    win = [sb(f"win{i}", [128, 2 * DI], BF16) for i in range(8)]
    x1b = [sb(f"x1b{i}", [128, 512], BF16) for i in range(8)]
    ev = sb("ev", [128, 512], F32)
    u0p = sb("u0p", [128, T + 3], F32)
    acc = sb("acc", [128, T], F32)
    acc2 = sb("acc2", [128, T], F32)
    cw = sb("cw", [128, 4], F32)
    cb = sb("cb", [128, 1], F32)
    uf = sb("uf", [128, T], F32)
    ub = sb("ub", [128, T], BF16)
    wxp = [sb(f"wxp{i}", [128, 96], BF16) for i in range(16)]
    ubt = [sb(f"ubt{i}", [128, 512], BF16) for i in range(16)]
    xdbt = sb("xdbt", [64, 512], BF16)
    bct = sb("bct", [32, 512], F32)
    wdt = sb("wdt", [64, DI], BF16)
    bdt = sb("bdt", [128, 16], F32)
    ta = sb("ta", [128, 512], F32)
    tl = sb("tl", [128, 512], F32)
    tr = sb("tr", [128, 512], F32)
    dl = sb("dl", [128, 512], F32)
    P1 = ps("P1", [128, 512], F32)
    P96 = ps("P96", [96, 512], F32)

    p = Prog(nc)
    for i in range(8):
        p.dma("gpsimd", win[i][:], WIN[i * 128:(i + 1) * 128, :])
    for tc in range(8):
        csl = slice(tc * 512, (tc + 1) * 512)
        for i in range(8):
            p.dma("sync", x1b[i][:], X1B[i * 128:(i + 1) * 128, csl])
        for oc in range(32):
            for kc in range(8):
                p.op("tensor", lambda t, oc=oc, kc=kc: t.matmul(
                    P1[:], win[kc][:, oc * 128:(oc + 1) * 128], x1b[kc][:],
                    start=(kc == 0), stop=(kc == 7)))
            if oc < 16:
                p.op("vector", lambda v: v.tensor_copy(ev[:], P1[:]))
                p.dma("sync", U0D[oc * 128:(oc + 1) * 128, csl], ev[:])
            else:
                p.op("scalar", lambda s: s.activation(ev[:], P1[:], AF.Silu))
                p.dma("sync", SRES[(oc - 16) * 128:(oc - 15) * 128, csl], ev[:])
    for ct in range(16):
        p.dma("sync", cw[:], CW[ct])
        p.dma("sync", cb[:], CB[ct])
        for b in range(B):
            rs = slice(ct * 128, (ct + 1) * 128)
            cs = slice(b * T, (b + 1) * T)
            p.op("vector", lambda v: v.memset(u0p[:, 0:3], 0.0))
            p.dma("sync", u0p[:, 3:], U0D[rs, cs])
            p.op("vector", lambda v: v.tensor_scalar(acc[:], u0p[:, 3:3 + T], cw[:, 3:4], None, ALU.mult))
            p.op("vector", lambda v: v.scalar_tensor_tensor(acc2[:], u0p[:, 2:2 + T], cw[:, 2:3], acc[:], ALU.mult, ALU.add))
            p.op("vector", lambda v: v.scalar_tensor_tensor(acc[:], u0p[:, 1:1 + T], cw[:, 1:2], acc2[:], ALU.mult, ALU.add))
            p.op("vector", lambda v: v.scalar_tensor_tensor(acc2[:], u0p[:, 0:T], cw[:, 0:1], acc[:], ALU.mult, ALU.add))
            p.op("scalar", lambda s: s.activation(uf[:], acc2[:], AF.Silu, bias=cb[:]))
            p.op("vector", lambda v: v.tensor_copy(ub[:], uf[:]))
            p.dma("sync", UF[rs, cs], uf[:])
            p.dma("sync", UB[rs, cs], ub[:])
    for i in range(16):
        p.dma("gpsimd", wxp[i][:], WXP[i * 128:(i + 1) * 128, :])
    for tc in range(8):
        csl = slice(tc * 512, (tc + 1) * 512)
        for i in range(16):
            p.dma("sync", ubt[i][:], UB[i * 128:(i + 1) * 128, csl])
        for kc in range(16):
            p.op("tensor", lambda t, kc=kc: t.matmul(
                P96[:], wxp[kc][:], ubt[kc][:], start=(kc == 0), stop=(kc == 15)))
        p.op("vector", lambda v: v.tensor_copy(xdbt[:], P96[0:64, :]))
        p.dma("sync", XDB[:, csl], xdbt[:])
        p.op("vector", lambda v: v.tensor_copy(bct[:], P96[64:96, :]))
        b, tloc = tc // 4, (tc % 4) * 512
        p.dma("sync", BM[b, :, tloc:tloc + 512], bct[0:16, :])
        p.dma("sync", CM[b, :, tloc:tloc + 512], bct[16:32, :])
    p.dma("gpsimd", wdt[:], WDT)
    for i in range(16):
        p.dma("sync", bdt[:, i:i + 1], BDT[i])
    for tc in range(8):
        csl = slice(tc * 512, (tc + 1) * 512)
        p.dma("sync", xdbt[:], XDB[:, csl])
        for oc in range(16):
            p.op("tensor", lambda t, oc=oc: t.matmul(
                P1[:], wdt[:, oc * 128:(oc + 1) * 128], xdbt[:], start=True, stop=True))
            p.op("scalar", lambda s, oc=oc: s.activation(ta[:], P1[:], AF.Abs, bias=bdt[:, oc:oc + 1]))
            p.op("scalar", lambda s: s.activation(tl[:], ta[:], AF.Exp, scale=-1.0))
            p.op("scalar", lambda s: s.activation(tl[:], tl[:], AF.Ln, bias=1.0))
            p.op("scalar", lambda s, oc=oc: s.activation(tr[:], P1[:], AF.Relu, bias=bdt[:, oc:oc + 1]))
            p.op("vector", lambda v: v.tensor_tensor(dl[:], tl[:], tr[:], ALU.add))
            p.dma("sync", DLT[oc * 128:(oc + 1) * 128, csl], dl[:])
    finish(nc, p, es)
    return nc


def build_s5():
    nc = new_nc()
    DLT = nc.dram_tensor("DLT", [DI, TT], F32, kind="ExternalInput").ap()
    UF = nc.dram_tensor("UF", [DI, TT], F32, kind="ExternalInput").ap()
    SRES = nc.dram_tensor("SRES", [DI, TT], F32, kind="ExternalInput").ap()
    BM = nc.dram_tensor("BM", [B, DS, T], F32, kind="ExternalInput").ap()
    CM = nc.dram_tensor("CM", [B, DS, T], F32, kind="ExternalInput").ap()
    ALOG = nc.dram_tensor("ALOG", [16, 128, DS], F32, kind="ExternalInput").ap()
    DPV = nc.dram_tensor("DPV", [16, 128, 1], F32, kind="ExternalInput").ap()
    WOUT = nc.dram_tensor("WOUT", [DI, D], F32, kind="ExternalInput").ap()
    X1F = nc.dram_tensor("X1F", [D, TT], F32, kind="ExternalInput").ap()
    ONESF = nc.dram_tensor("ONESF", [1, 128], F32, kind="ExternalInput").ap()
    S0F = nc.dram_tensor("S0F", [D, TT], F32, kind="ExternalOutput").ap()

    es = contextlib.ExitStack()
    sb = lambda n, s, d: es.enter_context(nc.sbuf_tensor(n, s, d))
    ps = lambda n, s, d: es.enter_context(nc.psum_tensor(n, s, d))
    onesf = sb("onesf", [1, 128], F32)
    An = [sb(f"An{i}", [128, DS], F32) for i in range(16)]
    dpv = sb("dpv", [128, 16], F32)
    wo = [sb(f"wo{i}", [128, D], BF16) for i in range(16)]
    brow = sb("brow", [1, 512], F32)
    crow = sb("crow", [1, 512], F32)
    Bb = sb("Bb", [128, DS * 512], F32)
    Cb = sb("Cb", [128, DS * 512], F32)
    dltc = sb("dltc", [128, 512], F32)
    dudc = sb("dudc", [128, 512], F32)
    ufc = sb("ufc", [128, 512], F32)
    sresc = sb("sresc", [128, 512], F32)
    dA = sb("dA", [128, 512], F32)
    dbu = sb("dbu", [128, 512], F32)
    hbuf = sb("hbuf", [128, 512], F32)
    prod = sb("prod", [128, 512], F32)
    yc = sb("yc", [128, 512], F32)
    hstates = sb("hstates", [128, 16 * DS], F32)
    ygb = sb("ygb", [128, 16 * 512], BF16)
    x1t = sb("x1t", [128, 512], F32)
    s0t = sb("s0t", [128, 512], F32)
    PA = ps("PA", [128, 512], F32)
    PBp = ps("PBp", [128, 512], F32)

    p = Prog(nc)
    p.dma("sync", onesf[:], ONESF)
    for i in range(16):
        p.dma("sync", An[i][:], ALOG[i])
        p.op("scalar", lambda s, i=i: s.activation(An[i][:], An[i][:], AF.Exp))
        p.op("vector", lambda v, i=i: v.tensor_scalar(An[i][:], An[i][:], -1.0, None, ALU.mult))
        p.dma("sync", dpv[:, i:i + 1], DPV[i])
        p.dma("gpsimd", wo[i][:], WOUT[i * 128:(i + 1) * 128, :])
    for b in range(B):
        p.op("vector", lambda v: v.memset(hstates[:], 0.0))
        for TC in range(4):
            tsl = slice(b * T + TC * 512, b * T + (TC + 1) * 512)
            lsl = slice(TC * 512, (TC + 1) * 512)
            for n in range(DS):
                p.dma("sync", brow[:], BM[b, n:n + 1, lsl])
                p.dma("sync", crow[:], CM[b, n:n + 1, lsl])
                p.op("tensor", lambda t: t.matmul(PA[:], onesf[:], brow[:], start=True, stop=True))
                p.op("tensor", lambda t: t.matmul(PBp[:], onesf[:], crow[:], start=True, stop=True))
                p.op("vector", lambda v, n=n: v.tensor_copy(Bb[:, n * 512:(n + 1) * 512], PA[:]))
                p.op("vector", lambda v, n=n: v.tensor_copy(Cb[:, n * 512:(n + 1) * 512], PBp[:]))
            for ct in range(16):
                rs = slice(ct * 128, (ct + 1) * 128)
                p.dma("sync", dltc[:], DLT[rs, tsl])
                p.dma("sync", ufc[:], UF[rs, tsl])
                p.dma("sync", sresc[:], SRES[rs, tsl])
                p.op("vector", lambda v: v.tensor_tensor(dudc[:], dltc[:], ufc[:], ALU.mult))
                for n in range(DS):
                    nsl = slice(n * 512, (n + 1) * 512)
                    hs = slice(ct * DS + n, ct * DS + n + 1)
                    p.op("scalar", lambda s, n=n, ct=ct: s.activation(
                        dA[:], dltc[:], AF.Exp, scale=An[ct][:, n:n + 1]))
                    p.op("vector", lambda v, nsl=nsl: v.tensor_tensor(dbu[:], dudc[:], Bb[:, nsl], ALU.mult))
                    p.op("vector", lambda v, hs=hs: v.tensor_tensor_scan(
                        hbuf[:], dA[:], dbu[:], hstates[:, hs], ALU.mult, ALU.add))
                    p.op("vector", lambda v, hs=hs: v.tensor_copy(hstates[:, hs], hbuf[:, 511:512]))
                    if n == 0:
                        p.op("vector", lambda v, nsl=nsl: v.tensor_tensor(yc[:], hbuf[:], Cb[:, nsl], ALU.mult))
                    else:
                        p.op("vector", lambda v, nsl=nsl: v.tensor_tensor(prod[:], hbuf[:], Cb[:, nsl], ALU.mult))
                        p.op("vector", lambda v: v.tensor_tensor(yc[:], yc[:], prod[:], ALU.add))
                p.op("vector", lambda v, ct=ct: v.scalar_tensor_tensor(
                    yc[:], ufc[:], dpv[:, ct:ct + 1], yc[:], ALU.mult, ALU.add))
                p.op("vector", lambda v, ct=ct: v.tensor_tensor(
                    ygb[:, ct * 512:(ct + 1) * 512], yc[:], sresc[:], ALU.mult))
            for oc in range(8):
                for ct in range(16):
                    p.op("tensor", lambda t, oc=oc, ct=ct: t.matmul(
                        PA[:], wo[ct][:, oc * 128:(oc + 1) * 128],
                        ygb[:, ct * 512:(ct + 1) * 512], start=(ct == 0), stop=(ct == 15)))
                p.dma("sync", x1t[:], X1F[oc * 128:(oc + 1) * 128, tsl])
                p.op("vector", lambda v: v.tensor_tensor(s0t[:], PA[:], x1t[:], ALU.add))
                p.dma("sync", S0F[oc * 128:(oc + 1) * 128, tsl], s0t[:])
    finish(nc, p, es)
    return nc


def build_s6():
    nc = new_nc()
    XNB = nc.dram_tensor("XNB", [D, TT], BF16, kind="ExternalInput").ap()
    WF = nc.dram_tensor("WF", [D, D], F32, kind="ExternalInput").ap()
    FB = nc.dram_tensor("FB", [8, 128, 1], F32, kind="ExternalInput").ap()
    VR = nc.dram_tensor("VR", [D, 32], F32, kind="ExternalInput").ap()
    CT = nc.dram_tensor("CT", [32, D], F32, kind="ExternalInput").ap()
    S0F = nc.dram_tensor("S0F", [D, TT], F32, kind="ExternalInput").ap()
    XF = nc.dram_tensor("XF", [D, TT], F32, kind="ExternalInput").ap()
    X2F = nc.dram_tensor("X2F", [D, TT], F32, kind="ExternalOutput").ap()
    X2B = nc.dram_tensor("X2B", [D, TT], BF16, kind="ExternalOutput").ap()

    es = contextlib.ExitStack()
    sb = lambda n, s, d: es.enter_context(nc.sbuf_tensor(n, s, d))
    ps = lambda n, s, d: es.enter_context(nc.psum_tensor(n, s, d))
    wf = [sb(f"wf{i}", [128, D], BF16) for i in range(8)]
    vr = [sb(f"vr{i}", [128, 32], BF16) for i in range(8)]
    ct = sb("ct", [32, D], BF16)
    fb = sb("fb", [128, 8], F32)
    xnb = [sb(f"xnb{i}", [128, 512], BF16) for i in range(8)]
    t1b = sb("t1b", [32, 512], BF16)
    gt = sb("gt", [128, 512], F32)
    s0t = sb("s0t", [128, 512], F32)
    xft = sb("xft", [128, 512], F32)
    x2t = sb("x2t", [128, 512], F32)
    x2b = sb("x2b", [128, 512], BF16)
    P1 = ps("P1", [128, 512], F32)
    P32 = ps("P32", [32, 512], F32)

    p = Prog(nc)
    for i in range(8):
        p.dma("gpsimd", wf[i][:], WF[i * 128:(i + 1) * 128, :])
        p.dma("gpsimd", vr[i][:], VR[i * 128:(i + 1) * 128, :])
        p.dma("sync", fb[:, i:i + 1], FB[i])
    p.dma("gpsimd", ct[:], CT)
    for tc in range(8):
        csl = slice(tc * 512, (tc + 1) * 512)
        for i in range(8):
            p.dma("sync", xnb[i][:], XNB[i * 128:(i + 1) * 128, csl])
        for kc in range(8):
            p.op("tensor", lambda t, kc=kc: t.matmul(
                P32[:], vr[kc][:], xnb[kc][:], start=(kc == 0), stop=(kc == 7)))
        p.op("vector", lambda v: v.tensor_copy(t1b[:], P32[:]))
        for oc in range(8):
            for kc in range(8):
                p.op("tensor", lambda t, oc=oc, kc=kc: t.matmul(
                    P1[:], wf[kc][:, oc * 128:(oc + 1) * 128], xnb[kc][:],
                    start=(kc == 0), stop=False))
            p.op("tensor", lambda t, oc=oc: t.matmul(
                P1[:], ct[:, oc * 128:(oc + 1) * 128], t1b[:], start=False, stop=True))
            p.op("scalar", lambda s, oc=oc: s.activation(gt[:], P1[:], AF.Silu, bias=fb[:, oc:oc + 1]))
            p.dma("sync", s0t[:], S0F[oc * 128:(oc + 1) * 128, csl])
            p.dma("sync", xft[:], XF[oc * 128:(oc + 1) * 128, csl])
            p.op("vector", lambda v: v.tensor_tensor(gt[:], gt[:], s0t[:], ALU.mult))
            p.op("vector", lambda v: v.tensor_tensor(x2t[:], gt[:], xft[:], ALU.add))
            p.op("vector", lambda v: v.tensor_copy(x2b[:], x2t[:]))
            p.dma("sync", X2F[oc * 128:(oc + 1) * 128, csl], x2t[:])
            p.dma("sync", X2B[oc * 128:(oc + 1) * 128, csl], x2b[:])
    finish(nc, p, es)
    return nc


def build_s7():
    nc = new_nc()
    X2F = nc.dram_tensor("X2F", [D, TT], F32, kind="ExternalInput").ap()
    identf = nc.dram_tensor("identf", [128, 128], F32, kind="ExternalInput").ap()
    XN2B = nc.dram_tensor("XN2B", [D, TT], BF16, kind="ExternalOutput").ap()

    es = contextlib.ExitStack()
    sb = lambda n, s, d: es.enter_context(nc.sbuf_tensor(n, s, d))
    ps = lambda n, s, d: es.enter_context(nc.psum_tensor(n, s, d))
    idf = sb("idf", [128, 128], F32)
    fmt = sb("fmt", [128, 128], F32)
    xtm = sb("xtm", [128, D], F32)
    xsq = sb("xsq", [128, D], F32)
    xn = sb("xn", [128, D], F32)
    s1 = sb("s1", [128, 1], F32)
    s2 = sb("s2", [128, 1], F32)
    mean = sb("mean", [128, 1], F32)
    ex2 = sb("ex2", [128, 1], F32)
    msq = sb("msq", [128, 1], F32)
    var = sb("var", [128, 1], F32)
    rstd = sb("rstd", [128, 1], F32)
    epsc = sb("epsc", [128, 1], F32)
    mr = sb("mr", [128, 1], F32)
    evb = sb("evb", [128, 128], BF16)
    pT = ps("pT", [128, 128], F32)

    p = Prog(nc)
    p.op("vector", lambda v: v.memset(epsc[:], 1e-5))
    p.dma("sync", idf[:], identf)
    for tt in range(TT // 128):
        for dc in range(8):
            p.dma("sync", fmt[:], X2F[dc * 128:(dc + 1) * 128, tt * 128:(tt + 1) * 128])
            p.op("tensor", lambda t: t.transpose(pT[:], fmt[:], idf[:]))
            p.op("vector", lambda v, dc=dc: v.tensor_copy(xtm[:, dc * 128:(dc + 1) * 128], pT[:]))
        p.op("vector", lambda v: v.tensor_reduce(s1[:], xtm[:], AX.X, ALU.add))
        p.op("scalar", lambda s: s.activation(xsq[:], xtm[:], AF.Square))
        p.op("vector", lambda v: v.tensor_reduce(s2[:], xsq[:], AX.X, ALU.add))
        p.op("vector", lambda v: v.tensor_scalar(mean[:], s1[:], 1.0 / D, None, ALU.mult))
        p.op("vector", lambda v: v.tensor_scalar(ex2[:], s2[:], 1.0 / D, None, ALU.mult))
        p.op("scalar", lambda s: s.activation(msq[:], mean[:], AF.Square))
        p.op("vector", lambda v: v.tensor_tensor(var[:], ex2[:], msq[:], ALU.subtract))
        p.op("scalar", lambda s: s.activation(var[:], var[:], AF.Ln, bias=epsc[:]))
        p.op("scalar", lambda s: s.activation(rstd[:], var[:], AF.Exp, scale=-0.5))
        p.op("vector", lambda v: v.tensor_tensor(mr[:], mean[:], rstd[:], ALU.mult))
        p.op("vector", lambda v: v.tensor_scalar(xn[:], xtm[:], rstd[:], mr[:], ALU.mult, ALU.subtract))
        for dc in range(8):
            p.op("tensor", lambda t, dc=dc: t.transpose(pT[:], xn[:, dc * 128:(dc + 1) * 128], idf[:]))
            p.op("scalar", lambda s: s.copy(evb[:], pT[:]))
            p.dma("sync", XN2B[dc * 128:(dc + 1) * 128, tt * 128:(tt + 1) * 128], evb[:])
    finish(nc, p, es)
    return nc


def build_s8():
    nc = new_nc()
    XN2B = nc.dram_tensor("XN2B", [D, TT], BF16, kind="ExternalInput").ap()
    WQKVG = nc.dram_tensor("WQKVG", [D, 193], F32, kind="ExternalInput").ap()
    WO1 = nc.dram_tensor("WO1", [64, D], F32, kind="ExternalInput").ap()
    BO = nc.dram_tensor("BO", [8, 128, 1], F32, kind="ExternalInput").ap()
    BG1 = nc.dram_tensor("BG1", [1, 1], F32, kind="ExternalInput").ap()
    DCP = nc.dram_tensor("DCP", [1, 1], F32, kind="ExternalInput").ap()
    IOTAM = nc.dram_tensor("IOTAM", [128, 128], F32, kind="ExternalInput").ap()
    TRIU = nc.dram_tensor("TRIU", [128, 128], F32, kind="ExternalInput").ap()
    IOTP1 = nc.dram_tensor("IOTP1", [128, 1], F32, kind="ExternalInput").ap()
    IOT127 = nc.dram_tensor("IOT127", [128, 1], F32, kind="ExternalInput").ap()
    C128 = nc.dram_tensor("C128", [1, 1], F32, kind="ExternalInput").ap()
    ONESF = nc.dram_tensor("ONESF", [1, 128], F32, kind="ExternalInput").ap()
    identb = nc.dram_tensor("identb", [128, 128], BF16, kind="ExternalInput").ap()
    S1F = nc.dram_tensor("S1F", [D, TT], F32, kind="ExternalOutput").ap()

    es = contextlib.ExitStack()
    sb = lambda n, s, d: es.enter_context(nc.sbuf_tensor(n, s, d))
    ps = lambda n, s, d: es.enter_context(nc.psum_tensor(n, s, d))
    wqg = [sb(f"wqg{i}", [128, 193], BF16) for i in range(8)]
    wo1 = sb("wo1", [64, D], BF16)
    bo = sb("bo", [128, 8], F32)
    bg1 = sb("bg1", [1, 1], F32)
    dcp = sb("dcp", [1, 1], F32)
    iotam = sb("iotam", [128, 128], F32)
    triu = sb("triu", [128, 128], F32)
    iotp1 = sb("iotp1", [128, 1], F32)
    iot127 = sb("iot127", [128, 1], F32)
    c128 = sb("c128", [1, 1], F32)
    onesf = sb("onesf", [1, 128], F32)
    idb = sb("idb", [128, 128], BF16)
    dval = sb("dval", [1, 1], F32)
    ldv = sb("ldv", [1, 1], F32)
    ldcol = sb("ldcol", [128, 1], F32)
    Mb = sb("Mb", [128, 128], BF16)
    Mf = sb("Mf", [128, 128], F32)
    apow = sb("apow", [128, 1], F32)
    bpow = sb("bpow", [128, 1], F32)
    d128 = sb("d128", [1, 1], F32)
    d128c = sb("d128c", [64, 1], F32)
    xnb = [sb(f"xnb{i}", [128, 512], BF16) for i in range(8)]
    qf = sb("qf", [64, T], BF16)
    kf = sb("kf", [64, T], BF16)
    vf = sb("vf", [64, T], BF16)
    gr = sb("gr", [1, T], F32)
    gball = sb("gball", [64, T], F32)
    gkb = sb("gkb", [64, T], BF16)
    gvb = sb("gvb", [64, T], BF16)
    gktm = sb("gktm", [128, 16 * 64], BF16)
    gvtm = sb("gvtm", [128, 16 * 64], BF16)
    Sst = sb("Sst", [64, 64], F32)
    S2b = sb("S2b", [64, 64], F32)
    Sbf = sb("Sbf", [64, 64], BF16)
    o1 = sb("o1", [128, 64], F32)
    amd = sb("amd", [128, 128], BF16)
    otm = sb("otm", [128, 16 * 64], BF16)
    ofmb = sb("ofmb", [64, T], BF16)
    s1t = sb("s1t", [128, 512], F32)
    PA = ps("PA", [128, 512], F32)
    PB65 = ps("PB65", [65, 512], F32)
    PC = ps("PC", [128, 128], F32)
    PTb = ps("PTb", [128, 128], BF16)

    p = Prog(nc)
    for i in range(8):
        p.dma("gpsimd", wqg[i][:], WQKVG[i * 128:(i + 1) * 128, :])
    p.dma("gpsimd", wo1[:], WO1)
    for i in range(8):
        p.dma("sync", bo[:, i:i + 1], BO[i])
    for src, dst in [(BG1, bg1), (DCP, dcp), (IOTAM, iotam), (TRIU, triu),
                     (IOTP1, iotp1), (IOT127, iot127), (C128, c128), (ONESF, onesf)]:
        p.dma("sync", dst[:], src)
    p.dma("gpsimd", idb[:], identb)
    p.op("scalar", lambda s: s.activation(dval[:], dcp[:], AF.Sigmoid))
    p.op("vector", lambda v: v.tensor_scalar(dval[:], dval[:], 0.099, 0.9, ALU.mult, ALU.add))
    p.op("scalar", lambda s: s.activation(ldv[:], dval[:], AF.Ln))
    p.op("tensor", lambda t: t.matmul(PC[:, 0:1], onesf[:], ldv[:], start=True, stop=True))
    p.op("vector", lambda v: v.tensor_copy(ldcol[:], PC[:, 0:1]))
    p.op("scalar", lambda s: s.activation(Mf[:], iotam[:], AF.Exp, scale=ldcol[:]))
    p.op("vector", lambda v: v.tensor_tensor(Mf[:], Mf[:], triu[:], ALU.mult))
    p.op("vector", lambda v: v.tensor_copy(Mb[:], Mf[:]))
    p.op("scalar", lambda s: s.activation(apow[:], iotp1[:], AF.Exp, scale=ldcol[:]))
    p.op("scalar", lambda s: s.activation(bpow[:], iot127[:], AF.Exp, scale=ldcol[:]))
    p.op("scalar", lambda s: s.activation(d128[:], c128[:], AF.Exp, scale=ldv[:]))
    p.op("tensor", lambda t: t.matmul(PC[0:64, 0:1], onesf[:, 0:64], d128[:], start=True, stop=True))
    p.op("vector", lambda v: v.tensor_copy(d128c[:], PC[0:64, 0:1]))
    for b in range(B):
        for tc in range(4):
            csl = slice(b * T + tc * 512, b * T + (tc + 1) * 512)
            lsl = slice(tc * 512, (tc + 1) * 512)
            for i in range(8):
                p.dma("sync", xnb[i][:], XN2B[i * 128:(i + 1) * 128, csl])
            for kc in range(8):
                p.op("tensor", lambda t, kc=kc: t.matmul(
                    PA[:], wqg[kc][:, 0:128], xnb[kc][:], start=(kc == 0), stop=(kc == 7)))
            for kc in range(8):
                p.op("tensor", lambda t, kc=kc: t.matmul(
                    PB65[:], wqg[kc][:, 128:193], xnb[kc][:], start=(kc == 0), stop=(kc == 7)))
            p.op("vector", lambda v, lsl=lsl: v.tensor_copy(qf[:, lsl], PA[0:64, :]))
            p.op("vector", lambda v, lsl=lsl: v.tensor_copy(kf[:, lsl], PA[64:128, :]))
            p.op("vector", lambda v, lsl=lsl: v.tensor_copy(vf[:, lsl], PB65[0:64, :]))
            p.op("scalar", lambda s, lsl=lsl: s.activation(gr[:, lsl], PB65[64:65, :], AF.Sigmoid, bias=bg1[:]))
        for tc in range(4):
            lsl = slice(tc * 512, (tc + 1) * 512)
            p.op("tensor", lambda t, lsl=lsl: t.matmul(PB65[0:64, :], onesf[:, 0:64], gr[:, lsl], start=True, stop=True))
            p.op("vector", lambda v, lsl=lsl: v.tensor_copy(gball[:, lsl], PB65[0:64, :]))
        p.op("vector", lambda v: v.tensor_tensor(gkb[:], kf[:], gball[:], ALU.mult))
        p.op("vector", lambda v: v.tensor_tensor(gvb[:], vf[:], gball[:], ALU.mult))
        for ch in range(16):
            cs = slice(ch * 128, (ch + 1) * 128)
            os_ = slice(ch * 64, (ch + 1) * 64)
            p.op("tensor", lambda t, cs=cs: t.transpose(PTb[:, 0:64], gkb[:, cs], idb[0:64, 0:64]))
            p.op("vector", lambda v, os_=os_: v.tensor_scalar(gktm[:, os_], PTb[:, 0:64], bpow[:], None, ALU.mult))
            p.op("tensor", lambda t, cs=cs: t.transpose(PTb[:, 0:64], gvb[:, cs], idb[0:64, 0:64]))
            p.op("vector", lambda v, os_=os_: v.tensor_copy(gvtm[:, os_], PTb[:, 0:64]))
        p.op("vector", lambda v: v.memset(Sst[:], 0.0))
        for ch in range(16):
            cs = slice(ch * 128, (ch + 1) * 128)
            os_ = slice(ch * 64, (ch + 1) * 64)
            p.op("vector", lambda v: v.tensor_copy(Sbf[:], Sst[:]))
            p.op("tensor", lambda t, cs=cs: t.matmul(PC[:, 0:64], qf[:, cs], Sbf[:], start=True, stop=True))
            p.op("vector", lambda v: v.tensor_scalar(o1[:], PC[:, 0:64], apow[:], None, ALU.mult))
            p.op("tensor", lambda t, cs=cs: t.matmul(PC[:], gkb[:, cs], qf[:, cs], start=True, stop=True))
            p.op("vector", lambda v: v.tensor_tensor(amd[:], PC[:], Mb[:], ALU.mult))
            p.op("tensor", lambda t, os_=os_: t.matmul(PA[:, 0:64], amd[:], gvtm[:, os_], start=True, stop=True))
            p.op("vector", lambda v, os_=os_: v.tensor_tensor(otm[:, os_], PA[:, 0:64], o1[:], ALU.add))
            p.op("vector", lambda v: v.tensor_scalar(S2b[:], Sst[:], d128c[:], None, ALU.mult))
            p.op("tensor", lambda t, os_=os_: t.matmul(PC[0:64, 0:64], gktm[:, os_], gvtm[:, os_], start=True, stop=True))
            p.op("vector", lambda v: v.tensor_tensor(Sst[:], S2b[:], PC[0:64, 0:64], ALU.add))
        for ch in range(16):
            cs = slice(ch * 128, (ch + 1) * 128)
            os_ = slice(ch * 64, (ch + 1) * 64)
            p.op("tensor", lambda t, os_=os_: t.transpose(PTb[0:64, :], otm[:, os_], idb[:]))
            p.op("vector", lambda v, cs=cs: v.tensor_copy(ofmb[:, cs], PTb[0:64, :]))
        for tc in range(4):
            csl = slice(b * T + tc * 512, b * T + (tc + 1) * 512)
            lsl = slice(tc * 512, (tc + 1) * 512)
            for oc in range(8):
                p.op("tensor", lambda t, oc=oc, lsl=lsl: t.matmul(
                    PA[:], wo1[:, oc * 128:(oc + 1) * 128], ofmb[:, lsl], start=True, stop=True))
                p.op("vector", lambda v, oc=oc: v.tensor_scalar(s1t[:], PA[:], bo[:, oc:oc + 1], None, ALU.add))
                p.dma("sync", S1F[oc * 128:(oc + 1) * 128, csl], s1t[:])
    finish(nc, p, es)
    return nc


def build_s9():
    nc = new_nc()
    XN2B = nc.dram_tensor("XN2B", [D, TT], BF16, kind="ExternalInput").ap()
    X2F = nc.dram_tensor("X2F", [D, TT], F32, kind="ExternalInput").ap()
    S1F = nc.dram_tensor("S1F", [D, TT], F32, kind="ExternalInput").ap()
    W1T = nc.dram_tensor("W1T", [NE, D, HID], F32, kind="ExternalInput").ap()
    W2T = nc.dram_tensor("W2T", [NE, HID, D], F32, kind="ExternalInput").ap()
    B1 = nc.dram_tensor("B1", [NE, 16, 128, 1], F32, kind="ExternalInput").ap()
    B2 = nc.dram_tensor("B2", [NE, 8, 128, 1], F32, kind="ExternalInput").ap()
    WGT = nc.dram_tensor("WGT", [D, NE], F32, kind="ExternalInput").ap()
    BGM = nc.dram_tensor("BGM", [NE, 1], F32, kind="ExternalInput").ap()
    IOTA8 = nc.dram_tensor("IOTA8", [128, 8], F32, kind="ExternalInput").ap()
    IOTA8P = nc.dram_tensor("IOTA8P", [128, 8], F32, kind="ExternalInput").ap()
    SEL = nc.dram_tensor("SEL", [8, 8 * 128], F32, kind="ExternalInput").ap()
    identf = nc.dram_tensor("identf", [128, 128], F32, kind="ExternalInput").ap()
    OUT = nc.dram_tensor("OUT", [TT, D], F32, kind="ExternalOutput").ap()
    YACC = nc.dram_tensor("YACC", [D, TT], F32, kind="Internal").ap()

    es = contextlib.ExitStack()
    sb = lambda n, s, d: es.enter_context(nc.sbuf_tensor(n, s, d))
    ps = lambda n, s, d: es.enter_context(nc.psum_tensor(n, s, d))
    wgt = [sb(f"wgt{i}", [128, 8], BF16) for i in range(8)]
    bgm = sb("bgm", [8, 1], F32)
    iota8 = sb("iota8", [128, 8], F32)
    iota8p = sb("iota8p", [128, 8], F32)
    sel = sb("sel", [8, 8 * 128], F32)
    idf = sb("idf", [128, 128], F32)
    xnb = [sb(f"xnb{i}", [128, 512], BF16) for i in range(8)]
    lgf = sb("lgf", [8, 512], F32)
    lgt = sb("lgt", [128, 8], F32)
    m1 = sb("m1", [128, 1], F32)
    d1 = sb("d1", [128, 8], F32)
    ind1p = sb("ind1p", [128, 8], F32)
    penal = sb("penal", [128, 8], F32)
    idx = sb("idx", [128, 1], F32)
    ind1 = sb("ind1", [128, 8], F32)
    lg2 = sb("lg2", [128, 8], F32)
    m2 = sb("m2", [128, 1], F32)
    ind2 = sb("ind2", [128, 8], F32)
    wdiff = sb("wdiff", [128, 1], F32)
    w1c = sb("w1c", [128, 1], F32)
    w2c = sb("w2c", [128, 1], F32)
    comb = sb("comb", [128, 8], F32)
    combfm = sb("combfm", [8, TT], F32)
    w1tb = [sb(f"w1tb{i}", [128, HID], BF16) for i in range(8)]
    w2tb = [sb(f"w2tb{i}", [128, D], BF16) for i in range(16)]
    b1c = sb("b1c", [128, 16], F32)
    b2c = sb("b2c", [128, 8], F32)
    hba = sb("hba", [128, 16 * 512], BF16)
    cbet = sb("cbet", [128, 512], F32)
    yt = sb("yt", [128, 512], F32)
    ytw = sb("ytw", [128, 512], F32)
    ya = sb("ya", [128, 512], F32)
    g1t = sb("g1t", [128, 512], F32)
    s1t = sb("s1t", [128, 512], F32)
    x2t = sb("x2t", [128, 512], F32)
    x3t = sb("x3t", [128, 512], F32)
    ot = sb("ot", [128, 128], F32)
    P1 = ps("P1", [128, 512], F32)
    P8 = ps("P8", [8, 512], F32)
    Pcb = ps("Pcb", [128, 512], F32)
    PT = ps("PT", [128, 128], F32)

    p = Prog(nc)
    for i in range(8):
        p.dma("gpsimd", wgt[i][:], WGT[i * 128:(i + 1) * 128, :])
    for src, dst in [(BGM, bgm), (IOTA8, iota8), (IOTA8P, iota8p), (SEL, sel), (identf, idf)]:
        p.dma("sync", dst[:], src)
    for tc in range(8):
        csl = slice(tc * 512, (tc + 1) * 512)
        for i in range(8):
            p.dma("sync", xnb[i][:], XN2B[i * 128:(i + 1) * 128, csl])
        for kc in range(8):
            p.op("tensor", lambda t, kc=kc: t.matmul(
                P8[:], wgt[kc][:], xnb[kc][:], start=(kc == 0), stop=(kc == 7)))
        p.op("vector", lambda v: v.tensor_scalar(lgf[:], P8[:], bgm[:], None, ALU.add))
        for q in range(4):
            tt = tc * 4 + q
            p.op("tensor", lambda t, q=q: t.transpose(PT[:, 0:8], lgf[:, q * 128:(q + 1) * 128], idf[0:8, 0:8]))
            p.op("vector", lambda v: v.tensor_copy(lgt[:], PT[:, 0:8]))
            p.op("vector", lambda v: v.tensor_reduce(m1[:], lgt[:], AX.X, ALU.max))
            p.op("vector", lambda v: v.tensor_scalar(d1[:], lgt[:], m1[:], None, ALU.subtract))
            p.op("vector", lambda v: v.tensor_scalar(ind1p[:], d1[:], 0.0, None, ALU.is_ge))
            p.op("vector", lambda v: v.scalar_tensor_tensor(penal[:], ind1p[:], -1000.0, iota8p[:], ALU.mult, ALU.add))
            p.op("vector", lambda v: v.tensor_reduce(idx[:], penal[:], AX.X, ALU.min))
            p.op("vector", lambda v: v.tensor_scalar(ind1[:], iota8[:], idx[:], None, ALU.subtract))
            p.op("scalar", lambda s: s.activation(ind1[:], ind1[:], AF.Abs))
            p.op("scalar", lambda s: s.activation(ind1[:], ind1[:], AF.Relu, scale=-1.0, bias=1.0))
            p.op("vector", lambda v: v.scalar_tensor_tensor(lg2[:], ind1[:], -1e30, lgt[:], ALU.mult, ALU.add))
            p.op("vector", lambda v: v.tensor_reduce(m2[:], lg2[:], AX.X, ALU.max))
            p.op("vector", lambda v: v.tensor_scalar(d1[:], lg2[:], m2[:], None, ALU.subtract))
            p.op("vector", lambda v: v.tensor_scalar(ind1p[:], d1[:], 0.0, None, ALU.is_ge))
            p.op("vector", lambda v: v.scalar_tensor_tensor(penal[:], ind1p[:], -1000.0, iota8p[:], ALU.mult, ALU.add))
            p.op("vector", lambda v: v.tensor_reduce(idx[:], penal[:], AX.X, ALU.min))
            p.op("vector", lambda v: v.tensor_scalar(ind2[:], iota8[:], idx[:], None, ALU.subtract))
            p.op("scalar", lambda s: s.activation(ind2[:], ind2[:], AF.Abs))
            p.op("scalar", lambda s: s.activation(ind2[:], ind2[:], AF.Relu, scale=-1.0, bias=1.0))
            p.op("vector", lambda v: v.tensor_tensor(wdiff[:], m1[:], m2[:], ALU.subtract))
            p.op("scalar", lambda s: s.activation(w1c[:], wdiff[:], AF.Sigmoid))
            p.op("vector", lambda v: v.tensor_scalar(w2c[:], w1c[:], -1.0, 1.0, ALU.mult, ALU.add))
            p.op("vector", lambda v: v.tensor_scalar(comb[:], ind1[:], w1c[:], None, ALU.mult))
            p.op("vector", lambda v: v.scalar_tensor_tensor(comb[:], ind2[:], w2c[:], comb[:], ALU.mult, ALU.add))
            p.op("tensor", lambda t: t.transpose(PT[0:8, :], comb[:], idf[:]))
            p.op("vector", lambda v, tt=tt: v.tensor_copy(combfm[:, tt * 128:(tt + 1) * 128], PT[0:8, :]))
    for e in range(NE):
        for i in range(8):
            p.dma("gpsimd", w1tb[i][:], W1T[e, i * 128:(i + 1) * 128, :])
        for i in range(16):
            p.dma("gpsimd", w2tb[i][:], W2T[e, i * 128:(i + 1) * 128, :])
        for i in range(16):
            p.dma("sync", b1c[:, i:i + 1], B1[e, i])
        for i in range(8):
            p.dma("sync", b2c[:, i:i + 1], B2[e, i])
        for tc in range(8):
            csl = slice(tc * 512, (tc + 1) * 512)
            for i in range(8):
                p.dma("sync", xnb[i][:], XN2B[i * 128:(i + 1) * 128, csl])
            p.op("tensor", lambda t, e=e, csl=csl: t.matmul(
                Pcb[:], sel[:, e * 128:(e + 1) * 128], combfm[:, csl], start=True, stop=True))
            p.op("vector", lambda v: v.tensor_copy(cbet[:], Pcb[:]))
            for hc in range(16):
                for kc in range(8):
                    p.op("tensor", lambda t, hc=hc, kc=kc: t.matmul(
                        P1[:], w1tb[kc][:, hc * 128:(hc + 1) * 128], xnb[kc][:],
                        start=(kc == 0), stop=(kc == 7)))
                p.op("scalar", lambda s, hc=hc: s.activation(
                    hba[:, hc * 512:(hc + 1) * 512], P1[:], AF.Relu, bias=b1c[:, hc:hc + 1]))
            for oc in range(8):
                for hc in range(16):
                    p.op("tensor", lambda t, oc=oc, hc=hc: t.matmul(
                        P1[:], w2tb[hc][:, oc * 128:(oc + 1) * 128], hba[:, hc * 512:(hc + 1) * 512],
                        start=(hc == 0), stop=(hc == 15)))
                p.op("vector", lambda v, oc=oc: v.tensor_scalar(yt[:], P1[:], b2c[:, oc:oc + 1], None, ALU.add))
                p.op("vector", lambda v: v.tensor_tensor(ytw[:], yt[:], cbet[:], ALU.mult))
                if e == 0:
                    p.dma("gpsimd", YACC[oc * 128:(oc + 1) * 128, csl], ytw[:])
                else:
                    p.dma("gpsimd", YACC[oc * 128:(oc + 1) * 128, csl], ytw[:], accum_op=ALU.add)
    for tc in range(8):
        csl = slice(tc * 512, (tc + 1) * 512)
        for oc in range(8):
            p.dma("sync", ya[:], YACC[oc * 128:(oc + 1) * 128, csl])
            p.op("scalar", lambda s: s.activation(g1t[:], ya[:], AF.Silu))
            p.dma("sync", s1t[:], S1F[oc * 128:(oc + 1) * 128, csl])
            p.dma("sync", x2t[:], X2F[oc * 128:(oc + 1) * 128, csl])
            p.op("vector", lambda v: v.tensor_tensor(g1t[:], g1t[:], s1t[:], ALU.mult))
            p.op("vector", lambda v: v.tensor_tensor(x3t[:], g1t[:], x2t[:], ALU.add))
            for q in range(4):
                p.op("tensor", lambda t, q=q: t.transpose(PT[:], x3t[:, q * 128:(q + 1) * 128], idf[:]))
                p.op("vector", lambda v: v.tensor_copy(ot[:], PT[:]))
                p.dma("sync", OUT[tc * 512 + q * 128:tc * 512 + (q + 1) * 128, oc * 128:(oc + 1) * 128], ot[:])
    finish(nc, p, es)
    return nc


def _f32(a):
    return np.ascontiguousarray(a, dtype=np.float32)


def make_consts():
    c = {}
    c["identf"] = np.eye(128, dtype=np.float32)
    c["identb"] = np.eye(128, dtype=np.float32).astype(BF)
    m = np.zeros((128, 4 * 512), dtype=np.float32)
    kl = np.arange(128)[:, None]
    ql = np.arange(512)[None, :]
    for rr in range(4):
        m[:, rr * 512:(rr + 1) * 512] = (ql >= kl + 128 * rr)
    c["MASKC"] = m.astype(BF)
    c["ONESF"] = np.ones((1, 128), dtype=np.float32)
    c["IOTAM"] = (np.arange(128)[None, :] - np.arange(128)[:, None]).astype(np.float32)
    c["TRIU"] = (np.arange(128)[None, :] >= np.arange(128)[:, None]).astype(np.float32)
    c["IOTP1"] = (np.arange(128)[:, None] + 1).astype(np.float32)
    c["IOT127"] = (127 - np.arange(128)[:, None]).astype(np.float32)
    c["C128"] = np.full((1, 1), 128.0, dtype=np.float32)
    c["IOTA8"] = np.broadcast_to(np.arange(8, dtype=np.float32), (128, 8)).copy()
    c["IOTA8P"] = c["IOTA8"] + 1000.0
    sel = np.zeros((8, 8 * 128), dtype=np.float32)
    for e in range(8):
        sel[e, e * 128:(e + 1) * 128] = 1.0
    c["SEL"] = sel
    return c


def kernel(**inputs):
    cst = make_consts()
    x = _f32(inputs["L0_attn_x"])

    wqkv = _f32(inputs["L0_attn_wqkv"])
    bq = _f32(inputs["L0_attn_bqkv"]).reshape(9, 128, 1)
    wao = _f32(inputs["L0_attn_w_attn_out"])
    bao = _f32(inputs["L0_attn_b_attn_out"]).reshape(8, 128, 1)
    win = _f32(inputs["L0_attn_w_in"])
    cw = _f32(inputs["L0_attn_conv_w"]).reshape(16, 128, 4)
    cb = _f32(inputs["L0_attn_conv_b"]).reshape(16, 128, 1)
    wxp = _f32(inputs["L0_attn_w_xproj"])
    wdt = _f32(inputs["L0_attn_w_dt"])
    bdt = _f32(inputs["L0_attn_b_dt"]).reshape(16, 128, 1)
    alog = _f32(inputs["L0_attn_A_log"]).reshape(16, 128, DS)
    dpv = _f32(inputs["L0_attn_Dp"]).reshape(16, 128, 1)
    wout = _f32(inputs["L0_attn_w_out"])
    wf = _f32(inputs["L0_ffn_W"].T)
    fb = _f32(inputs["L0_ffn_b"]).reshape(8, 128, 1)
    vr = _f32(inputs["L0_ffn_V_r"])
    ctm = _f32(inputs["L0_ffn_C"].T)
    wqkvg = _f32(np.concatenate([inputs["L1_attn_Wq"], inputs["L1_attn_Wk"],
                                 inputs["L1_attn_Wv"], inputs["L1_attn_Wg"]], axis=1))
    wo1 = _f32(inputs["L1_attn_Wo"])
    bo = _f32(inputs["L1_attn_bo"]).reshape(8, 128, 1)
    bg1 = _f32(inputs["L1_attn_bg"]).reshape(1, 1)
    dcp = _f32(inputs["L1_attn_decay_param"]).reshape(1, 1)
    w1t = _f32(np.transpose(inputs["L1_ffn_W1"], (0, 2, 1)))
    w2t = _f32(np.transpose(inputs["L1_ffn_W2"], (0, 2, 1)))
    b1 = _f32(inputs["L1_ffn_b1"]).reshape(NE, 16, 128, 1)
    b2 = _f32(inputs["L1_ffn_b2"]).reshape(NE, 8, 128, 1)
    wgt = _f32(inputs["L1_ffn_Wg"].T)
    bgm = _f32(inputs["L1_ffn_bg"]).reshape(NE, 1)

    def shared(d):
        return [dict(d) for _ in range(NC)]

    maps = shared(dict(identf=cst["identf"]))
    for c in range(NC):
        maps[c]["XTM"] = np.ascontiguousarray(x[B * c:B * (c + 1)].reshape(TT, D))
    r1 = _run(build_s1, maps)
    maps = shared(dict(WQKV=wqkv, BQ=bq, identb=cst["identb"]))
    for c in range(NC):
        maps[c]["XNB"] = r1[c]["XNB"]
    r2 = _run(build_s2, maps)
    maps = shared(dict(WAO=wao, BAO=bao, MASKC=cst["MASKC"], ONESF=cst["ONESF"]))
    for c in range(NC):
        maps[c].update(QH=r2[c]["QH"], KB=r2[c]["KB"], VA=r2[c]["VA"], XNF=r1[c]["XNF"])
    r3 = _run(build_s3, maps)
    maps = shared(dict(WIN=win, CW=cw, CB=cb, WXP=wxp, WDT=wdt, BDT=bdt))
    for c in range(NC):
        maps[c]["X1B"] = r3[c]["X1B"]
    r4 = _run(build_s4, maps)
    maps = shared(dict(ALOG=alog, DPV=dpv, WOUT=wout, ONESF=cst["ONESF"]))
    for c in range(NC):
        maps[c].update(DLT=r4[c]["DLT"], UF=r4[c]["UF"], SRES=r4[c]["SRES"],
                       BM=r4[c]["BM"], CM=r4[c]["CM"], X1F=r3[c]["X1F"])
    r5 = _run(build_s5, maps)
    maps = shared(dict(WF=wf, FB=fb, VR=vr, CT=ctm))
    for c in range(NC):
        maps[c].update(XNB=r1[c]["XNB"], S0F=r5[c]["S0F"], XF=r1[c]["XF"])
    r6 = _run(build_s6, maps)
    maps = shared(dict(identf=cst["identf"]))
    for c in range(NC):
        maps[c]["X2F"] = r6[c]["X2F"]
    r7 = _run(build_s7, maps)
    maps = shared(dict(WQKVG=wqkvg, WO1=wo1, BO=bo, BG1=bg1, DCP=dcp,
                       IOTAM=cst["IOTAM"], TRIU=cst["TRIU"], IOTP1=cst["IOTP1"],
                       IOT127=cst["IOT127"], C128=cst["C128"], ONESF=cst["ONESF"],
                       identb=cst["identb"]))
    for c in range(NC):
        maps[c]["XN2B"] = r7[c]["XN2B"]
    r8 = _run(build_s8, maps)
    maps = shared(dict(W1T=w1t, W2T=w2t, B1=b1, B2=b2, WGT=wgt, BGM=bgm,
                       IOTA8=cst["IOTA8"], IOTA8P=cst["IOTA8P"], SEL=cst["SEL"],
                       identf=cst["identf"]))
    for c in range(NC):
        maps[c].update(XN2B=r7[c]["XN2B"], X2F=r6[c]["X2F"], S1F=r8[c]["S1F"])
    r9 = _run(build_s9, maps)

    out = np.empty((16, 2048, 1024), dtype=np.float32)
    for c in range(NC):
        out[B * c:B * (c + 1)] = np.asarray(r9[c]["OUT"], dtype=np.float32).reshape(B, T, D)
    return out


# version 02
# speedup vs baseline: 1.0103x; 1.0103x over previous
import contextlib
import numpy as np
import ml_dtypes
import concourse.bass as bass
import concourse.mybir as mybir
from concourse.bass_utils import run_bass_kernel_spmd

F32 = mybir.dt.float32
BF16 = mybir.dt.bfloat16
U8 = mybir.dt.uint8
AF = mybir.ActivationFunctionType
ALU = mybir.AluOpType
AX = mybir.AxisListType
BF = ml_dtypes.bfloat16

NC = 8
B = 2
T = 2048
TT = B * T
D = 1024
DI = 2048
NHEAD = 16
HD = 64
DS = 16
DTR = 64
NE = 8
HID = 2048


class Prog:
    ENGINES = ("sync", "vector", "scalar", "tensor", "gpsimd")

    def __init__(self, nc):
        self.nc = nc
        self.ndma = 0
        self.ncmp = 0
        self.items = {e: [] for e in self.ENGINES}

    def op(self, eng, fn):
        self.items[eng].append((self.ndma, self.ncmp, fn, "cmp"))
        self.ncmp += 1

    def dma(self, eng, out, in_, **kw):
        def fn(E, out=out, in_=in_, kw=kw):
            return E.dma_start(out=out, in_=in_, **kw)
        self.items[eng].append((self.ndma, self.ncmp, fn, "dma"))
        self.ndma += 16

    def finalize(self, block, dmasem, cmpsem):
        tot_d, tot_c = self.ndma, self.ncmp
        for eng in self.ENGINES:
            lst = self.items[eng]

            def make_section(lst=lst):
                def section(E):
                    last_d = last_c = -1
                    for (wd, wc, fn, kind) in lst:
                        if wd > 0 and wd != last_d:
                            E.wait_ge(dmasem, wd)
                            last_d = wd
                        if wc > 0 and wc != last_c:
                            E.wait_ge(cmpsem, wc)
                            last_c = wc
                        r = fn(E)
                        if kind == "dma":
                            r.then_inc(dmasem, 16)
                        else:
                            r.then_inc(cmpsem, 1)
                    if tot_d > 0:
                        E.wait_ge(dmasem, tot_d)
                    if tot_c > 0:
                        E.wait_ge(cmpsem, tot_c)
                return section
            getattr(block, eng)(make_section())


def new_nc():
    return bass.Bass("TRN2", target_bir_lowering=False)


def _run(build_fn, per_core_maps, cache={}):
    key = build_fn.__name__
    if key not in cache:
        cache[key] = build_fn()
    nc = cache[key]
    res = run_bass_kernel_spmd(nc, per_core_maps, core_ids=list(range(len(per_core_maps))))
    return res.results


def finish(nc, prog, es):
    dmasem = es.enter_context(nc.semaphore("dmas"))
    cmpsem = es.enter_context(nc.semaphore("cmps"))
    block = es.enter_context(nc.Block())
    prog.finalize(block, dmasem, cmpsem)
    es.close()


DRAM_SPECS = {
    "XTM": ([TT, D], F32),
    "identf": ([128, 128], F32),
    "XF": ([D, TT], F32),
    "XNF": ([D, TT], F32),
    "XNB": ([D, TT], BF16),
    "WQKV": ([D, 1152], F32),
    "BQ": ([9, 128, 1], F32),
    "identb": ([128, 128], BF16),
    "QH": ([B, 64, NHEAD * T], BF16),
    "KB": ([B, 64, T], BF16),
    "VA": ([B, 128, 16 * 65], BF16),
    "WAO": ([D, D], F32),
    "BAO": ([8, 128, 1], F32),
    "MASKC": ([128, 4 * 512], BF16),
    "ONESF": ([1, 128], F32),
    "X1F": ([D, TT], F32),
    "X1B": ([D, TT], BF16),
    "WIN": ([D, 2 * DI], F32),
    "CW": ([16, 128, 4], F32),
    "CB": ([16, 128, 1], F32),
    "WXP": ([DI, 96], F32),
    "WDT": ([DTR, DI], F32),
    "BDT": ([16, 128, 1], F32),
    "UB": ([DI, TT], BF16),
    "UF": ([DI, TT], F32),
    "SRES": ([DI, TT], F32),
    "DLT": ([DI, TT], F32),
    "BM": ([B, DS, T], F32),
    "CM": ([B, DS, T], F32),
    "U0D": ([DI, TT], F32),
    "XDB": ([DTR, TT], BF16),
    "ALOG": ([16, 128, DS], F32),
    "DPV": ([16, 128, 1], F32),
    "WOUT": ([DI, D], F32),
    "S0F": ([D, TT], F32),
    "WF": ([D, D], F32),
    "FB": ([8, 128, 1], F32),
    "VR": ([D, 32], F32),
    "CT": ([32, D], F32),
    "X2F": ([D, TT], F32),
    "X2B": ([D, TT], BF16),
    "XN2B": ([D, TT], BF16),
    "WQKVG": ([D, 193], F32),
    "WO1": ([64, D], F32),
    "BO": ([8, 128, 1], F32),
    "BG1": ([1, 1], F32),
    "DCP": ([1, 1], F32),
    "IOTAM": ([128, 128], F32),
    "TRIU": ([128, 128], F32),
    "IOTP1": ([128, 1], F32),
    "IOT127": ([128, 1], F32),
    "C128": ([1, 1], F32),
    "S1F": ([D, TT], F32),
    "W1T": ([NE, D, HID], F32),
    "W2T": ([NE, HID, D], F32),
    "B1": ([NE, 16, 128, 1], F32),
    "B2": ([NE, 8, 128, 1], F32),
    "WGT": ([D, NE], F32),
    "BGM": ([NE, 1], F32),
    "IOTA8": ([128, 8], F32),
    "IOTA8P": ([128, 8], F32),
    "SEL": ([8, 8 * 128], F32),
    "OUT": ([TT, D], F32),
    "YACC": ([D, TT], F32),
}

STAGE_DECLS = {
    "s1": [('XTM', 'ExternalInput'), ('identf', 'ExternalInput'), ('XF', 'ExternalOutput'), ('XNF', 'ExternalOutput'), ('XNB', 'ExternalOutput')],
    "s2": [('XNB', 'ExternalInput'), ('WQKV', 'ExternalInput'), ('BQ', 'ExternalInput'), ('identb', 'ExternalInput'), ('QH', 'ExternalOutput'), ('KB', 'ExternalOutput'), ('VA', 'ExternalOutput')],
    "s3": [('QH', 'ExternalInput'), ('KB', 'ExternalInput'), ('VA', 'ExternalInput'), ('WAO', 'ExternalInput'), ('BAO', 'ExternalInput'), ('XNF', 'ExternalInput'), ('MASKC', 'ExternalInput'), ('ONESF', 'ExternalInput'), ('X1F', 'ExternalOutput'), ('X1B', 'ExternalOutput')],
    "s4": [('X1B', 'ExternalInput'), ('WIN', 'ExternalInput'), ('CW', 'ExternalInput'), ('CB', 'ExternalInput'), ('WXP', 'ExternalInput'), ('WDT', 'ExternalInput'), ('BDT', 'ExternalInput'), ('UB', 'ExternalOutput'), ('UF', 'ExternalOutput'), ('SRES', 'ExternalOutput'), ('DLT', 'ExternalOutput'), ('BM', 'ExternalOutput'), ('CM', 'ExternalOutput'), ('U0D', 'Internal'), ('XDB', 'Internal')],
    "s5": [('DLT', 'ExternalInput'), ('UF', 'ExternalInput'), ('SRES', 'ExternalInput'), ('BM', 'ExternalInput'), ('CM', 'ExternalInput'), ('ALOG', 'ExternalInput'), ('DPV', 'ExternalInput'), ('WOUT', 'ExternalInput'), ('X1F', 'ExternalInput'), ('ONESF', 'ExternalInput'), ('S0F', 'ExternalOutput')],
    "s6": [('XNB', 'ExternalInput'), ('WF', 'ExternalInput'), ('FB', 'ExternalInput'), ('VR', 'ExternalInput'), ('CT', 'ExternalInput'), ('S0F', 'ExternalInput'), ('XF', 'ExternalInput'), ('X2F', 'ExternalOutput'), ('X2B', 'ExternalOutput')],
    "s7": [('X2F', 'ExternalInput'), ('identf', 'ExternalInput'), ('XN2B', 'ExternalOutput')],
    "s8": [('XN2B', 'ExternalInput'), ('WQKVG', 'ExternalInput'), ('WO1', 'ExternalInput'), ('BO', 'ExternalInput'), ('BG1', 'ExternalInput'), ('DCP', 'ExternalInput'), ('IOTAM', 'ExternalInput'), ('TRIU', 'ExternalInput'), ('IOTP1', 'ExternalInput'), ('IOT127', 'ExternalInput'), ('C128', 'ExternalInput'), ('ONESF', 'ExternalInput'), ('identb', 'ExternalInput'), ('S1F', 'ExternalOutput')],
    "s9": [('XN2B', 'ExternalInput'), ('X2F', 'ExternalInput'), ('S1F', 'ExternalInput'), ('W1T', 'ExternalInput'), ('W2T', 'ExternalInput'), ('B1', 'ExternalInput'), ('B2', 'ExternalInput'), ('WGT', 'ExternalInput'), ('BGM', 'ExternalInput'), ('IOTA8', 'ExternalInput'), ('IOTA8P', 'ExternalInput'), ('SEL', 'ExternalInput'), ('identf', 'ExternalInput'), ('OUT', 'ExternalOutput'), ('YACC', 'Internal')],
}

def emit_s1(nc, p, DR):
    XTM = DR["XTM"]
    identf = DR["identf"]
    XF = DR["XF"]
    XNF = DR["XNF"]
    XNB = DR["XNB"]

    es = contextlib.ExitStack()
    sb = lambda n, s, d: es.enter_context(nc.sbuf_tensor('s1_' + n, s, d))
    ps = lambda n, s, d: es.enter_context(nc.psum_tensor('s1_' + n, s, d))
    xt = sb("xt", [128, D], F32)
    xsq = sb("xsq", [128, D], F32)
    xn = sb("xn", [128, D], F32)
    idf = sb("idf", [128, 128], F32)
    s1 = sb("s1", [128, 1], F32)
    s2 = sb("s2", [128, 1], F32)
    mean = sb("mean", [128, 1], F32)
    ex2 = sb("ex2", [128, 1], F32)
    msq = sb("msq", [128, 1], F32)
    var = sb("var", [128, 1], F32)
    rstd = sb("rstd", [128, 1], F32)
    epsc = sb("epsc", [128, 1], F32)
    mr = sb("mr", [128, 1], F32)
    ev = sb("ev", [128, 128], F32)
    evb = sb("evb", [128, 128], BF16)
    pT = ps("pT", [128, 128], F32)

    p.op("vector", lambda v: v.memset(epsc[:], 1e-5))
    p.dma("sync", idf[:], identf)
    for tt in range(TT // 128):
        p.dma("sync", xt[:], XTM[tt * 128:(tt + 1) * 128, :])
        p.op("vector", lambda v: v.tensor_reduce(s1[:], xt[:], AX.X, ALU.add))
        p.op("scalar", lambda s: s.activation(xsq[:], xt[:], AF.Square))
        p.op("vector", lambda v: v.tensor_reduce(s2[:], xsq[:], AX.X, ALU.add))
        p.op("vector", lambda v: v.tensor_scalar(mean[:], s1[:], 1.0 / D, None, ALU.mult))
        p.op("vector", lambda v: v.tensor_scalar(ex2[:], s2[:], 1.0 / D, None, ALU.mult))
        p.op("scalar", lambda s: s.activation(msq[:], mean[:], AF.Square))
        p.op("vector", lambda v: v.tensor_tensor(var[:], ex2[:], msq[:], ALU.subtract))
        p.op("scalar", lambda s: s.activation(var[:], var[:], AF.Ln, bias=epsc[:]))
        p.op("scalar", lambda s: s.activation(rstd[:], var[:], AF.Exp, scale=-0.5))
        p.op("vector", lambda v: v.tensor_tensor(mr[:], mean[:], rstd[:], ALU.mult))
        p.op("vector", lambda v: v.tensor_scalar(xn[:], xt[:], rstd[:], mr[:], ALU.mult, ALU.subtract))
        for dc in range(D // 128):
            csl = slice(dc * 128, (dc + 1) * 128)
            p.op("tensor", lambda t, csl=csl: t.transpose(pT[:], xn[:, csl], idf[:]))
            p.op("vector", lambda v: v.tensor_copy(ev[:], pT[:]))
            p.op("scalar", lambda s: s.copy(evb[:], pT[:]))
            p.dma("sync", XNF[dc * 128:(dc + 1) * 128, tt * 128:(tt + 1) * 128], ev[:])
            p.dma("sync", XNB[dc * 128:(dc + 1) * 128, tt * 128:(tt + 1) * 128], evb[:])
            p.op("tensor", lambda t, csl=csl: t.transpose(pT[:], xt[:, csl], idf[:]))
            p.op("vector", lambda v: v.tensor_copy(ev[:], pT[:]))
            p.dma("sync", XF[dc * 128:(dc + 1) * 128, tt * 128:(tt + 1) * 128], ev[:])
    es.close()
def emit_s2(nc, p, DR):
    XNB = DR["XNB"]
    WQKV = DR["WQKV"]
    BQ = DR["BQ"]
    identb = DR["identb"]
    QH = DR["QH"]
    KB = DR["KB"]
    VA = DR["VA"]

    es = contextlib.ExitStack()
    sb = lambda n, s, d: es.enter_context(nc.sbuf_tensor('s2_' + n, s, d))
    ps = lambda n, s, d: es.enter_context(nc.psum_tensor('s2_' + n, s, d))
    wq = [sb(f"wq{i}", [128, 1152], BF16) for i in range(8)]
    bq = sb("bq", [128, 9], F32)
    xnb = [sb(f"xnb{i}", [128, 512], BF16) for i in range(8)]
    idb = sb("idb", [128, 128], BF16)
    qtmp = sb("qtmp", [64, 512], BF16)
    ktmp = sb("ktmp", [64, 512], BF16)
    vtmp = sb("vtmp", [64, 512], BF16)
    vat = sb("vat", [128, 64], BF16)
    onec = sb("onec", [128, 1], BF16)
    P1 = ps("P1", [128, 512], F32)
    PT = ps("PT", [128, 128], BF16)

    p.dma("sync", idb[:], identb)
    p.op("vector", lambda v: v.memset(onec[:], 1.0))
    for i in range(8):
        p.dma("gpsimd", wq[i][:], WQKV[i * 128:(i + 1) * 128, :])
    for i in range(9):
        p.dma("sync", bq[:, i:i + 1], BQ[i])
    for b in range(B):
        for kt in range(16):
            p.dma("sync", VA[b, :, kt * 65 + 64:kt * 65 + 65], onec[:], allow_slow_non_contiguous=True)
    for tc in range(8):
        b, tloc = tc // 4, (tc % 4) * 512
        csl = slice(tc * 512, (tc + 1) * 512)
        for i in range(8):
            p.dma("sync", xnb[i][:], XNB[i * 128:(i + 1) * 128, csl])
        for oc in range(9):
            for kc in range(8):
                p.op("tensor", lambda t, oc=oc, kc=kc: t.matmul(
                    P1[:], wq[kc][:, oc * 128:(oc + 1) * 128], xnb[kc][:],
                    start=(kc == 0), stop=(kc == 7)))
            if oc < 8:
                for half in range(2):
                    h = 2 * oc + half
                    hs = slice(half * 64, half * 64 + 64)
                    p.op("vector", lambda v, hs=hs, oc=oc: v.tensor_scalar(
                        qtmp[:], P1[hs, :], bq[hs, oc:oc + 1], 0.125, ALU.add, ALU.mult))
                    p.dma("sync", QH[b, :, h * T + tloc:h * T + tloc + 512], qtmp[:])
            else:
                p.op("vector", lambda v: v.tensor_scalar(
                    ktmp[:], P1[0:64, :], bq[0:64, 8:9], None, ALU.add))
                p.dma("sync", KB[b, :, tloc:tloc + 512], ktmp[:])
                p.op("vector", lambda v: v.tensor_scalar(
                    vtmp[:], P1[64:128, :], bq[64:128, 8:9], None, ALU.add))
                for i in range(4):
                    kt = (tloc // 128) + i
                    p.op("tensor", lambda t, i=i: t.transpose(PT[:, 0:64], vtmp[:, i * 128:(i + 1) * 128], idb[0:64, 0:64]))
                    p.op("vector", lambda v: v.tensor_copy(vat[:], PT[:, 0:64]))
                    p.dma("sync", VA[b, :, kt * 65:kt * 65 + 64], vat[:])
    es.close()
def emit_s3(nc, p, DR):
    QH = DR["QH"]
    KB = DR["KB"]
    VA = DR["VA"]
    WAO = DR["WAO"]
    BAO = DR["BAO"]
    XNF = DR["XNF"]
    MASKC = DR["MASKC"]
    ONESF = DR["ONESF"]
    X1F = DR["X1F"]
    X1B = DR["X1B"]

    es = contextlib.ExitStack()
    sb = lambda n, s, d: es.enter_context(nc.sbuf_tensor('s3_' + n, s, d))
    ps = lambda n, s, d: es.enter_context(nc.psum_tensor('s3_' + n, s, d))
    qh = sb("qh", [64, NHEAD * T], BF16)
    kb = sb("kb", [64, T], BF16)
    va = sb("va", [128, 16 * 65], BF16)
    wao = [sb(f"wao{i}", [128, D], BF16) for i in range(8)]
    bao = sb("bao", [128, 8], F32)
    maskc = sb("maskc", [128, 4 * 512], BF16)
    onesf = sb("onesf", [1, 128], F32)
    onesb64 = sb("onesb64", [64, 1], BF16)
    sqb = sb("sqb", [64, 2048], BF16)
    sums = sb("sums", [1, 2048], F32)
    red = sb("red", [1, 1], F32)
    qmax2 = sb("qmax2", [1, 1], F32)
    kmax2 = sb("kmax2", [1, 1], F32)
    bmag = sb("bmag", [1, 1], F32)
    negb = sb("negb", [1, 1], F32)
    negbc = sb("negbc", [128, 1], F32)
    et = sb("et", [128, 512], BF16)
    et2 = sb("et2", [128, 512], BF16)
    rc = sb("rc", [1, 512], F32)
    rcb = sb("rcb", [64, 512], F32)
    ohx = sb("ohx", [64, 512], BF16)
    ofm = sb("ofm", [128, 8 * T], BF16)
    xattn = sb("xattn", [128, 512], F32)
    xnt = sb("xnt", [128, 512], F32)
    x1t = sb("x1t", [128, 512], F32)
    x1b = sb("x1b", [128, 512], BF16)
    PS = ps("PS", [128, 512], F32)
    PO = ps("PO", [65, 512], F32)
    PB = ps("PB", [64, 512], F32)
    PN = ps("PN", [1, 2048], F32)

    p.dma("sync", maskc[:], MASKC)
    p.dma("sync", onesf[:], ONESF)
    p.op("vector", lambda v: v.memset(onesb64[:], 1.0))
    for i in range(8):
        p.dma("gpsimd", wao[i][:], WAO[i * 128:(i + 1) * 128, :])
    for i in range(8):
        p.dma("sync", bao[:, i:i + 1], BAO[i])
    for b in range(B):
        p.dma("sync", qh[:], QH[b])
        p.dma("sync", kb[:], KB[b])
        p.dma("sync", va[:], VA[b])
        p.op("scalar", lambda s: s.activation(sqb[:], kb[:], AF.Square))
        for c in range(4):
            p.op("tensor", lambda t, c=c: t.matmul(PN[:, c * 512:(c + 1) * 512], onesb64[:], sqb[:, c * 512:(c + 1) * 512], start=True, stop=True))
        p.op("vector", lambda v: v.tensor_copy(sums[:], PN[:]))
        p.op("vector", lambda v: v.tensor_reduce(kmax2[:], sums[:], AX.X, ALU.max))
        p.op("vector", lambda v: v.memset(qmax2[:], 0.0))
        for blk in range(16):
            p.op("scalar", lambda s, blk=blk: s.activation(sqb[:], qh[:, blk * 2048:(blk + 1) * 2048], AF.Square))
            for c in range(4):
                p.op("tensor", lambda t, c=c: t.matmul(
                    PN[:, c * 512:(c + 1) * 512], onesb64[:],
                    sqb[:, c * 512:(c + 1) * 512], start=True, stop=True))
            p.op("vector", lambda v: v.tensor_copy(sums[:], PN[:]))
            p.op("vector", lambda v: v.tensor_reduce(red[:], sums[:], AX.X, ALU.max))
            p.op("vector", lambda v: v.tensor_tensor(qmax2[:], qmax2[:], red[:], ALU.max))
        p.op("vector", lambda v: v.tensor_tensor(bmag[:], qmax2[:], kmax2[:], ALU.mult))
        p.op("scalar", lambda s: s.activation(bmag[:], bmag[:], AF.Sqrt))
        p.op("vector", lambda v: v.tensor_scalar(negb[:], bmag[:], -0.1300, None, ALU.mult))
        p.op("tensor", lambda t: t.matmul(PS[:, 0:1], onesf[:], negb[:], start=True, stop=True))
        p.op("vector", lambda v: v.tensor_copy(negbc[:], PS[:, 0:1]))
        for h in range(NHEAD):
            for qc in range(4):
                q512 = qh[:, h * T + qc * 512: h * T + (qc + 1) * 512]
                nkt = 4 * qc + 4
                for kt in range(nkt):
                    p.op("tensor", lambda t, kt=kt, q512=q512: t.matmul(
                        PS[:], kb[:, kt * 128:(kt + 1) * 128], q512, start=True, stop=True))
                    p.op("scalar", lambda s: s.activation(et[:], PS[:], AF.Exp, bias=negbc[:]))
                    if kt >= 4 * qc:
                        rr = kt - 4 * qc
                        p.op("vector", lambda v, rr=rr: v.tensor_tensor(
                            et2[:], et[:], maskc[:, rr * 512:(rr + 1) * 512], ALU.mult))
                        src = et2
                    else:
                        src = et
                    p.op("tensor", lambda t, kt=kt, src=src, nkt=nkt: t.matmul(
                        PO[:], va[:, kt * 65:kt * 65 + 65], src[:],
                        start=(kt == 0), stop=(kt == nkt - 1)))
                p.op("vector", lambda v: v.reciprocal(rc[:], PO[64:65, :]))
                p.op("tensor", lambda t: t.matmul(PB[:], onesf[:, 0:64], rc[:], start=True, stop=True))
                p.op("vector", lambda v: v.tensor_copy(rcb[:], PB[:]))
                p.op("vector", lambda v: v.tensor_tensor(ohx[:], PO[0:64, :], rcb[:], ALU.mult))
                dc, half = h // 2, h % 2
                p.op("vector", lambda v, dc=dc, half=half, qc=qc: v.tensor_copy(
                    ofm[half * 64:half * 64 + 64, dc * T + qc * 512:dc * T + (qc + 1) * 512], ohx[:]))
        for oc in range(8):
            for tc4 in range(4):
                for dc in range(8):
                    p.op("tensor", lambda t, oc=oc, tc4=tc4, dc=dc: t.matmul(
                        PS[:], wao[dc][:, oc * 128:(oc + 1) * 128],
                        ofm[:, dc * T + tc4 * 512:dc * T + (tc4 + 1) * 512],
                        start=(dc == 0), stop=(dc == 7)))
                gcols = slice(b * T + tc4 * 512, b * T + (tc4 + 1) * 512)
                p.op("vector", lambda v, oc=oc: v.tensor_scalar(
                    xattn[:], PS[:], bao[:, oc:oc + 1], None, ALU.add))
                p.dma("sync", xnt[:], XNF[oc * 128:(oc + 1) * 128, gcols])
                p.op("vector", lambda v: v.tensor_tensor(x1t[:], xattn[:], xnt[:], ALU.add))
                p.op("vector", lambda v: v.tensor_copy(x1b[:], x1t[:]))
                p.dma("sync", X1F[oc * 128:(oc + 1) * 128, gcols], x1t[:])
                p.dma("sync", X1B[oc * 128:(oc + 1) * 128, gcols], x1b[:])
    es.close()
def emit_s4(nc, p, DR):
    X1B = DR["X1B"]
    WIN = DR["WIN"]
    CW = DR["CW"]
    CB = DR["CB"]
    WXP = DR["WXP"]
    WDT = DR["WDT"]
    BDT = DR["BDT"]
    UB = DR["UB"]
    UF = DR["UF"]
    SRES = DR["SRES"]
    DLT = DR["DLT"]
    BM = DR["BM"]
    CM = DR["CM"]
    U0D = DR["U0D"]
    XDB = DR["XDB"]

    es = contextlib.ExitStack()
    sb = lambda n, s, d: es.enter_context(nc.sbuf_tensor('s4_' + n, s, d))
    ps = lambda n, s, d: es.enter_context(nc.psum_tensor('s4_' + n, s, d))
    win = [sb(f"win{i}", [128, 2 * DI], BF16) for i in range(8)]
    x1b = [sb(f"x1b{i}", [128, 512], BF16) for i in range(8)]
    ev = sb("ev", [128, 512], F32)
    u0p = sb("u0p", [128, T + 3], F32)
    acc = sb("acc", [128, T], F32)
    acc2 = sb("acc2", [128, T], F32)
    cw = sb("cw", [128, 4], F32)
    cb = sb("cb", [128, 1], F32)
    uf = sb("uf", [128, T], F32)
    ub = sb("ub", [128, T], BF16)
    wxp = [sb(f"wxp{i}", [128, 96], BF16) for i in range(16)]
    ubt = [sb(f"ubt{i}", [128, 512], BF16) for i in range(16)]
    xdbt = sb("xdbt", [64, 512], BF16)
    bct = sb("bct", [32, 512], F32)
    wdt = sb("wdt", [64, DI], BF16)
    bdt = sb("bdt", [128, 16], F32)
    ta = sb("ta", [128, 512], F32)
    tl = sb("tl", [128, 512], F32)
    tr = sb("tr", [128, 512], F32)
    dl = sb("dl", [128, 512], F32)
    P1 = ps("P1", [128, 512], F32)
    P96 = ps("P96", [96, 512], F32)

    for i in range(8):
        p.dma("gpsimd", win[i][:], WIN[i * 128:(i + 1) * 128, :])
    for tc in range(8):
        csl = slice(tc * 512, (tc + 1) * 512)
        for i in range(8):
            p.dma("sync", x1b[i][:], X1B[i * 128:(i + 1) * 128, csl])
        for oc in range(32):
            for kc in range(8):
                p.op("tensor", lambda t, oc=oc, kc=kc: t.matmul(
                    P1[:], win[kc][:, oc * 128:(oc + 1) * 128], x1b[kc][:],
                    start=(kc == 0), stop=(kc == 7)))
            if oc < 16:
                p.op("vector", lambda v: v.tensor_copy(ev[:], P1[:]))
                p.dma("sync", U0D[oc * 128:(oc + 1) * 128, csl], ev[:])
            else:
                p.op("scalar", lambda s: s.activation(ev[:], P1[:], AF.Silu))
                p.dma("sync", SRES[(oc - 16) * 128:(oc - 15) * 128, csl], ev[:])
    for ct in range(16):
        p.dma("sync", cw[:], CW[ct])
        p.dma("sync", cb[:], CB[ct])
        for b in range(B):
            rs = slice(ct * 128, (ct + 1) * 128)
            cs = slice(b * T, (b + 1) * T)
            p.op("vector", lambda v: v.memset(u0p[:, 0:3], 0.0))
            p.dma("sync", u0p[:, 3:], U0D[rs, cs])
            p.op("vector", lambda v: v.tensor_scalar(acc[:], u0p[:, 3:3 + T], cw[:, 3:4], None, ALU.mult))
            p.op("vector", lambda v: v.scalar_tensor_tensor(acc2[:], u0p[:, 2:2 + T], cw[:, 2:3], acc[:], ALU.mult, ALU.add))
            p.op("vector", lambda v: v.scalar_tensor_tensor(acc[:], u0p[:, 1:1 + T], cw[:, 1:2], acc2[:], ALU.mult, ALU.add))
            p.op("vector", lambda v: v.scalar_tensor_tensor(acc2[:], u0p[:, 0:T], cw[:, 0:1], acc[:], ALU.mult, ALU.add))
            p.op("scalar", lambda s: s.activation(uf[:], acc2[:], AF.Silu, bias=cb[:]))
            p.op("vector", lambda v: v.tensor_copy(ub[:], uf[:]))
            p.dma("sync", UF[rs, cs], uf[:])
            p.dma("sync", UB[rs, cs], ub[:])
    for i in range(16):
        p.dma("gpsimd", wxp[i][:], WXP[i * 128:(i + 1) * 128, :])
    for tc in range(8):
        csl = slice(tc * 512, (tc + 1) * 512)
        for i in range(16):
            p.dma("sync", ubt[i][:], UB[i * 128:(i + 1) * 128, csl])
        for kc in range(16):
            p.op("tensor", lambda t, kc=kc: t.matmul(
                P96[:], wxp[kc][:], ubt[kc][:], start=(kc == 0), stop=(kc == 15)))
        p.op("vector", lambda v: v.tensor_copy(xdbt[:], P96[0:64, :]))
        p.dma("sync", XDB[:, csl], xdbt[:])
        p.op("vector", lambda v: v.tensor_copy(bct[:], P96[64:96, :]))
        b, tloc = tc // 4, (tc % 4) * 512
        p.dma("sync", BM[b, :, tloc:tloc + 512], bct[0:16, :])
        p.dma("sync", CM[b, :, tloc:tloc + 512], bct[16:32, :])
    p.dma("gpsimd", wdt[:], WDT)
    for i in range(16):
        p.dma("sync", bdt[:, i:i + 1], BDT[i])
    for tc in range(8):
        csl = slice(tc * 512, (tc + 1) * 512)
        p.dma("sync", xdbt[:], XDB[:, csl])
        for oc in range(16):
            p.op("tensor", lambda t, oc=oc: t.matmul(
                P1[:], wdt[:, oc * 128:(oc + 1) * 128], xdbt[:], start=True, stop=True))
            p.op("scalar", lambda s, oc=oc: s.activation(ta[:], P1[:], AF.Abs, bias=bdt[:, oc:oc + 1]))
            p.op("scalar", lambda s: s.activation(tl[:], ta[:], AF.Exp, scale=-1.0))
            p.op("scalar", lambda s: s.activation(tl[:], tl[:], AF.Ln, bias=1.0))
            p.op("scalar", lambda s, oc=oc: s.activation(tr[:], P1[:], AF.Relu, bias=bdt[:, oc:oc + 1]))
            p.op("vector", lambda v: v.tensor_tensor(dl[:], tl[:], tr[:], ALU.add))
            p.dma("sync", DLT[oc * 128:(oc + 1) * 128, csl], dl[:])
    es.close()
def emit_s5(nc, p, DR):
    DLT = DR["DLT"]
    UF = DR["UF"]
    SRES = DR["SRES"]
    BM = DR["BM"]
    CM = DR["CM"]
    ALOG = DR["ALOG"]
    DPV = DR["DPV"]
    WOUT = DR["WOUT"]
    X1F = DR["X1F"]
    ONESF = DR["ONESF"]
    S0F = DR["S0F"]

    es = contextlib.ExitStack()
    sb = lambda n, s, d: es.enter_context(nc.sbuf_tensor('s5_' + n, s, d))
    ps = lambda n, s, d: es.enter_context(nc.psum_tensor('s5_' + n, s, d))
    onesf = sb("onesf", [1, 128], F32)
    An = [sb(f"An{i}", [128, DS], F32) for i in range(16)]
    dpv = sb("dpv", [128, 16], F32)
    wo = [sb(f"wo{i}", [128, D], BF16) for i in range(16)]
    brow = sb("brow", [1, 512], F32)
    crow = sb("crow", [1, 512], F32)
    Bb = sb("Bb", [128, DS * 512], F32)
    Cb = sb("Cb", [128, DS * 512], F32)
    dltc = sb("dltc", [128, 512], F32)
    dudc = sb("dudc", [128, 512], F32)
    ufc = sb("ufc", [128, 512], F32)
    sresc = sb("sresc", [128, 512], F32)
    dA = sb("dA", [128, 512], F32)
    dbu = sb("dbu", [128, 512], F32)
    hbuf = sb("hbuf", [128, 512], F32)
    prod = sb("prod", [128, 512], F32)
    yc = sb("yc", [128, 512], F32)
    hstates = sb("hstates", [128, 16 * DS], F32)
    ygb = sb("ygb", [128, 16 * 512], BF16)
    x1t = sb("x1t", [128, 512], F32)
    s0t = sb("s0t", [128, 512], F32)
    PA = ps("PA", [128, 512], F32)
    PBp = ps("PBp", [128, 512], F32)

    p.dma("sync", onesf[:], ONESF)
    for i in range(16):
        p.dma("sync", An[i][:], ALOG[i])
        p.op("scalar", lambda s, i=i: s.activation(An[i][:], An[i][:], AF.Exp))
        p.op("vector", lambda v, i=i: v.tensor_scalar(An[i][:], An[i][:], -1.0, None, ALU.mult))
        p.dma("sync", dpv[:, i:i + 1], DPV[i])
        p.dma("gpsimd", wo[i][:], WOUT[i * 128:(i + 1) * 128, :])
    for b in range(B):
        p.op("vector", lambda v: v.memset(hstates[:], 0.0))
        for TC in range(4):
            tsl = slice(b * T + TC * 512, b * T + (TC + 1) * 512)
            lsl = slice(TC * 512, (TC + 1) * 512)
            for n in range(DS):
                p.dma("sync", brow[:], BM[b, n:n + 1, lsl])
                p.dma("sync", crow[:], CM[b, n:n + 1, lsl])
                p.op("tensor", lambda t: t.matmul(PA[:], onesf[:], brow[:], start=True, stop=True))
                p.op("tensor", lambda t: t.matmul(PBp[:], onesf[:], crow[:], start=True, stop=True))
                p.op("vector", lambda v, n=n: v.tensor_copy(Bb[:, n * 512:(n + 1) * 512], PA[:]))
                p.op("vector", lambda v, n=n: v.tensor_copy(Cb[:, n * 512:(n + 1) * 512], PBp[:]))
            for ct in range(16):
                rs = slice(ct * 128, (ct + 1) * 128)
                p.dma("sync", dltc[:], DLT[rs, tsl])
                p.dma("sync", ufc[:], UF[rs, tsl])
                p.dma("sync", sresc[:], SRES[rs, tsl])
                p.op("vector", lambda v: v.tensor_tensor(dudc[:], dltc[:], ufc[:], ALU.mult))
                for n in range(DS):
                    nsl = slice(n * 512, (n + 1) * 512)
                    hs = slice(ct * DS + n, ct * DS + n + 1)
                    p.op("scalar", lambda s, n=n, ct=ct: s.activation(
                        dA[:], dltc[:], AF.Exp, scale=An[ct][:, n:n + 1]))
                    p.op("vector", lambda v, nsl=nsl: v.tensor_tensor(dbu[:], dudc[:], Bb[:, nsl], ALU.mult))
                    p.op("vector", lambda v, hs=hs: v.tensor_tensor_scan(
                        hbuf[:], dA[:], dbu[:], hstates[:, hs], ALU.mult, ALU.add))
                    p.op("vector", lambda v, hs=hs: v.tensor_copy(hstates[:, hs], hbuf[:, 511:512]))
                    if n == 0:
                        p.op("vector", lambda v, nsl=nsl: v.tensor_tensor(yc[:], hbuf[:], Cb[:, nsl], ALU.mult))
                    else:
                        p.op("vector", lambda v, nsl=nsl: v.tensor_tensor(prod[:], hbuf[:], Cb[:, nsl], ALU.mult))
                        p.op("vector", lambda v: v.tensor_tensor(yc[:], yc[:], prod[:], ALU.add))
                p.op("vector", lambda v, ct=ct: v.scalar_tensor_tensor(
                    yc[:], ufc[:], dpv[:, ct:ct + 1], yc[:], ALU.mult, ALU.add))
                p.op("vector", lambda v, ct=ct: v.tensor_tensor(
                    ygb[:, ct * 512:(ct + 1) * 512], yc[:], sresc[:], ALU.mult))
            for oc in range(8):
                for ct in range(16):
                    p.op("tensor", lambda t, oc=oc, ct=ct: t.matmul(
                        PA[:], wo[ct][:, oc * 128:(oc + 1) * 128],
                        ygb[:, ct * 512:(ct + 1) * 512], start=(ct == 0), stop=(ct == 15)))
                p.dma("sync", x1t[:], X1F[oc * 128:(oc + 1) * 128, tsl])
                p.op("vector", lambda v: v.tensor_tensor(s0t[:], PA[:], x1t[:], ALU.add))
                p.dma("sync", S0F[oc * 128:(oc + 1) * 128, tsl], s0t[:])
    es.close()
def emit_s6(nc, p, DR):
    XNB = DR["XNB"]
    WF = DR["WF"]
    FB = DR["FB"]
    VR = DR["VR"]
    CT = DR["CT"]
    S0F = DR["S0F"]
    XF = DR["XF"]
    X2F = DR["X2F"]
    X2B = DR["X2B"]

    es = contextlib.ExitStack()
    sb = lambda n, s, d: es.enter_context(nc.sbuf_tensor('s6_' + n, s, d))
    ps = lambda n, s, d: es.enter_context(nc.psum_tensor('s6_' + n, s, d))
    wf = [sb(f"wf{i}", [128, D], BF16) for i in range(8)]
    vr = [sb(f"vr{i}", [128, 32], BF16) for i in range(8)]
    ct = sb("ct", [32, D], BF16)
    fb = sb("fb", [128, 8], F32)
    xnb = [sb(f"xnb{i}", [128, 512], BF16) for i in range(8)]
    t1b = sb("t1b", [32, 512], BF16)
    gt = sb("gt", [128, 512], F32)
    s0t = sb("s0t", [128, 512], F32)
    xft = sb("xft", [128, 512], F32)
    x2t = sb("x2t", [128, 512], F32)
    x2b = sb("x2b", [128, 512], BF16)
    P1 = ps("P1", [128, 512], F32)
    P32 = ps("P32", [32, 512], F32)

    for i in range(8):
        p.dma("gpsimd", wf[i][:], WF[i * 128:(i + 1) * 128, :])
        p.dma("gpsimd", vr[i][:], VR[i * 128:(i + 1) * 128, :])
        p.dma("sync", fb[:, i:i + 1], FB[i])
    p.dma("gpsimd", ct[:], CT)
    for tc in range(8):
        csl = slice(tc * 512, (tc + 1) * 512)
        for i in range(8):
            p.dma("sync", xnb[i][:], XNB[i * 128:(i + 1) * 128, csl])
        for kc in range(8):
            p.op("tensor", lambda t, kc=kc: t.matmul(
                P32[:], vr[kc][:], xnb[kc][:], start=(kc == 0), stop=(kc == 7)))
        p.op("vector", lambda v: v.tensor_copy(t1b[:], P32[:]))
        for oc in range(8):
            for kc in range(8):
                p.op("tensor", lambda t, oc=oc, kc=kc: t.matmul(
                    P1[:], wf[kc][:, oc * 128:(oc + 1) * 128], xnb[kc][:],
                    start=(kc == 0), stop=False))
            p.op("tensor", lambda t, oc=oc: t.matmul(
                P1[:], ct[:, oc * 128:(oc + 1) * 128], t1b[:], start=False, stop=True))
            p.op("scalar", lambda s, oc=oc: s.activation(gt[:], P1[:], AF.Silu, bias=fb[:, oc:oc + 1]))
            p.dma("sync", s0t[:], S0F[oc * 128:(oc + 1) * 128, csl])
            p.dma("sync", xft[:], XF[oc * 128:(oc + 1) * 128, csl])
            p.op("vector", lambda v: v.tensor_tensor(gt[:], gt[:], s0t[:], ALU.mult))
            p.op("vector", lambda v: v.tensor_tensor(x2t[:], gt[:], xft[:], ALU.add))
            p.op("vector", lambda v: v.tensor_copy(x2b[:], x2t[:]))
            p.dma("sync", X2F[oc * 128:(oc + 1) * 128, csl], x2t[:])
            p.dma("sync", X2B[oc * 128:(oc + 1) * 128, csl], x2b[:])
    es.close()
def emit_s7(nc, p, DR):
    X2F = DR["X2F"]
    identf = DR["identf"]
    XN2B = DR["XN2B"]

    es = contextlib.ExitStack()
    sb = lambda n, s, d: es.enter_context(nc.sbuf_tensor('s7_' + n, s, d))
    ps = lambda n, s, d: es.enter_context(nc.psum_tensor('s7_' + n, s, d))
    idf = sb("idf", [128, 128], F32)
    fmt = sb("fmt", [128, 128], F32)
    xtm = sb("xtm", [128, D], F32)
    xsq = sb("xsq", [128, D], F32)
    xn = sb("xn", [128, D], F32)
    s1 = sb("s1", [128, 1], F32)
    s2 = sb("s2", [128, 1], F32)
    mean = sb("mean", [128, 1], F32)
    ex2 = sb("ex2", [128, 1], F32)
    msq = sb("msq", [128, 1], F32)
    var = sb("var", [128, 1], F32)
    rstd = sb("rstd", [128, 1], F32)
    epsc = sb("epsc", [128, 1], F32)
    mr = sb("mr", [128, 1], F32)
    evb = sb("evb", [128, 128], BF16)
    pT = ps("pT", [128, 128], F32)

    p.op("vector", lambda v: v.memset(epsc[:], 1e-5))
    p.dma("sync", idf[:], identf)
    for tt in range(TT // 128):
        for dc in range(8):
            p.dma("sync", fmt[:], X2F[dc * 128:(dc + 1) * 128, tt * 128:(tt + 1) * 128])
            p.op("tensor", lambda t: t.transpose(pT[:], fmt[:], idf[:]))
            p.op("vector", lambda v, dc=dc: v.tensor_copy(xtm[:, dc * 128:(dc + 1) * 128], pT[:]))
        p.op("vector", lambda v: v.tensor_reduce(s1[:], xtm[:], AX.X, ALU.add))
        p.op("scalar", lambda s: s.activation(xsq[:], xtm[:], AF.Square))
        p.op("vector", lambda v: v.tensor_reduce(s2[:], xsq[:], AX.X, ALU.add))
        p.op("vector", lambda v: v.tensor_scalar(mean[:], s1[:], 1.0 / D, None, ALU.mult))
        p.op("vector", lambda v: v.tensor_scalar(ex2[:], s2[:], 1.0 / D, None, ALU.mult))
        p.op("scalar", lambda s: s.activation(msq[:], mean[:], AF.Square))
        p.op("vector", lambda v: v.tensor_tensor(var[:], ex2[:], msq[:], ALU.subtract))
        p.op("scalar", lambda s: s.activation(var[:], var[:], AF.Ln, bias=epsc[:]))
        p.op("scalar", lambda s: s.activation(rstd[:], var[:], AF.Exp, scale=-0.5))
        p.op("vector", lambda v: v.tensor_tensor(mr[:], mean[:], rstd[:], ALU.mult))
        p.op("vector", lambda v: v.tensor_scalar(xn[:], xtm[:], rstd[:], mr[:], ALU.mult, ALU.subtract))
        for dc in range(8):
            p.op("tensor", lambda t, dc=dc: t.transpose(pT[:], xn[:, dc * 128:(dc + 1) * 128], idf[:]))
            p.op("scalar", lambda s: s.copy(evb[:], pT[:]))
            p.dma("sync", XN2B[dc * 128:(dc + 1) * 128, tt * 128:(tt + 1) * 128], evb[:])
    es.close()
def emit_s8(nc, p, DR):
    XN2B = DR["XN2B"]
    WQKVG = DR["WQKVG"]
    WO1 = DR["WO1"]
    BO = DR["BO"]
    BG1 = DR["BG1"]
    DCP = DR["DCP"]
    IOTAM = DR["IOTAM"]
    TRIU = DR["TRIU"]
    IOTP1 = DR["IOTP1"]
    IOT127 = DR["IOT127"]
    C128 = DR["C128"]
    ONESF = DR["ONESF"]
    identb = DR["identb"]
    S1F = DR["S1F"]

    es = contextlib.ExitStack()
    sb = lambda n, s, d: es.enter_context(nc.sbuf_tensor('s8_' + n, s, d))
    ps = lambda n, s, d: es.enter_context(nc.psum_tensor('s8_' + n, s, d))
    wqg = [sb(f"wqg{i}", [128, 193], BF16) for i in range(8)]
    wo1 = sb("wo1", [64, D], BF16)
    bo = sb("bo", [128, 8], F32)
    bg1 = sb("bg1", [1, 1], F32)
    dcp = sb("dcp", [1, 1], F32)
    iotam = sb("iotam", [128, 128], F32)
    triu = sb("triu", [128, 128], F32)
    iotp1 = sb("iotp1", [128, 1], F32)
    iot127 = sb("iot127", [128, 1], F32)
    c128 = sb("c128", [1, 1], F32)
    onesf = sb("onesf", [1, 128], F32)
    idb = sb("idb", [128, 128], BF16)
    dval = sb("dval", [1, 1], F32)
    ldv = sb("ldv", [1, 1], F32)
    ldcol = sb("ldcol", [128, 1], F32)
    Mb = sb("Mb", [128, 128], BF16)
    Mf = sb("Mf", [128, 128], F32)
    apow = sb("apow", [128, 1], F32)
    bpow = sb("bpow", [128, 1], F32)
    d128 = sb("d128", [1, 1], F32)
    d128c = sb("d128c", [64, 1], F32)
    xnb = [sb(f"xnb{i}", [128, 512], BF16) for i in range(8)]
    qf = sb("qf", [64, T], BF16)
    kf = sb("kf", [64, T], BF16)
    vf = sb("vf", [64, T], BF16)
    gr = sb("gr", [1, T], F32)
    gball = sb("gball", [64, T], F32)
    gkb = sb("gkb", [64, T], BF16)
    gvb = sb("gvb", [64, T], BF16)
    gktm = sb("gktm", [128, 16 * 64], BF16)
    gvtm = sb("gvtm", [128, 16 * 64], BF16)
    Sst = sb("Sst", [64, 64], F32)
    S2b = sb("S2b", [64, 64], F32)
    Sbf = sb("Sbf", [64, 64], BF16)
    o1 = sb("o1", [128, 64], F32)
    amd = sb("amd", [128, 128], BF16)
    otm = sb("otm", [128, 16 * 64], BF16)
    ofmb = sb("ofmb", [64, T], BF16)
    s1t = sb("s1t", [128, 512], F32)
    PA = ps("PA", [128, 512], F32)
    PB65 = ps("PB65", [65, 512], F32)
    PC = ps("PC", [128, 128], F32)
    PTb = ps("PTb", [128, 128], BF16)

    for i in range(8):
        p.dma("gpsimd", wqg[i][:], WQKVG[i * 128:(i + 1) * 128, :])
    p.dma("gpsimd", wo1[:], WO1)
    for i in range(8):
        p.dma("sync", bo[:, i:i + 1], BO[i])
    for src, dst in [(BG1, bg1), (DCP, dcp), (IOTAM, iotam), (TRIU, triu),
                     (IOTP1, iotp1), (IOT127, iot127), (C128, c128), (ONESF, onesf)]:
        p.dma("sync", dst[:], src)
    p.dma("gpsimd", idb[:], identb)
    p.op("scalar", lambda s: s.activation(dval[:], dcp[:], AF.Sigmoid))
    p.op("vector", lambda v: v.tensor_scalar(dval[:], dval[:], 0.099, 0.9, ALU.mult, ALU.add))
    p.op("scalar", lambda s: s.activation(ldv[:], dval[:], AF.Ln))
    p.op("tensor", lambda t: t.matmul(PC[:, 0:1], onesf[:], ldv[:], start=True, stop=True))
    p.op("vector", lambda v: v.tensor_copy(ldcol[:], PC[:, 0:1]))
    p.op("scalar", lambda s: s.activation(Mf[:], iotam[:], AF.Exp, scale=ldcol[:]))
    p.op("vector", lambda v: v.tensor_tensor(Mf[:], Mf[:], triu[:], ALU.mult))
    p.op("vector", lambda v: v.tensor_copy(Mb[:], Mf[:]))
    p.op("scalar", lambda s: s.activation(apow[:], iotp1[:], AF.Exp, scale=ldcol[:]))
    p.op("scalar", lambda s: s.activation(bpow[:], iot127[:], AF.Exp, scale=ldcol[:]))
    p.op("scalar", lambda s: s.activation(d128[:], c128[:], AF.Exp, scale=ldv[:]))
    p.op("tensor", lambda t: t.matmul(PC[0:64, 0:1], onesf[:, 0:64], d128[:], start=True, stop=True))
    p.op("vector", lambda v: v.tensor_copy(d128c[:], PC[0:64, 0:1]))
    for b in range(B):
        for tc in range(4):
            csl = slice(b * T + tc * 512, b * T + (tc + 1) * 512)
            lsl = slice(tc * 512, (tc + 1) * 512)
            for i in range(8):
                p.dma("sync", xnb[i][:], XN2B[i * 128:(i + 1) * 128, csl])
            for kc in range(8):
                p.op("tensor", lambda t, kc=kc: t.matmul(
                    PA[:], wqg[kc][:, 0:128], xnb[kc][:], start=(kc == 0), stop=(kc == 7)))
            for kc in range(8):
                p.op("tensor", lambda t, kc=kc: t.matmul(
                    PB65[:], wqg[kc][:, 128:193], xnb[kc][:], start=(kc == 0), stop=(kc == 7)))
            p.op("vector", lambda v, lsl=lsl: v.tensor_copy(qf[:, lsl], PA[0:64, :]))
            p.op("vector", lambda v, lsl=lsl: v.tensor_copy(kf[:, lsl], PA[64:128, :]))
            p.op("vector", lambda v, lsl=lsl: v.tensor_copy(vf[:, lsl], PB65[0:64, :]))
            p.op("scalar", lambda s, lsl=lsl: s.activation(gr[:, lsl], PB65[64:65, :], AF.Sigmoid, bias=bg1[:]))
        for tc in range(4):
            lsl = slice(tc * 512, (tc + 1) * 512)
            p.op("tensor", lambda t, lsl=lsl: t.matmul(PB65[0:64, :], onesf[:, 0:64], gr[:, lsl], start=True, stop=True))
            p.op("vector", lambda v, lsl=lsl: v.tensor_copy(gball[:, lsl], PB65[0:64, :]))
        p.op("vector", lambda v: v.tensor_tensor(gkb[:], kf[:], gball[:], ALU.mult))
        p.op("vector", lambda v: v.tensor_tensor(gvb[:], vf[:], gball[:], ALU.mult))
        for ch in range(16):
            cs = slice(ch * 128, (ch + 1) * 128)
            os_ = slice(ch * 64, (ch + 1) * 64)
            p.op("tensor", lambda t, cs=cs: t.transpose(PTb[:, 0:64], gkb[:, cs], idb[0:64, 0:64]))
            p.op("vector", lambda v, os_=os_: v.tensor_scalar(gktm[:, os_], PTb[:, 0:64], bpow[:], None, ALU.mult))
            p.op("tensor", lambda t, cs=cs: t.transpose(PTb[:, 0:64], gvb[:, cs], idb[0:64, 0:64]))
            p.op("vector", lambda v, os_=os_: v.tensor_copy(gvtm[:, os_], PTb[:, 0:64]))
        p.op("vector", lambda v: v.memset(Sst[:], 0.0))
        for ch in range(16):
            cs = slice(ch * 128, (ch + 1) * 128)
            os_ = slice(ch * 64, (ch + 1) * 64)
            p.op("vector", lambda v: v.tensor_copy(Sbf[:], Sst[:]))
            p.op("tensor", lambda t, cs=cs: t.matmul(PC[:, 0:64], qf[:, cs], Sbf[:], start=True, stop=True))
            p.op("vector", lambda v: v.tensor_scalar(o1[:], PC[:, 0:64], apow[:], None, ALU.mult))
            p.op("tensor", lambda t, cs=cs: t.matmul(PC[:], gkb[:, cs], qf[:, cs], start=True, stop=True))
            p.op("vector", lambda v: v.tensor_tensor(amd[:], PC[:], Mb[:], ALU.mult))
            p.op("tensor", lambda t, os_=os_: t.matmul(PA[:, 0:64], amd[:], gvtm[:, os_], start=True, stop=True))
            p.op("vector", lambda v, os_=os_: v.tensor_tensor(otm[:, os_], PA[:, 0:64], o1[:], ALU.add))
            p.op("vector", lambda v: v.tensor_scalar(S2b[:], Sst[:], d128c[:], None, ALU.mult))
            p.op("tensor", lambda t, os_=os_: t.matmul(PC[0:64, 0:64], gktm[:, os_], gvtm[:, os_], start=True, stop=True))
            p.op("vector", lambda v: v.tensor_tensor(Sst[:], S2b[:], PC[0:64, 0:64], ALU.add))
        for ch in range(16):
            cs = slice(ch * 128, (ch + 1) * 128)
            os_ = slice(ch * 64, (ch + 1) * 64)
            p.op("tensor", lambda t, os_=os_: t.transpose(PTb[0:64, :], otm[:, os_], idb[:]))
            p.op("vector", lambda v, cs=cs: v.tensor_copy(ofmb[:, cs], PTb[0:64, :]))
        for tc in range(4):
            csl = slice(b * T + tc * 512, b * T + (tc + 1) * 512)
            lsl = slice(tc * 512, (tc + 1) * 512)
            for oc in range(8):
                p.op("tensor", lambda t, oc=oc, lsl=lsl: t.matmul(
                    PA[:], wo1[:, oc * 128:(oc + 1) * 128], ofmb[:, lsl], start=True, stop=True))
                p.op("vector", lambda v, oc=oc: v.tensor_scalar(s1t[:], PA[:], bo[:, oc:oc + 1], None, ALU.add))
                p.dma("sync", S1F[oc * 128:(oc + 1) * 128, csl], s1t[:])
    es.close()
def emit_s9(nc, p, DR):
    XN2B = DR["XN2B"]
    X2F = DR["X2F"]
    S1F = DR["S1F"]
    W1T = DR["W1T"]
    W2T = DR["W2T"]
    B1 = DR["B1"]
    B2 = DR["B2"]
    WGT = DR["WGT"]
    BGM = DR["BGM"]
    IOTA8 = DR["IOTA8"]
    IOTA8P = DR["IOTA8P"]
    SEL = DR["SEL"]
    identf = DR["identf"]
    OUT = DR["OUT"]
    YACC = DR["YACC"]

    es = contextlib.ExitStack()
    sb = lambda n, s, d: es.enter_context(nc.sbuf_tensor('s9_' + n, s, d))
    ps = lambda n, s, d: es.enter_context(nc.psum_tensor('s9_' + n, s, d))
    wgt = [sb(f"wgt{i}", [128, 8], BF16) for i in range(8)]
    bgm = sb("bgm", [8, 1], F32)
    iota8 = sb("iota8", [128, 8], F32)
    iota8p = sb("iota8p", [128, 8], F32)
    sel = sb("sel", [8, 8 * 128], F32)
    idf = sb("idf", [128, 128], F32)
    xnb = [sb(f"xnb{i}", [128, 512], BF16) for i in range(8)]
    lgf = sb("lgf", [8, 512], F32)
    lgt = sb("lgt", [128, 8], F32)
    m1 = sb("m1", [128, 1], F32)
    d1 = sb("d1", [128, 8], F32)
    ind1p = sb("ind1p", [128, 8], F32)
    penal = sb("penal", [128, 8], F32)
    idx = sb("idx", [128, 1], F32)
    ind1 = sb("ind1", [128, 8], F32)
    lg2 = sb("lg2", [128, 8], F32)
    m2 = sb("m2", [128, 1], F32)
    ind2 = sb("ind2", [128, 8], F32)
    wdiff = sb("wdiff", [128, 1], F32)
    w1c = sb("w1c", [128, 1], F32)
    w2c = sb("w2c", [128, 1], F32)
    comb = sb("comb", [128, 8], F32)
    combfm = sb("combfm", [8, TT], F32)
    w1tb = [sb(f"w1tb{i}", [128, HID], BF16) for i in range(8)]
    w2tb = [sb(f"w2tb{i}", [128, D], BF16) for i in range(16)]
    b1c = sb("b1c", [128, 16], F32)
    b2c = sb("b2c", [128, 8], F32)
    hba = sb("hba", [128, 16 * 512], BF16)
    cbet = sb("cbet", [128, 512], F32)
    yt = sb("yt", [128, 512], F32)
    ytw = sb("ytw", [128, 512], F32)
    ya = sb("ya", [128, 512], F32)
    g1t = sb("g1t", [128, 512], F32)
    s1t = sb("s1t", [128, 512], F32)
    x2t = sb("x2t", [128, 512], F32)
    x3t = sb("x3t", [128, 512], F32)
    ot = sb("ot", [128, 128], F32)
    P1 = ps("P1", [128, 512], F32)
    P8 = ps("P8", [8, 512], F32)
    Pcb = ps("Pcb", [128, 512], F32)
    PT = ps("PT", [128, 128], F32)

    for i in range(8):
        p.dma("gpsimd", wgt[i][:], WGT[i * 128:(i + 1) * 128, :])
    for src, dst in [(BGM, bgm), (IOTA8, iota8), (IOTA8P, iota8p), (SEL, sel), (identf, idf)]:
        p.dma("sync", dst[:], src)
    for tc in range(8):
        csl = slice(tc * 512, (tc + 1) * 512)
        for i in range(8):
            p.dma("sync", xnb[i][:], XN2B[i * 128:(i + 1) * 128, csl])
        for kc in range(8):
            p.op("tensor", lambda t, kc=kc: t.matmul(
                P8[:], wgt[kc][:], xnb[kc][:], start=(kc == 0), stop=(kc == 7)))
        p.op("vector", lambda v: v.tensor_scalar(lgf[:], P8[:], bgm[:], None, ALU.add))
        for q in range(4):
            tt = tc * 4 + q
            p.op("tensor", lambda t, q=q: t.transpose(PT[:, 0:8], lgf[:, q * 128:(q + 1) * 128], idf[0:8, 0:8]))
            p.op("vector", lambda v: v.tensor_copy(lgt[:], PT[:, 0:8]))
            p.op("vector", lambda v: v.tensor_reduce(m1[:], lgt[:], AX.X, ALU.max))
            p.op("vector", lambda v: v.tensor_scalar(d1[:], lgt[:], m1[:], None, ALU.subtract))
            p.op("vector", lambda v: v.tensor_scalar(ind1p[:], d1[:], 0.0, None, ALU.is_ge))
            p.op("vector", lambda v: v.scalar_tensor_tensor(penal[:], ind1p[:], -1000.0, iota8p[:], ALU.mult, ALU.add))
            p.op("vector", lambda v: v.tensor_reduce(idx[:], penal[:], AX.X, ALU.min))
            p.op("vector", lambda v: v.tensor_scalar(ind1[:], iota8[:], idx[:], None, ALU.subtract))
            p.op("scalar", lambda s: s.activation(ind1[:], ind1[:], AF.Abs))
            p.op("scalar", lambda s: s.activation(ind1[:], ind1[:], AF.Relu, scale=-1.0, bias=1.0))
            p.op("vector", lambda v: v.scalar_tensor_tensor(lg2[:], ind1[:], -1e30, lgt[:], ALU.mult, ALU.add))
            p.op("vector", lambda v: v.tensor_reduce(m2[:], lg2[:], AX.X, ALU.max))
            p.op("vector", lambda v: v.tensor_scalar(d1[:], lg2[:], m2[:], None, ALU.subtract))
            p.op("vector", lambda v: v.tensor_scalar(ind1p[:], d1[:], 0.0, None, ALU.is_ge))
            p.op("vector", lambda v: v.scalar_tensor_tensor(penal[:], ind1p[:], -1000.0, iota8p[:], ALU.mult, ALU.add))
            p.op("vector", lambda v: v.tensor_reduce(idx[:], penal[:], AX.X, ALU.min))
            p.op("vector", lambda v: v.tensor_scalar(ind2[:], iota8[:], idx[:], None, ALU.subtract))
            p.op("scalar", lambda s: s.activation(ind2[:], ind2[:], AF.Abs))
            p.op("scalar", lambda s: s.activation(ind2[:], ind2[:], AF.Relu, scale=-1.0, bias=1.0))
            p.op("vector", lambda v: v.tensor_tensor(wdiff[:], m1[:], m2[:], ALU.subtract))
            p.op("scalar", lambda s: s.activation(w1c[:], wdiff[:], AF.Sigmoid))
            p.op("vector", lambda v: v.tensor_scalar(w2c[:], w1c[:], -1.0, 1.0, ALU.mult, ALU.add))
            p.op("vector", lambda v: v.tensor_scalar(comb[:], ind1[:], w1c[:], None, ALU.mult))
            p.op("vector", lambda v: v.scalar_tensor_tensor(comb[:], ind2[:], w2c[:], comb[:], ALU.mult, ALU.add))
            p.op("tensor", lambda t: t.transpose(PT[0:8, :], comb[:], idf[:]))
            p.op("vector", lambda v, tt=tt: v.tensor_copy(combfm[:, tt * 128:(tt + 1) * 128], PT[0:8, :]))
    for e in range(NE):
        for i in range(8):
            p.dma("gpsimd", w1tb[i][:], W1T[e, i * 128:(i + 1) * 128, :])
        for i in range(16):
            p.dma("gpsimd", w2tb[i][:], W2T[e, i * 128:(i + 1) * 128, :])
        for i in range(16):
            p.dma("sync", b1c[:, i:i + 1], B1[e, i])
        for i in range(8):
            p.dma("sync", b2c[:, i:i + 1], B2[e, i])
        for tc in range(8):
            csl = slice(tc * 512, (tc + 1) * 512)
            for i in range(8):
                p.dma("sync", xnb[i][:], XN2B[i * 128:(i + 1) * 128, csl])
            p.op("tensor", lambda t, e=e, csl=csl: t.matmul(
                Pcb[:], sel[:, e * 128:(e + 1) * 128], combfm[:, csl], start=True, stop=True))
            p.op("vector", lambda v: v.tensor_copy(cbet[:], Pcb[:]))
            for hc in range(16):
                for kc in range(8):
                    p.op("tensor", lambda t, hc=hc, kc=kc: t.matmul(
                        P1[:], w1tb[kc][:, hc * 128:(hc + 1) * 128], xnb[kc][:],
                        start=(kc == 0), stop=(kc == 7)))
                p.op("scalar", lambda s, hc=hc: s.activation(
                    hba[:, hc * 512:(hc + 1) * 512], P1[:], AF.Relu, bias=b1c[:, hc:hc + 1]))
            for oc in range(8):
                for hc in range(16):
                    p.op("tensor", lambda t, oc=oc, hc=hc: t.matmul(
                        P1[:], w2tb[hc][:, oc * 128:(oc + 1) * 128], hba[:, hc * 512:(hc + 1) * 512],
                        start=(hc == 0), stop=(hc == 15)))
                p.op("vector", lambda v, oc=oc: v.tensor_scalar(yt[:], P1[:], b2c[:, oc:oc + 1], None, ALU.add))
                p.op("vector", lambda v: v.tensor_tensor(ytw[:], yt[:], cbet[:], ALU.mult))
                if e == 0:
                    p.dma("gpsimd", YACC[oc * 128:(oc + 1) * 128, csl], ytw[:])
                else:
                    p.dma("gpsimd", YACC[oc * 128:(oc + 1) * 128, csl], ytw[:], accum_op=ALU.add)
    for tc in range(8):
        csl = slice(tc * 512, (tc + 1) * 512)
        for oc in range(8):
            p.dma("sync", ya[:], YACC[oc * 128:(oc + 1) * 128, csl])
            p.op("scalar", lambda s: s.activation(g1t[:], ya[:], AF.Silu))
            p.dma("sync", s1t[:], S1F[oc * 128:(oc + 1) * 128, csl])
            p.dma("sync", x2t[:], X2F[oc * 128:(oc + 1) * 128, csl])
            p.op("vector", lambda v: v.tensor_tensor(g1t[:], g1t[:], s1t[:], ALU.mult))
            p.op("vector", lambda v: v.tensor_tensor(x3t[:], g1t[:], x2t[:], ALU.add))
            for q in range(4):
                p.op("tensor", lambda t, q=q: t.transpose(PT[:], x3t[:, q * 128:(q + 1) * 128], idf[:]))
                p.op("vector", lambda v: v.tensor_copy(ot[:], PT[:]))
                p.dma("sync", OUT[tc * 512 + q * 128:tc * 512 + (q + 1) * 128, oc * 128:(oc + 1) * 128], ot[:])
    es.close()




def declare_drams(nc, decls):
    D = {}
    for name, kind in decls:
        shape, dtype = DRAM_SPECS[name]
        D[name] = nc.dram_tensor(name, shape, dtype, kind=kind).ap()
    return D


def _finish2(nc, p):
    with nc.semaphore("dmas") as dmasem, nc.semaphore("cmps") as cmpsem, nc.Block() as block:
        p.finalize(block, dmasem, cmpsem)


EMITS = {}


def _mk_stage_builder(sname):
    def build():
        nc = new_nc()
        D = declare_drams(nc, STAGE_DECLS[sname])
        p = Prog(nc)
        EMITS[sname](nc, p, D)
        _finish2(nc, p)
        return nc
    build.__name__ = "build_" + sname
    return build


FUSED_INPUT_NAMES = None


def fused_decls():
    produced = set()
    consumed = set()
    order = []
    for sname in ["s1", "s2", "s3", "s4", "s5", "s6", "s7", "s8", "s9"]:
        for name, kind in STAGE_DECLS[sname]:
            if name not in produced and name not in consumed and name not in [n for n, _ in order]:
                order.append((name, None))
            if kind == "ExternalOutput":
                produced.add(name)
            elif kind == "ExternalInput":
                consumed.add(name)
    decls = []
    for name, _ in order:
        if name == "OUT":
            decls.append((name, "ExternalOutput"))
        elif name in produced:
            decls.append((name, "Internal"))
        elif name in consumed:
            decls.append((name, "ExternalInput"))
        else:
            decls.append((name, "Internal"))
    return decls


def build_fused():
    nc = new_nc()
    D = declare_drams(nc, fused_decls())
    p = Prog(nc)
    for sname in ["s1", "s2", "s3", "s4", "s5", "s6", "s7", "s8", "s9"]:
        EMITS[sname](nc, p, D)
    _finish2(nc, p)
    return nc


EMITS.update(s1=emit_s1, s2=emit_s2, s3=emit_s3, s4=emit_s4, s5=emit_s5, s6=emit_s6, s7=emit_s7, s8=emit_s8, s9=emit_s9)
build_s1 = _mk_stage_builder('s1')
build_s2 = _mk_stage_builder('s2')
build_s3 = _mk_stage_builder('s3')
build_s4 = _mk_stage_builder('s4')
build_s5 = _mk_stage_builder('s5')
build_s6 = _mk_stage_builder('s6')
build_s7 = _mk_stage_builder('s7')
build_s8 = _mk_stage_builder('s8')
build_s9 = _mk_stage_builder('s9')


def _f32(a):
    return np.ascontiguousarray(a, dtype=np.float32)


def make_consts():
    c = {}
    c["identf"] = np.eye(128, dtype=np.float32)
    c["identb"] = np.eye(128, dtype=np.float32).astype(BF)
    m = np.zeros((128, 4 * 512), dtype=np.float32)
    kl = np.arange(128)[:, None]
    ql = np.arange(512)[None, :]
    for rr in range(4):
        m[:, rr * 512:(rr + 1) * 512] = (ql >= kl + 128 * rr)
    c["MASKC"] = m.astype(BF)
    c["ONESF"] = np.ones((1, 128), dtype=np.float32)
    c["IOTAM"] = (np.arange(128)[None, :] - np.arange(128)[:, None]).astype(np.float32)
    c["TRIU"] = (np.arange(128)[None, :] >= np.arange(128)[:, None]).astype(np.float32)
    c["IOTP1"] = (np.arange(128)[:, None] + 1).astype(np.float32)
    c["IOT127"] = (127 - np.arange(128)[:, None]).astype(np.float32)
    c["C128"] = np.full((1, 1), 128.0, dtype=np.float32)
    c["IOTA8"] = np.broadcast_to(np.arange(8, dtype=np.float32), (128, 8)).copy()
    c["IOTA8P"] = c["IOTA8"] + 1000.0
    sel = np.zeros((8, 8 * 128), dtype=np.float32)
    for e in range(8):
        sel[e, e * 128:(e + 1) * 128] = 1.0
    c["SEL"] = sel
    return c


FUSED = True


def host_inputs(inputs):
    cst = make_consts()
    m = dict(cst)
    m["WQKV"] = _f32(inputs["L0_attn_wqkv"])
    m["BQ"] = _f32(inputs["L0_attn_bqkv"]).reshape(9, 128, 1)
    m["WAO"] = _f32(inputs["L0_attn_w_attn_out"])
    m["BAO"] = _f32(inputs["L0_attn_b_attn_out"]).reshape(8, 128, 1)
    m["WIN"] = _f32(inputs["L0_attn_w_in"])
    m["CW"] = _f32(inputs["L0_attn_conv_w"]).reshape(16, 128, 4)
    m["CB"] = _f32(inputs["L0_attn_conv_b"]).reshape(16, 128, 1)
    m["WXP"] = _f32(inputs["L0_attn_w_xproj"])
    m["WDT"] = _f32(inputs["L0_attn_w_dt"])
    m["BDT"] = _f32(inputs["L0_attn_b_dt"]).reshape(16, 128, 1)
    m["ALOG"] = _f32(inputs["L0_attn_A_log"]).reshape(16, 128, DS)
    m["DPV"] = _f32(inputs["L0_attn_Dp"]).reshape(16, 128, 1)
    m["WOUT"] = _f32(inputs["L0_attn_w_out"])
    m["WF"] = _f32(np.asarray(inputs["L0_ffn_W"]).T)
    m["FB"] = _f32(inputs["L0_ffn_b"]).reshape(8, 128, 1)
    m["VR"] = _f32(inputs["L0_ffn_V_r"])
    m["CT"] = _f32(np.asarray(inputs["L0_ffn_C"]).T)
    m["WQKVG"] = _f32(np.concatenate([inputs["L1_attn_Wq"], inputs["L1_attn_Wk"],
                                      inputs["L1_attn_Wv"], inputs["L1_attn_Wg"]], axis=1))
    m["WO1"] = _f32(inputs["L1_attn_Wo"])
    m["BO"] = _f32(inputs["L1_attn_bo"]).reshape(8, 128, 1)
    m["BG1"] = _f32(inputs["L1_attn_bg"]).reshape(1, 1)
    m["DCP"] = _f32(inputs["L1_attn_decay_param"]).reshape(1, 1)
    m["W1T"] = _f32(np.transpose(np.asarray(inputs["L1_ffn_W1"]), (0, 2, 1)))
    m["W2T"] = _f32(np.transpose(np.asarray(inputs["L1_ffn_W2"]), (0, 2, 1)))
    m["B1"] = _f32(inputs["L1_ffn_b1"]).reshape(NE, 16, 128, 1)
    m["B2"] = _f32(inputs["L1_ffn_b2"]).reshape(NE, 8, 128, 1)
    m["WGT"] = _f32(np.asarray(inputs["L1_ffn_Wg"]).T)
    m["BGM"] = _f32(inputs["L1_ffn_bg"]).reshape(NE, 1)
    return m


def kernel(**inputs):
    x = _f32(inputs["L0_attn_x"])
    shared_map = host_inputs(inputs)
    in_names = {n for n, k in fused_decls() if k == "ExternalInput"}
    base = {k: v for k, v in shared_map.items() if k in in_names}
    maps = []
    for c in range(NC):
        m = dict(base)
        m["XTM"] = np.ascontiguousarray(x[B * c:B * (c + 1)].reshape(TT, D))
        maps.append(m)
    res = _run(build_fused, maps)
    out = np.empty((16, 2048, 1024), dtype=np.float32)
    for c in range(NC):
        out[B * c:B * (c + 1)] = np.asarray(res[c]["OUT"], dtype=np.float32).reshape(B, T, D)
    return out


# version 07
# speedup vs baseline: 1.3924x; 1.3781x over previous
import contextlib
import numpy as np
import ml_dtypes
import concourse.bass as bass
import concourse.mybir as mybir
from concourse.bass_utils import run_bass_kernel_spmd

F32 = mybir.dt.float32
BF16 = mybir.dt.bfloat16
U8 = mybir.dt.uint8
AF = mybir.ActivationFunctionType
ALU = mybir.AluOpType
AX = mybir.AxisListType
BF = ml_dtypes.bfloat16

NC = 8
B = 2
T = 2048
TT = B * T
D = 1024
DI = 2048
NHEAD = 16
HD = 64
DS = 16
DTR = 64
NE = 8
HID = 2048


def _is_ap(x):
    return hasattr(x, "space") and hasattr(x, "offset") and hasattr(x, "tensor")


class _FakeInstr:
    def then_inc(self, *a, **k):
        return self


class _Recorder:
    def __init__(self):
        self.calls = []

    def __getattr__(self, name):
        calls = self.calls

        def m(*a, **k):
            calls.append((name, a, k))
            return _FakeInstr()
        return m


class Prog:
    ENGINES = ("sync", "vector", "scalar", "tensor", "gpsimd")
    MODE = "dep"
    PAIRS = ()

    def __init__(self, nc):
        self.nc = nc
        self.seq = {e: [] for e in self.ENGINES}
        self.cmp_n = {e: 0 for e in self.ENGINES}
        self.dma_n = {e: 0 for e in self.ENGINES}
        self.last_tok = {}
        self.last_psum = {}
        self.tmap = {}
        self.ivals = []
        self.parent = {}
        self.gstate = {}
        self.ngid = 0

    def _find(self, g):
        while self.parent[g] != g:
            self.parent[g] = self.parent[self.parent[g]]
            g = self.parent[g]
        return g

    def _union(self, a, b):
        ra, rb = self._find(a), self._find(b)
        if ra == rb:
            return ra
        self.parent[rb] = ra
        wa, rda = self.gstate[ra]
        wb, rdb = self.gstate.pop(rb)
        self.gstate[ra] = [wa | wb, rda | rdb]
        return ra

    def _new_gid(self):
        g = self.ngid
        self.ngid += 1
        self.parent[g] = g
        self.gstate[g] = [set(), set()]
        return g

    def _gid(self, ap):
        name = ap.name
        g = self.tmap.get(name)
        if g is not None:
            return self._find(g)
        sp = str(ap.space)
        if "DRAM" in sp:
            g = self._new_gid()
            self.tmap[name] = g
            return g
        t = ap.tensor
        ml = self.nc.lookup_mloc(t)
        shape = list(t.shape)
        per_part = 1
        for s in shape[1:]:
            per_part *= int(s)
        per_part *= mybir.dt.size(t.dtype)
        b0 = int(ml.addr)
        b1 = b0 + per_part
        p0 = int(t.base_partition)
        p1 = p0 + int(shape[0])
        if "PSUM" in sp:
            p0, p1 = 0, 128
            b0 = int(ml.bank) * 2048
            b1 = b0 + ((per_part + 2047) // 2048) * 2048
        g = None
        for (sp2, q0, q1, c0, c1, g2) in self.ivals:
            if sp2 == sp and q0 < p1 and p0 < q1 and c0 < b1 and b0 < c1:
                g = g2 if g is None else self._union(g, g2)
        if g is None:
            g = self._new_gid()
        else:
            g = self._find(g)
        self.ivals.append((sp, p0, p1, b0, b1, g))
        self.tmap[name] = g
        return g

    def _emit(self, eng, kind, fn, reads, writes):
        touches_psum = any("PSUM" in str(ap.space) for ap in reads + writes)
        waits = set()
        rg = [self._find(self._gid(ap)) for ap in reads]
        wg = [self._find(self._gid(ap)) for ap in writes]
        rg = [self._find(g) for g in rg]
        wg = [self._find(g) for g in wg]
        for g in rg:
            waits |= self.gstate[g][0]
        for g in wg:
            st = self.gstate[g]
            waits |= st[0]
            waits |= st[1]
        if kind == "dma":
            self.dma_n[eng] += 1
            tok = (eng, "dma", self.dma_n[eng])
            w = 16 if eng == "gpsimd" else 32
            if tok[2] > w:
                waits.add((eng, "dma", tok[2] - w))
        else:
            self.cmp_n[eng] += 1
            tok = (eng, "cmp", self.cmp_n[eng])
        if self.MODE == "serial" or (self.MODE == "dmaserial" and kind == "dma"):
            waits |= set(self.last_tok.values())
        if self.MODE == "engserial" and kind == "cmp":
            waits |= {t for (k, e), t in self.last_tok.items() if k == "cmp"}
        if self.MODE == "pair" and kind == "cmp":
            for a, b in self.PAIRS:
                other = b if eng == a else (a if eng == b else None)
                if other is not None:
                    t = self.last_tok.get(("cmp", other))
                    if t is not None:
                        waits.add(t)
        if (self.MODE == "psumpair" and kind == "cmp"
                and eng in ("vector", "scalar") and touches_psum):
            other = "scalar" if eng == "vector" else "vector"
            t = self.last_psum.get(other)
            if t is not None:
                waits.add(t)
            self.last_psum[eng] = tok
        if self.MODE == "serial":
            self.last_tok = {("x", "x"): tok}
        else:
            self.last_tok[(kind, eng)] = tok
        waits.discard(tok)
        for g in rg:
            self.gstate[g][1].add(tok)
        for g in wg:
            self.gstate[g][0] = {tok}
            self.gstate[g][1] = set()
        self.seq[eng].append((kind, fn, waits))

    def op(self, eng, fn):
        rec = _Recorder()
        fn(rec)
        reads, writes = [], []
        for (meth, a, k) in rec.calls:
            aps = [x for x in a if _is_ap(x)]
            kaps = {n: x for n, x in k.items() if _is_ap(x)}
            if "out" in kaps:
                out = kaps.pop("out")
            elif aps:
                out = aps.pop(0)
            else:
                out = None
            if out is not None:
                writes.append(out)
            reads.extend(aps)
            reads.extend(kaps.values())
            if meth == "matmul" and out is not None and k.get("start") is False:
                reads.append(out)
        writes = writes + [ap for ap in reads if "PSUM" in str(ap.space)]
        self._emit(eng, "cmp", fn, reads, writes)

    def dma(self, eng, out, in_, **kw):
        def fn(E, out=out, in_=in_, kw=kw):
            return E.dma_start(out=out, in_=in_, **kw)
        reads = [in_]
        if kw.get("accum_op") is not None:
            reads.append(out)
        self._emit(eng, "dma", fn, reads, [out])

    def finalize(self, block, sems):
        finals = {}
        for e in self.ENGINES:
            if self.cmp_n[e]:
                finals[("cmp", e)] = self.cmp_n[e]
            if self.dma_n[e]:
                finals[("dma", e)] = 16 * self.dma_n[e]
        for eng in self.ENGINES:
            lst = self.seq[eng]

            def make_section(lst=lst, eng=eng):
                def section(E):
                    hw = {}
                    for (kind, fn, waits) in lst:
                        need = {}
                        for (teng, tkind, tidx) in waits:
                            if (tkind == "cmp" and teng == "tensor"
                                    and eng == "tensor" and kind == "cmp"):
                                continue
                            key = (tkind, teng)
                            val = tidx if tkind == "cmp" else 16 * tidx
                            if hw.get(key, 0) >= val:
                                continue
                            if need.get(key, 0) < val:
                                need[key] = val
                        for key, val in need.items():
                            E.wait_ge(sems[key], val)
                            hw[key] = val
                        r = fn(E)
                        r.then_inc(sems[(kind, eng)], 16 if kind == "dma" else 1)
                    for key, tot in finals.items():
                        if hw.get(key, 0) < tot:
                            E.wait_ge(sems[key], tot)
                return section
            getattr(block, eng)(make_section())


def new_nc():
    return bass.Bass("TRN2", target_bir_lowering=False)


def _run(build_fn, per_core_maps, cache={}):
    key = build_fn.__name__
    if key not in cache:
        cache[key] = build_fn()
    nc = cache[key]
    res = run_bass_kernel_spmd(nc, per_core_maps, core_ids=list(range(len(per_core_maps))))
    return res.results


def _make_sems(nc, p, es):
    sems = {}
    for eng in Prog.ENGINES:
        if p.cmp_n[eng]:
            sems[("cmp", eng)] = es.enter_context(nc.semaphore("c_" + eng))
        if p.dma_n[eng]:
            sems[("dma", eng)] = es.enter_context(nc.semaphore("d_" + eng))
    return sems


DRAM_SPECS = {
    "XTM": ([TT, D], F32),
    "identf": ([128, 128], F32),
    "XF": ([D, TT], F32),
    "XNF": ([D, TT], F32),
    "XNB": ([D, TT], BF16),
    "WQKV": ([D, 1152], F32),
    "BQ": ([9, 128, 1], F32),
    "identb": ([128, 128], BF16),
    "QH": ([B, 64, NHEAD * T], BF16),
    "KB": ([B, 64, T], BF16),
    "VA": ([B, 128, 16 * 65], BF16),
    "WAO": ([D, D], F32),
    "BAO": ([8, 128, 1], F32),
    "MASKC": ([128, 4 * 512], BF16),
    "ONESF": ([1, 128], F32),
    "X1F": ([D, TT], F32),
    "X1B": ([D, TT], BF16),
    "WIN": ([D, 2 * DI], F32),
    "CW": ([16, 128, 4], F32),
    "CB": ([16, 128, 1], F32),
    "WXP": ([DI, 96], F32),
    "WDT": ([DTR, DI], F32),
    "BDT": ([16, 128, 1], F32),
    "UB": ([DI, TT], BF16),
    "UF": ([DI, TT], F32),
    "SRES": ([DI, TT], F32),
    "DLT": ([DI, TT], F32),
    "BM": ([B, DS, T], F32),
    "CM": ([B, DS, T], F32),
    "U0D": ([DI, TT], F32),
    "XDB": ([DTR, TT], BF16),
    "ALOG": ([16, 128, DS], F32),
    "DPV": ([16, 128, 1], F32),
    "WOUT": ([DI, D], F32),
    "S0F": ([D, TT], F32),
    "WF": ([D, D], F32),
    "FB": ([8, 128, 1], F32),
    "VR": ([D, 32], F32),
    "CT": ([32, D], F32),
    "X2F": ([D, TT], F32),
    "X2B": ([D, TT], BF16),
    "XN2B": ([D, TT], BF16),
    "WQKVG": ([D, 193], F32),
    "WO1": ([64, D], F32),
    "BO": ([8, 128, 1], F32),
    "BG1": ([1, 1], F32),
    "DCP": ([1, 1], F32),
    "IOTAM": ([128, 128], F32),
    "TRIU": ([128, 128], F32),
    "IOTP1": ([128, 1], F32),
    "IOT127": ([128, 1], F32),
    "C128": ([1, 1], F32),
    "S1F": ([D, TT], F32),
    "W1T": ([NE, D, HID], F32),
    "W2T": ([NE, HID, D], F32),
    "B1": ([NE, 16, 128, 1], F32),
    "B2": ([NE, 8, 128, 1], F32),
    "WGT": ([D, NE], F32),
    "BGM": ([NE, 1], F32),
    "IOTA8": ([128, 8], F32),
    "IOTA8P": ([128, 8], F32),
    "SEL": ([8, 8 * 128], F32),
    "OUT": ([TT, D], F32),
    "YACC": ([D, TT], F32),
}

STAGE_DECLS = {
    "s1": [('XTM', 'ExternalInput'), ('identf', 'ExternalInput'), ('XF', 'ExternalOutput'), ('XNF', 'ExternalOutput'), ('XNB', 'ExternalOutput')],
    "s2": [('XNB', 'ExternalInput'), ('WQKV', 'ExternalInput'), ('BQ', 'ExternalInput'), ('identb', 'ExternalInput'), ('QH', 'ExternalOutput'), ('KB', 'ExternalOutput'), ('VA', 'ExternalOutput')],
    "s3": [('QH', 'ExternalInput'), ('KB', 'ExternalInput'), ('VA', 'ExternalInput'), ('WAO', 'ExternalInput'), ('BAO', 'ExternalInput'), ('XNF', 'ExternalInput'), ('MASKC', 'ExternalInput'), ('ONESF', 'ExternalInput'), ('X1F', 'ExternalOutput'), ('X1B', 'ExternalOutput')],
    "s4": [('X1B', 'ExternalInput'), ('WIN', 'ExternalInput'), ('CW', 'ExternalInput'), ('CB', 'ExternalInput'), ('WXP', 'ExternalInput'), ('WDT', 'ExternalInput'), ('BDT', 'ExternalInput'), ('UB', 'ExternalOutput'), ('UF', 'ExternalOutput'), ('SRES', 'ExternalOutput'), ('DLT', 'ExternalOutput'), ('BM', 'ExternalOutput'), ('CM', 'ExternalOutput'), ('U0D', 'Internal'), ('XDB', 'Internal')],
    "s5": [('DLT', 'ExternalInput'), ('UF', 'ExternalInput'), ('SRES', 'ExternalInput'), ('BM', 'ExternalInput'), ('CM', 'ExternalInput'), ('ALOG', 'ExternalInput'), ('DPV', 'ExternalInput'), ('WOUT', 'ExternalInput'), ('X1F', 'ExternalInput'), ('ONESF', 'ExternalInput'), ('S0F', 'ExternalOutput')],
    "s6": [('XNB', 'ExternalInput'), ('WF', 'ExternalInput'), ('FB', 'ExternalInput'), ('VR', 'ExternalInput'), ('CT', 'ExternalInput'), ('S0F', 'ExternalInput'), ('XF', 'ExternalInput'), ('X2F', 'ExternalOutput'), ('X2B', 'ExternalOutput')],
    "s7": [('X2F', 'ExternalInput'), ('identf', 'ExternalInput'), ('XN2B', 'ExternalOutput')],
    "s8": [('XN2B', 'ExternalInput'), ('WQKVG', 'ExternalInput'), ('WO1', 'ExternalInput'), ('BO', 'ExternalInput'), ('BG1', 'ExternalInput'), ('DCP', 'ExternalInput'), ('IOTAM', 'ExternalInput'), ('TRIU', 'ExternalInput'), ('IOTP1', 'ExternalInput'), ('IOT127', 'ExternalInput'), ('C128', 'ExternalInput'), ('ONESF', 'ExternalInput'), ('identb', 'ExternalInput'), ('S1F', 'ExternalOutput')],
    "s9": [('XN2B', 'ExternalInput'), ('X2F', 'ExternalInput'), ('S1F', 'ExternalInput'), ('W1T', 'ExternalInput'), ('W2T', 'ExternalInput'), ('B1', 'ExternalInput'), ('B2', 'ExternalInput'), ('WGT', 'ExternalInput'), ('BGM', 'ExternalInput'), ('IOTA8', 'ExternalInput'), ('IOTA8P', 'ExternalInput'), ('SEL', 'ExternalInput'), ('identf', 'ExternalInput'), ('OUT', 'ExternalOutput'), ('YACC', 'Internal')],
}

def emit_s1(nc, p, DR):
    XTM = DR["XTM"]
    identf = DR["identf"]
    XF = DR["XF"]
    XNF = DR["XNF"]
    XNB = DR["XNB"]

    es = contextlib.ExitStack()
    sb = lambda n, s, d: es.enter_context(nc.sbuf_tensor('s1_' + n, s, d))
    ps = lambda n, s, d: es.enter_context(nc.psum_tensor('s1_' + n, s, d))
    xt = sb("xt", [128, D], F32)
    xsq = sb("xsq", [128, D], F32)
    xn = sb("xn", [128, D], F32)
    idf = sb("idf", [128, 128], F32)
    s1 = sb("s1", [128, 1], F32)
    s2 = sb("s2", [128, 1], F32)
    mean = sb("mean", [128, 1], F32)
    ex2 = sb("ex2", [128, 1], F32)
    msq = sb("msq", [128, 1], F32)
    var = sb("var", [128, 1], F32)
    rstd = sb("rstd", [128, 1], F32)
    epsc = sb("epsc", [128, 1], F32)
    mr = sb("mr", [128, 1], F32)
    ev = sb("ev", [128, 128], F32)
    evb = sb("evb", [128, 128], BF16)
    pT = ps("pT", [128, 128], F32)

    p.op("vector", lambda v: v.memset(epsc[:], 1e-5))
    p.dma("sync", idf[:], identf)
    for tt in range(TT // 128):
        p.dma("sync", xt[:], XTM[tt * 128:(tt + 1) * 128, :])
        p.op("vector", lambda v: v.tensor_reduce(s1[:], xt[:], AX.X, ALU.add))
        p.op("scalar", lambda s: s.activation(xsq[:], xt[:], AF.Square))
        p.op("vector", lambda v: v.tensor_reduce(s2[:], xsq[:], AX.X, ALU.add))
        p.op("vector", lambda v: v.tensor_scalar(mean[:], s1[:], 1.0 / D, None, ALU.mult))
        p.op("vector", lambda v: v.tensor_scalar(ex2[:], s2[:], 1.0 / D, None, ALU.mult))
        p.op("scalar", lambda s: s.activation(msq[:], mean[:], AF.Square))
        p.op("vector", lambda v: v.tensor_tensor(var[:], ex2[:], msq[:], ALU.subtract))
        p.op("scalar", lambda s: s.activation(var[:], var[:], AF.Ln, bias=epsc[:]))
        p.op("scalar", lambda s: s.activation(rstd[:], var[:], AF.Exp, scale=-0.5))
        p.op("vector", lambda v: v.tensor_tensor(mr[:], mean[:], rstd[:], ALU.mult))
        p.op("vector", lambda v: v.tensor_scalar(xn[:], xt[:], rstd[:], mr[:], ALU.mult, ALU.subtract))
        for dc in range(D // 128):
            csl = slice(dc * 128, (dc + 1) * 128)
            p.op("tensor", lambda t, csl=csl: t.transpose(pT[:], xn[:, csl], idf[:]))
            p.op("vector", lambda v: v.tensor_copy(ev[:], pT[:]))
            p.op("scalar", lambda s: s.copy(evb[:], pT[:]))
            p.dma("sync", XNF[dc * 128:(dc + 1) * 128, tt * 128:(tt + 1) * 128], ev[:])
            p.dma("sync", XNB[dc * 128:(dc + 1) * 128, tt * 128:(tt + 1) * 128], evb[:])
            p.op("tensor", lambda t, csl=csl: t.transpose(pT[:], xt[:, csl], idf[:]))
            p.op("vector", lambda v: v.tensor_copy(ev[:], pT[:]))
            p.dma("sync", XF[dc * 128:(dc + 1) * 128, tt * 128:(tt + 1) * 128], ev[:])
    es.close()
def emit_s2(nc, p, DR):
    XNB = DR["XNB"]
    WQKV = DR["WQKV"]
    BQ = DR["BQ"]
    identb = DR["identb"]
    QH = DR["QH"]
    KB = DR["KB"]
    VA = DR["VA"]

    es = contextlib.ExitStack()
    sb = lambda n, s, d: es.enter_context(nc.sbuf_tensor('s2_' + n, s, d))
    ps = lambda n, s, d: es.enter_context(nc.psum_tensor('s2_' + n, s, d))
    wq = [sb(f"wq{i}", [128, 1152], BF16) for i in range(8)]
    bq = sb("bq", [128, 9], F32)
    xnb = [sb(f"xnb{i}", [128, 512], BF16) for i in range(8)]
    idb = sb("idb", [128, 128], BF16)
    qtmp = sb("qtmp", [64, 512], BF16)
    ktmp = sb("ktmp", [64, 512], BF16)
    vtmp = sb("vtmp", [64, 512], BF16)
    vat = sb("vat", [128, 64], BF16)
    onec = sb("onec", [128, 1], BF16)
    P1 = ps("P1", [128, 512], F32)
    PT = ps("PT", [128, 128], BF16)

    p.dma("sync", idb[:], identb)
    p.op("vector", lambda v: v.memset(onec[:], 1.0))
    for i in range(8):
        p.dma("gpsimd", wq[i][:], WQKV[i * 128:(i + 1) * 128, :])
    for i in range(9):
        p.dma("sync", bq[:, i:i + 1], BQ[i])
    for b in range(B):
        for kt in range(16):
            p.dma("sync", VA[b, :, kt * 65 + 64:kt * 65 + 65], onec[:], allow_slow_non_contiguous=True)
    for tc in range(8):
        b, tloc = tc // 4, (tc % 4) * 512
        csl = slice(tc * 512, (tc + 1) * 512)
        for i in range(8):
            p.dma("sync", xnb[i][:], XNB[i * 128:(i + 1) * 128, csl])
        for oc in range(9):
            for kc in range(8):
                p.op("tensor", lambda t, oc=oc, kc=kc: t.matmul(
                    P1[:], wq[kc][:, oc * 128:(oc + 1) * 128], xnb[kc][:],
                    start=(kc == 0), stop=(kc == 7)))
            if oc < 8:
                for half in range(2):
                    h = 2 * oc + half
                    hs = slice(half * 64, half * 64 + 64)
                    p.op("vector", lambda v, hs=hs, oc=oc: v.tensor_scalar(
                        qtmp[:], P1[hs, :], bq[hs, oc:oc + 1], 0.125, ALU.add, ALU.mult))
                    p.dma("sync", QH[b, :, h * T + tloc:h * T + tloc + 512], qtmp[:])
            else:
                p.op("vector", lambda v: v.tensor_scalar(
                    ktmp[:], P1[0:64, :], bq[0:64, 8:9], None, ALU.add))
                p.dma("sync", KB[b, :, tloc:tloc + 512], ktmp[:])
                p.op("vector", lambda v: v.tensor_scalar(
                    vtmp[:], P1[64:128, :], bq[64:128, 8:9], None, ALU.add))
                for i in range(4):
                    kt = (tloc // 128) + i
                    p.op("tensor", lambda t, i=i: t.transpose(PT[:, 0:64], vtmp[:, i * 128:(i + 1) * 128], idb[0:64, 0:64]))
                    p.op("vector", lambda v: v.tensor_copy(vat[:], PT[:, 0:64]))
                    p.dma("sync", VA[b, :, kt * 65:kt * 65 + 64], vat[:])
    es.close()
def emit_s3(nc, p, DR):
    QH = DR["QH"]
    KB = DR["KB"]
    VA = DR["VA"]
    WAO = DR["WAO"]
    BAO = DR["BAO"]
    XNF = DR["XNF"]
    MASKC = DR["MASKC"]
    ONESF = DR["ONESF"]
    X1F = DR["X1F"]
    X1B = DR["X1B"]

    es = contextlib.ExitStack()
    sb = lambda n, s, d: es.enter_context(nc.sbuf_tensor('s3_' + n, s, d))
    ps = lambda n, s, d: es.enter_context(nc.psum_tensor('s3_' + n, s, d))
    qh = sb("qh", [64, NHEAD * T], BF16)
    kb = sb("kb", [64, T], BF16)
    va = sb("va", [128, 16 * 65], BF16)
    wao = [sb(f"wao{i}", [128, D], BF16) for i in range(8)]
    bao = sb("bao", [128, 8], F32)
    maskc = sb("maskc", [128, 4 * 512], BF16)
    onesf = sb("onesf", [1, 128], F32)
    onesb64 = sb("onesb64", [64, 1], BF16)
    sqb = sb("sqb", [64, 2048], BF16)
    sums = sb("sums", [1, 2048], F32)
    red = sb("red", [1, 1], F32)
    qmax2 = sb("qmax2", [1, 1], F32)
    kmax2 = sb("kmax2", [1, 1], F32)
    bmag = sb("bmag", [1, 1], F32)
    negb = sb("negb", [1, 1], F32)
    negbc = sb("negbc", [128, 1], F32)
    et = sb("et", [128, 512], BF16)
    et2 = sb("et2", [128, 512], BF16)
    rc = sb("rc", [1, 512], F32)
    rcb = sb("rcb", [64, 512], F32)
    ohx = sb("ohx", [64, 512], BF16)
    ofm = sb("ofm", [128, 8 * T], BF16)
    xattn = sb("xattn", [128, 512], F32)
    xnt = sb("xnt", [128, 512], F32)
    x1t = sb("x1t", [128, 512], F32)
    x1b = sb("x1b", [128, 512], BF16)
    PS = ps("PS", [128, 512], F32)
    PO = ps("PO", [65, 512], F32)
    PB = ps("PB", [64, 512], F32)
    PN = ps("PN", [1, 2048], F32)

    p.dma("sync", maskc[:], MASKC)
    p.dma("sync", onesf[:], ONESF)
    p.op("vector", lambda v: v.memset(onesb64[:], 1.0))
    for i in range(8):
        p.dma("gpsimd", wao[i][:], WAO[i * 128:(i + 1) * 128, :])
    for i in range(8):
        p.dma("sync", bao[:, i:i + 1], BAO[i])
    for b in range(B):
        p.dma("sync", qh[:], QH[b])
        p.dma("sync", kb[:], KB[b])
        p.dma("sync", va[:], VA[b])
        p.op("scalar", lambda s: s.activation(sqb[:], kb[:], AF.Square))
        for c in range(4):
            p.op("tensor", lambda t, c=c: t.matmul(PN[:, c * 512:(c + 1) * 512], onesb64[:], sqb[:, c * 512:(c + 1) * 512], start=True, stop=True))
        p.op("vector", lambda v: v.tensor_copy(sums[:], PN[:]))
        p.op("vector", lambda v: v.tensor_reduce(kmax2[:], sums[:], AX.X, ALU.max))
        p.op("vector", lambda v: v.memset(qmax2[:], 0.0))
        for blk in range(16):
            p.op("scalar", lambda s, blk=blk: s.activation(sqb[:], qh[:, blk * 2048:(blk + 1) * 2048], AF.Square))
            for c in range(4):
                p.op("tensor", lambda t, c=c: t.matmul(
                    PN[:, c * 512:(c + 1) * 512], onesb64[:],
                    sqb[:, c * 512:(c + 1) * 512], start=True, stop=True))
            p.op("vector", lambda v: v.tensor_copy(sums[:], PN[:]))
            p.op("vector", lambda v: v.tensor_reduce(red[:], sums[:], AX.X, ALU.max))
            p.op("vector", lambda v: v.tensor_tensor(qmax2[:], qmax2[:], red[:], ALU.max))
        p.op("vector", lambda v: v.tensor_tensor(bmag[:], qmax2[:], kmax2[:], ALU.mult))
        p.op("scalar", lambda s: s.activation(bmag[:], bmag[:], AF.Sqrt))
        p.op("vector", lambda v: v.tensor_scalar(negb[:], bmag[:], -0.1300, None, ALU.mult))
        p.op("tensor", lambda t: t.matmul(PS[:, 0:1], onesf[:], negb[:], start=True, stop=True))
        p.op("vector", lambda v: v.tensor_copy(negbc[:], PS[:, 0:1]))
        for h in range(NHEAD):
            for qc in range(4):
                q512 = qh[:, h * T + qc * 512: h * T + (qc + 1) * 512]
                nkt = 4 * qc + 4
                for kt in range(nkt):
                    p.op("tensor", lambda t, kt=kt, q512=q512: t.matmul(
                        PS[:], kb[:, kt * 128:(kt + 1) * 128], q512, start=True, stop=True))
                    p.op("scalar", lambda s: s.activation(et[:], PS[:], AF.Exp, bias=negbc[:]))
                    if kt >= 4 * qc:
                        rr = kt - 4 * qc
                        p.op("vector", lambda v, rr=rr: v.tensor_tensor(
                            et2[:], et[:], maskc[:, rr * 512:(rr + 1) * 512], ALU.mult))
                        src = et2
                    else:
                        src = et
                    p.op("tensor", lambda t, kt=kt, src=src, nkt=nkt: t.matmul(
                        PO[:], va[:, kt * 65:kt * 65 + 65], src[:],
                        start=(kt == 0), stop=(kt == nkt - 1)))
                p.op("vector", lambda v: v.reciprocal(rc[:], PO[64:65, :]))
                p.op("tensor", lambda t: t.matmul(PB[:], onesf[:, 0:64], rc[:], start=True, stop=True))
                p.op("vector", lambda v: v.tensor_copy(rcb[:], PB[:]))
                p.op("vector", lambda v: v.tensor_tensor(ohx[:], PO[0:64, :], rcb[:], ALU.mult))
                dc, half = h // 2, h % 2
                p.op("vector", lambda v, dc=dc, half=half, qc=qc: v.tensor_copy(
                    ofm[half * 64:half * 64 + 64, dc * T + qc * 512:dc * T + (qc + 1) * 512], ohx[:]))
        for oc in range(8):
            for tc4 in range(4):
                for dc in range(8):
                    p.op("tensor", lambda t, oc=oc, tc4=tc4, dc=dc: t.matmul(
                        PS[:], wao[dc][:, oc * 128:(oc + 1) * 128],
                        ofm[:, dc * T + tc4 * 512:dc * T + (tc4 + 1) * 512],
                        start=(dc == 0), stop=(dc == 7)))
                gcols = slice(b * T + tc4 * 512, b * T + (tc4 + 1) * 512)
                p.op("vector", lambda v, oc=oc: v.tensor_scalar(
                    xattn[:], PS[:], bao[:, oc:oc + 1], None, ALU.add))
                p.dma("sync", xnt[:], XNF[oc * 128:(oc + 1) * 128, gcols])
                p.op("vector", lambda v: v.tensor_tensor(x1t[:], xattn[:], xnt[:], ALU.add))
                p.op("vector", lambda v: v.tensor_copy(x1b[:], x1t[:]))
                p.dma("sync", X1F[oc * 128:(oc + 1) * 128, gcols], x1t[:])
                p.dma("sync", X1B[oc * 128:(oc + 1) * 128, gcols], x1b[:])
    es.close()
def emit_s4(nc, p, DR):
    X1B = DR["X1B"]
    WIN = DR["WIN"]
    CW = DR["CW"]
    CB = DR["CB"]
    WXP = DR["WXP"]
    WDT = DR["WDT"]
    BDT = DR["BDT"]
    UB = DR["UB"]
    UF = DR["UF"]
    SRES = DR["SRES"]
    DLT = DR["DLT"]
    BM = DR["BM"]
    CM = DR["CM"]
    U0D = DR["U0D"]
    XDB = DR["XDB"]

    es = contextlib.ExitStack()
    sb = lambda n, s, d: es.enter_context(nc.sbuf_tensor('s4_' + n, s, d))
    ps = lambda n, s, d: es.enter_context(nc.psum_tensor('s4_' + n, s, d))
    win = [sb(f"win{i}", [128, 2 * DI], BF16) for i in range(8)]
    x1b = [sb(f"x1b{i}", [128, 512], BF16) for i in range(8)]
    ev = sb("ev", [128, 512], F32)
    u0p = sb("u0p", [128, T + 3], F32)
    acc = sb("acc", [128, T], F32)
    acc2 = sb("acc2", [128, T], F32)
    cw = sb("cw", [128, 4], F32)
    cb = sb("cb", [128, 1], F32)
    uf = sb("uf", [128, T], F32)
    ub = sb("ub", [128, T], BF16)
    wxp = [sb(f"wxp{i}", [128, 96], BF16) for i in range(16)]
    ubt = [sb(f"ubt{i}", [128, 512], BF16) for i in range(16)]
    xdbt = sb("xdbt", [64, 512], BF16)
    bct = sb("bct", [32, 512], F32)
    wdt = sb("wdt", [64, DI], BF16)
    bdt = sb("bdt", [128, 16], F32)
    ta = sb("ta", [128, 512], F32)
    tl = sb("tl", [128, 512], F32)
    tr = sb("tr", [128, 512], F32)
    dl = sb("dl", [128, 512], F32)
    P1 = ps("P1", [128, 512], F32)
    P96 = ps("P96", [96, 512], F32)

    for i in range(8):
        p.dma("gpsimd", win[i][:], WIN[i * 128:(i + 1) * 128, :])
    for tc in range(8):
        csl = slice(tc * 512, (tc + 1) * 512)
        for i in range(8):
            p.dma("sync", x1b[i][:], X1B[i * 128:(i + 1) * 128, csl])
        for oc in range(32):
            for kc in range(8):
                p.op("tensor", lambda t, oc=oc, kc=kc: t.matmul(
                    P1[:], win[kc][:, oc * 128:(oc + 1) * 128], x1b[kc][:],
                    start=(kc == 0), stop=(kc == 7)))
            if oc < 16:
                p.op("vector", lambda v: v.tensor_copy(ev[:], P1[:]))
                p.dma("sync", U0D[oc * 128:(oc + 1) * 128, csl], ev[:])
            else:
                p.op("scalar", lambda s: s.activation(ev[:], P1[:], AF.Silu))
                p.dma("sync", SRES[(oc - 16) * 128:(oc - 15) * 128, csl], ev[:])
    for ct in range(16):
        p.dma("sync", cw[:], CW[ct])
        p.dma("sync", cb[:], CB[ct])
        for b in range(B):
            rs = slice(ct * 128, (ct + 1) * 128)
            cs = slice(b * T, (b + 1) * T)
            p.op("vector", lambda v: v.memset(u0p[:, 0:3], 0.0))
            p.dma("sync", u0p[:, 3:], U0D[rs, cs])
            p.op("vector", lambda v: v.tensor_scalar(acc[:], u0p[:, 3:3 + T], cw[:, 3:4], None, ALU.mult))
            p.op("vector", lambda v: v.scalar_tensor_tensor(acc2[:], u0p[:, 2:2 + T], cw[:, 2:3], acc[:], ALU.mult, ALU.add))
            p.op("vector", lambda v: v.scalar_tensor_tensor(acc[:], u0p[:, 1:1 + T], cw[:, 1:2], acc2[:], ALU.mult, ALU.add))
            p.op("vector", lambda v: v.scalar_tensor_tensor(acc2[:], u0p[:, 0:T], cw[:, 0:1], acc[:], ALU.mult, ALU.add))
            p.op("scalar", lambda s: s.activation(uf[:], acc2[:], AF.Silu, bias=cb[:]))
            p.op("vector", lambda v: v.tensor_copy(ub[:], uf[:]))
            p.dma("sync", UF[rs, cs], uf[:])
            p.dma("sync", UB[rs, cs], ub[:])
    for i in range(16):
        p.dma("gpsimd", wxp[i][:], WXP[i * 128:(i + 1) * 128, :])
    for tc in range(8):
        csl = slice(tc * 512, (tc + 1) * 512)
        for i in range(16):
            p.dma("sync", ubt[i][:], UB[i * 128:(i + 1) * 128, csl])
        for kc in range(16):
            p.op("tensor", lambda t, kc=kc: t.matmul(
                P96[:], wxp[kc][:], ubt[kc][:], start=(kc == 0), stop=(kc == 15)))
        p.op("vector", lambda v: v.tensor_copy(xdbt[:], P96[0:64, :]))
        p.dma("sync", XDB[:, csl], xdbt[:])
        p.op("vector", lambda v: v.tensor_copy(bct[:], P96[64:96, :]))
        b, tloc = tc // 4, (tc % 4) * 512
        p.dma("sync", BM[b, :, tloc:tloc + 512], bct[0:16, :])
        p.dma("sync", CM[b, :, tloc:tloc + 512], bct[16:32, :])
    p.dma("gpsimd", wdt[:], WDT)
    for i in range(16):
        p.dma("sync", bdt[:, i:i + 1], BDT[i])
    for tc in range(8):
        csl = slice(tc * 512, (tc + 1) * 512)
        p.dma("sync", xdbt[:], XDB[:, csl])
        for oc in range(16):
            p.op("tensor", lambda t, oc=oc: t.matmul(
                P1[:], wdt[:, oc * 128:(oc + 1) * 128], xdbt[:], start=True, stop=True))
            p.op("scalar", lambda s, oc=oc: s.activation(ta[:], P1[:], AF.Abs, bias=bdt[:, oc:oc + 1]))
            p.op("scalar", lambda s: s.activation(tl[:], ta[:], AF.Exp, scale=-1.0))
            p.op("scalar", lambda s: s.activation(tl[:], tl[:], AF.Ln, bias=1.0))
            p.op("scalar", lambda s, oc=oc: s.activation(tr[:], P1[:], AF.Relu, bias=bdt[:, oc:oc + 1]))
            p.op("vector", lambda v: v.tensor_tensor(dl[:], tl[:], tr[:], ALU.add))
            p.dma("sync", DLT[oc * 128:(oc + 1) * 128, csl], dl[:])
    es.close()
def emit_s5(nc, p, DR):
    DLT = DR["DLT"]
    UF = DR["UF"]
    SRES = DR["SRES"]
    BM = DR["BM"]
    CM = DR["CM"]
    ALOG = DR["ALOG"]
    DPV = DR["DPV"]
    WOUT = DR["WOUT"]
    X1F = DR["X1F"]
    ONESF = DR["ONESF"]
    S0F = DR["S0F"]

    es = contextlib.ExitStack()
    sb = lambda n, s, d: es.enter_context(nc.sbuf_tensor('s5_' + n, s, d))
    ps = lambda n, s, d: es.enter_context(nc.psum_tensor('s5_' + n, s, d))
    onesf = sb("onesf", [1, 128], F32)
    An = [sb(f"An{i}", [128, DS], F32) for i in range(16)]
    dpv = sb("dpv", [128, 16], F32)
    wo = [sb(f"wo{i}", [128, D], BF16) for i in range(16)]
    brow = sb("brow", [1, 512], F32)
    crow = sb("crow", [1, 512], F32)
    Bb = sb("Bb", [128, DS * 512], F32)
    Cb = sb("Cb", [128, DS * 512], F32)
    dltc = sb("dltc", [128, 512], F32)
    dudc = sb("dudc", [128, 512], F32)
    ufc = sb("ufc", [128, 512], F32)
    sresc = sb("sresc", [128, 512], F32)
    dA = sb("dA", [128, 512], F32)
    dbu = sb("dbu", [128, 512], F32)
    hbuf = sb("hbuf", [128, 512], F32)
    prod = sb("prod", [128, 512], F32)
    yc = sb("yc", [128, 512], F32)
    hstates = sb("hstates", [128, 16 * DS], F32)
    ygb = sb("ygb", [128, 16 * 512], BF16)
    x1t = sb("x1t", [128, 512], F32)
    s0t = sb("s0t", [128, 512], F32)
    PA = ps("PA", [128, 512], F32)
    PBp = ps("PBp", [128, 512], F32)

    p.dma("sync", onesf[:], ONESF)
    for i in range(16):
        p.dma("sync", An[i][:], ALOG[i])
        p.op("scalar", lambda s, i=i: s.activation(An[i][:], An[i][:], AF.Exp))
        p.op("vector", lambda v, i=i: v.tensor_scalar(An[i][:], An[i][:], -1.0, None, ALU.mult))
        p.dma("sync", dpv[:, i:i + 1], DPV[i])
        p.dma("gpsimd", wo[i][:], WOUT[i * 128:(i + 1) * 128, :])
    for b in range(B):
        p.op("vector", lambda v: v.memset(hstates[:], 0.0))
        for TC in range(4):
            tsl = slice(b * T + TC * 512, b * T + (TC + 1) * 512)
            lsl = slice(TC * 512, (TC + 1) * 512)
            for n in range(DS):
                p.dma("sync", brow[:], BM[b, n:n + 1, lsl])
                p.dma("sync", crow[:], CM[b, n:n + 1, lsl])
                p.op("tensor", lambda t: t.matmul(PA[:], onesf[:], brow[:], start=True, stop=True))
                p.op("tensor", lambda t: t.matmul(PBp[:], onesf[:], crow[:], start=True, stop=True))
                p.op("vector", lambda v, n=n: v.tensor_copy(Bb[:, n * 512:(n + 1) * 512], PA[:]))
                p.op("vector", lambda v, n=n: v.tensor_copy(Cb[:, n * 512:(n + 1) * 512], PBp[:]))
            for ct in range(16):
                rs = slice(ct * 128, (ct + 1) * 128)
                p.dma("sync", dltc[:], DLT[rs, tsl])
                p.dma("sync", ufc[:], UF[rs, tsl])
                p.dma("sync", sresc[:], SRES[rs, tsl])
                p.op("vector", lambda v: v.tensor_tensor(dudc[:], dltc[:], ufc[:], ALU.mult))
                for n in range(DS):
                    nsl = slice(n * 512, (n + 1) * 512)
                    hs = slice(ct * DS + n, ct * DS + n + 1)
                    p.op("scalar", lambda s, n=n, ct=ct: s.activation(
                        dA[:], dltc[:], AF.Exp, scale=An[ct][:, n:n + 1]))
                    p.op("vector", lambda v, nsl=nsl: v.tensor_tensor(dbu[:], dudc[:], Bb[:, nsl], ALU.mult))
                    p.op("vector", lambda v, hs=hs: v.tensor_tensor_scan(
                        hbuf[:], dA[:], dbu[:], hstates[:, hs], ALU.mult, ALU.add))
                    p.op("vector", lambda v, hs=hs: v.tensor_copy(hstates[:, hs], hbuf[:, 511:512]))
                    if n == 0:
                        p.op("vector", lambda v, nsl=nsl: v.tensor_tensor(yc[:], hbuf[:], Cb[:, nsl], ALU.mult))
                    else:
                        p.op("vector", lambda v, nsl=nsl: v.tensor_tensor(prod[:], hbuf[:], Cb[:, nsl], ALU.mult))
                        p.op("vector", lambda v: v.tensor_tensor(yc[:], yc[:], prod[:], ALU.add))
                p.op("vector", lambda v, ct=ct: v.scalar_tensor_tensor(
                    yc[:], ufc[:], dpv[:, ct:ct + 1], yc[:], ALU.mult, ALU.add))
                p.op("vector", lambda v, ct=ct: v.tensor_tensor(
                    ygb[:, ct * 512:(ct + 1) * 512], yc[:], sresc[:], ALU.mult))
            for oc in range(8):
                for ct in range(16):
                    p.op("tensor", lambda t, oc=oc, ct=ct: t.matmul(
                        PA[:], wo[ct][:, oc * 128:(oc + 1) * 128],
                        ygb[:, ct * 512:(ct + 1) * 512], start=(ct == 0), stop=(ct == 15)))
                p.dma("sync", x1t[:], X1F[oc * 128:(oc + 1) * 128, tsl])
                p.op("vector", lambda v: v.tensor_tensor(s0t[:], PA[:], x1t[:], ALU.add))
                p.dma("sync", S0F[oc * 128:(oc + 1) * 128, tsl], s0t[:])
    es.close()
def emit_s6(nc, p, DR):
    XNB = DR["XNB"]
    WF = DR["WF"]
    FB = DR["FB"]
    VR = DR["VR"]
    CT = DR["CT"]
    S0F = DR["S0F"]
    XF = DR["XF"]
    X2F = DR["X2F"]
    X2B = DR["X2B"]

    es = contextlib.ExitStack()
    sb = lambda n, s, d: es.enter_context(nc.sbuf_tensor('s6_' + n, s, d))
    ps = lambda n, s, d: es.enter_context(nc.psum_tensor('s6_' + n, s, d))
    wf = [sb(f"wf{i}", [128, D], BF16) for i in range(8)]
    vr = [sb(f"vr{i}", [128, 32], BF16) for i in range(8)]
    ct = sb("ct", [32, D], BF16)
    fb = sb("fb", [128, 8], F32)
    xnb = [sb(f"xnb{i}", [128, 512], BF16) for i in range(8)]
    t1b = sb("t1b", [32, 512], BF16)
    gt = sb("gt", [128, 512], F32)
    s0t = sb("s0t", [128, 512], F32)
    xft = sb("xft", [128, 512], F32)
    x2t = sb("x2t", [128, 512], F32)
    x2b = sb("x2b", [128, 512], BF16)
    P1 = ps("P1", [128, 512], F32)
    P32 = ps("P32", [32, 512], F32)

    for i in range(8):
        p.dma("gpsimd", wf[i][:], WF[i * 128:(i + 1) * 128, :])
        p.dma("gpsimd", vr[i][:], VR[i * 128:(i + 1) * 128, :])
        p.dma("sync", fb[:, i:i + 1], FB[i])
    p.dma("gpsimd", ct[:], CT)
    for tc in range(8):
        csl = slice(tc * 512, (tc + 1) * 512)
        for i in range(8):
            p.dma("sync", xnb[i][:], XNB[i * 128:(i + 1) * 128, csl])
        for kc in range(8):
            p.op("tensor", lambda t, kc=kc: t.matmul(
                P32[:], vr[kc][:], xnb[kc][:], start=(kc == 0), stop=(kc == 7)))
        p.op("vector", lambda v: v.tensor_copy(t1b[:], P32[:]))
        for oc in range(8):
            for kc in range(8):
                p.op("tensor", lambda t, oc=oc, kc=kc: t.matmul(
                    P1[:], wf[kc][:, oc * 128:(oc + 1) * 128], xnb[kc][:],
                    start=(kc == 0), stop=False))
            p.op("tensor", lambda t, oc=oc: t.matmul(
                P1[:], ct[:, oc * 128:(oc + 1) * 128], t1b[:], start=False, stop=True))
            p.op("scalar", lambda s, oc=oc: s.activation(gt[:], P1[:], AF.Silu, bias=fb[:, oc:oc + 1]))
            p.dma("sync", s0t[:], S0F[oc * 128:(oc + 1) * 128, csl])
            p.dma("sync", xft[:], XF[oc * 128:(oc + 1) * 128, csl])
            p.op("vector", lambda v: v.tensor_tensor(gt[:], gt[:], s0t[:], ALU.mult))
            p.op("vector", lambda v: v.tensor_tensor(x2t[:], gt[:], xft[:], ALU.add))
            p.op("vector", lambda v: v.tensor_copy(x2b[:], x2t[:]))
            p.dma("sync", X2F[oc * 128:(oc + 1) * 128, csl], x2t[:])
            p.dma("sync", X2B[oc * 128:(oc + 1) * 128, csl], x2b[:])
    es.close()
def emit_s7(nc, p, DR):
    X2F = DR["X2F"]
    identf = DR["identf"]
    XN2B = DR["XN2B"]

    es = contextlib.ExitStack()
    sb = lambda n, s, d: es.enter_context(nc.sbuf_tensor('s7_' + n, s, d))
    ps = lambda n, s, d: es.enter_context(nc.psum_tensor('s7_' + n, s, d))
    idf = sb("idf", [128, 128], F32)
    fmt = sb("fmt", [128, 128], F32)
    xtm = sb("xtm", [128, D], F32)
    xsq = sb("xsq", [128, D], F32)
    xn = sb("xn", [128, D], F32)
    s1 = sb("s1", [128, 1], F32)
    s2 = sb("s2", [128, 1], F32)
    mean = sb("mean", [128, 1], F32)
    ex2 = sb("ex2", [128, 1], F32)
    msq = sb("msq", [128, 1], F32)
    var = sb("var", [128, 1], F32)
    rstd = sb("rstd", [128, 1], F32)
    epsc = sb("epsc", [128, 1], F32)
    mr = sb("mr", [128, 1], F32)
    evb = sb("evb", [128, 128], BF16)
    pT = ps("pT", [128, 128], F32)

    p.op("vector", lambda v: v.memset(epsc[:], 1e-5))
    p.dma("sync", idf[:], identf)
    for tt in range(TT // 128):
        for dc in range(8):
            p.dma("sync", fmt[:], X2F[dc * 128:(dc + 1) * 128, tt * 128:(tt + 1) * 128])
            p.op("tensor", lambda t: t.transpose(pT[:], fmt[:], idf[:]))
            p.op("vector", lambda v, dc=dc: v.tensor_copy(xtm[:, dc * 128:(dc + 1) * 128], pT[:]))
        p.op("vector", lambda v: v.tensor_reduce(s1[:], xtm[:], AX.X, ALU.add))
        p.op("scalar", lambda s: s.activation(xsq[:], xtm[:], AF.Square))
        p.op("vector", lambda v: v.tensor_reduce(s2[:], xsq[:], AX.X, ALU.add))
        p.op("vector", lambda v: v.tensor_scalar(mean[:], s1[:], 1.0 / D, None, ALU.mult))
        p.op("vector", lambda v: v.tensor_scalar(ex2[:], s2[:], 1.0 / D, None, ALU.mult))
        p.op("scalar", lambda s: s.activation(msq[:], mean[:], AF.Square))
        p.op("vector", lambda v: v.tensor_tensor(var[:], ex2[:], msq[:], ALU.subtract))
        p.op("scalar", lambda s: s.activation(var[:], var[:], AF.Ln, bias=epsc[:]))
        p.op("scalar", lambda s: s.activation(rstd[:], var[:], AF.Exp, scale=-0.5))
        p.op("vector", lambda v: v.tensor_tensor(mr[:], mean[:], rstd[:], ALU.mult))
        p.op("vector", lambda v: v.tensor_scalar(xn[:], xtm[:], rstd[:], mr[:], ALU.mult, ALU.subtract))
        for dc in range(8):
            p.op("tensor", lambda t, dc=dc: t.transpose(pT[:], xn[:, dc * 128:(dc + 1) * 128], idf[:]))
            p.op("scalar", lambda s: s.copy(evb[:], pT[:]))
            p.dma("sync", XN2B[dc * 128:(dc + 1) * 128, tt * 128:(tt + 1) * 128], evb[:])
    es.close()
def emit_s8(nc, p, DR):
    XN2B = DR["XN2B"]
    WQKVG = DR["WQKVG"]
    WO1 = DR["WO1"]
    BO = DR["BO"]
    BG1 = DR["BG1"]
    DCP = DR["DCP"]
    IOTAM = DR["IOTAM"]
    TRIU = DR["TRIU"]
    IOTP1 = DR["IOTP1"]
    IOT127 = DR["IOT127"]
    C128 = DR["C128"]
    ONESF = DR["ONESF"]
    identb = DR["identb"]
    S1F = DR["S1F"]

    es = contextlib.ExitStack()
    sb = lambda n, s, d: es.enter_context(nc.sbuf_tensor('s8_' + n, s, d))
    ps = lambda n, s, d: es.enter_context(nc.psum_tensor('s8_' + n, s, d))
    wqg = [sb(f"wqg{i}", [128, 193], BF16) for i in range(8)]
    wo1 = sb("wo1", [64, D], BF16)
    bo = sb("bo", [128, 8], F32)
    bg1 = sb("bg1", [1, 1], F32)
    dcp = sb("dcp", [1, 1], F32)
    iotam = sb("iotam", [128, 128], F32)
    triu = sb("triu", [128, 128], F32)
    iotp1 = sb("iotp1", [128, 1], F32)
    iot127 = sb("iot127", [128, 1], F32)
    c128 = sb("c128", [1, 1], F32)
    onesf = sb("onesf", [1, 128], F32)
    idb = sb("idb", [128, 128], BF16)
    dval = sb("dval", [1, 1], F32)
    ldv = sb("ldv", [1, 1], F32)
    ldcol = sb("ldcol", [128, 1], F32)
    Mb = sb("Mb", [128, 128], BF16)
    Mf = sb("Mf", [128, 128], F32)
    apow = sb("apow", [128, 1], F32)
    bpow = sb("bpow", [128, 1], F32)
    d128 = sb("d128", [1, 1], F32)
    d128c = sb("d128c", [64, 1], F32)
    xnb = [sb(f"xnb{i}", [128, 512], BF16) for i in range(8)]
    qf = sb("qf", [64, T], BF16)
    kf = sb("kf", [64, T], BF16)
    vf = sb("vf", [64, T], BF16)
    gr = sb("gr", [1, T], F32)
    gball = sb("gball", [64, T], F32)
    gkb = sb("gkb", [64, T], BF16)
    gvb = sb("gvb", [64, T], BF16)
    gktm = sb("gktm", [128, 16 * 64], BF16)
    gvtm = sb("gvtm", [128, 16 * 64], BF16)
    Sst = sb("Sst", [64, 64], F32)
    S2b = sb("S2b", [64, 64], F32)
    Sbf = sb("Sbf", [64, 64], BF16)
    o1 = sb("o1", [128, 64], F32)
    amd = sb("amd", [128, 128], BF16)
    otm = sb("otm", [128, 16 * 64], BF16)
    ofmb = sb("ofmb", [64, T], BF16)
    s1t = sb("s1t", [128, 512], F32)
    PA = ps("PA", [128, 512], F32)
    PB65 = ps("PB65", [65, 512], F32)
    PC = ps("PC", [128, 128], F32)
    PTb = ps("PTb", [128, 128], BF16)

    for i in range(8):
        p.dma("gpsimd", wqg[i][:], WQKVG[i * 128:(i + 1) * 128, :])
    p.dma("gpsimd", wo1[:], WO1)
    for i in range(8):
        p.dma("sync", bo[:, i:i + 1], BO[i])
    for src, dst in [(BG1, bg1), (DCP, dcp), (IOTAM, iotam), (TRIU, triu),
                     (IOTP1, iotp1), (IOT127, iot127), (C128, c128), (ONESF, onesf)]:
        p.dma("sync", dst[:], src)
    p.dma("gpsimd", idb[:], identb)
    p.op("scalar", lambda s: s.activation(dval[:], dcp[:], AF.Sigmoid))
    p.op("vector", lambda v: v.tensor_scalar(dval[:], dval[:], 0.099, 0.9, ALU.mult, ALU.add))
    p.op("scalar", lambda s: s.activation(ldv[:], dval[:], AF.Ln))
    p.op("tensor", lambda t: t.matmul(PC[:, 0:1], onesf[:], ldv[:], start=True, stop=True))
    p.op("vector", lambda v: v.tensor_copy(ldcol[:], PC[:, 0:1]))
    p.op("scalar", lambda s: s.activation(Mf[:], iotam[:], AF.Exp, scale=ldcol[:]))
    p.op("vector", lambda v: v.tensor_tensor(Mf[:], Mf[:], triu[:], ALU.mult))
    p.op("vector", lambda v: v.tensor_copy(Mb[:], Mf[:]))
    p.op("scalar", lambda s: s.activation(apow[:], iotp1[:], AF.Exp, scale=ldcol[:]))
    p.op("scalar", lambda s: s.activation(bpow[:], iot127[:], AF.Exp, scale=ldcol[:]))
    p.op("scalar", lambda s: s.activation(d128[:], c128[:], AF.Exp, scale=ldv[:]))
    p.op("tensor", lambda t: t.matmul(PC[0:64, 0:1], onesf[:, 0:64], d128[:], start=True, stop=True))
    p.op("vector", lambda v: v.tensor_copy(d128c[:], PC[0:64, 0:1]))
    for b in range(B):
        for tc in range(4):
            csl = slice(b * T + tc * 512, b * T + (tc + 1) * 512)
            lsl = slice(tc * 512, (tc + 1) * 512)
            for i in range(8):
                p.dma("sync", xnb[i][:], XN2B[i * 128:(i + 1) * 128, csl])
            for kc in range(8):
                p.op("tensor", lambda t, kc=kc: t.matmul(
                    PA[:], wqg[kc][:, 0:128], xnb[kc][:], start=(kc == 0), stop=(kc == 7)))
            for kc in range(8):
                p.op("tensor", lambda t, kc=kc: t.matmul(
                    PB65[:], wqg[kc][:, 128:193], xnb[kc][:], start=(kc == 0), stop=(kc == 7)))
            p.op("vector", lambda v, lsl=lsl: v.tensor_copy(qf[:, lsl], PA[0:64, :]))
            p.op("vector", lambda v, lsl=lsl: v.tensor_copy(kf[:, lsl], PA[64:128, :]))
            p.op("vector", lambda v, lsl=lsl: v.tensor_copy(vf[:, lsl], PB65[0:64, :]))
            p.op("scalar", lambda s, lsl=lsl: s.activation(gr[:, lsl], PB65[64:65, :], AF.Sigmoid, bias=bg1[:]))
        for tc in range(4):
            lsl = slice(tc * 512, (tc + 1) * 512)
            p.op("tensor", lambda t, lsl=lsl: t.matmul(PB65[0:64, :], onesf[:, 0:64], gr[:, lsl], start=True, stop=True))
            p.op("vector", lambda v, lsl=lsl: v.tensor_copy(gball[:, lsl], PB65[0:64, :]))
        p.op("vector", lambda v: v.tensor_tensor(gkb[:], kf[:], gball[:], ALU.mult))
        p.op("vector", lambda v: v.tensor_tensor(gvb[:], vf[:], gball[:], ALU.mult))
        for ch in range(16):
            cs = slice(ch * 128, (ch + 1) * 128)
            os_ = slice(ch * 64, (ch + 1) * 64)
            p.op("tensor", lambda t, cs=cs: t.transpose(PTb[:, 0:64], gkb[:, cs], idb[0:64, 0:64]))
            p.op("vector", lambda v, os_=os_: v.tensor_scalar(gktm[:, os_], PTb[:, 0:64], bpow[:], None, ALU.mult))
            p.op("tensor", lambda t, cs=cs: t.transpose(PTb[:, 0:64], gvb[:, cs], idb[0:64, 0:64]))
            p.op("vector", lambda v, os_=os_: v.tensor_copy(gvtm[:, os_], PTb[:, 0:64]))
        p.op("vector", lambda v: v.memset(Sst[:], 0.0))
        for ch in range(16):
            cs = slice(ch * 128, (ch + 1) * 128)
            os_ = slice(ch * 64, (ch + 1) * 64)
            p.op("vector", lambda v: v.tensor_copy(Sbf[:], Sst[:]))
            p.op("tensor", lambda t, cs=cs: t.matmul(PC[:, 0:64], qf[:, cs], Sbf[:], start=True, stop=True))
            p.op("vector", lambda v: v.tensor_scalar(o1[:], PC[:, 0:64], apow[:], None, ALU.mult))
            p.op("tensor", lambda t, cs=cs: t.matmul(PC[:], gkb[:, cs], qf[:, cs], start=True, stop=True))
            p.op("vector", lambda v: v.tensor_tensor(amd[:], PC[:], Mb[:], ALU.mult))
            p.op("tensor", lambda t, os_=os_: t.matmul(PA[:, 0:64], amd[:], gvtm[:, os_], start=True, stop=True))
            p.op("vector", lambda v, os_=os_: v.tensor_tensor(otm[:, os_], PA[:, 0:64], o1[:], ALU.add))
            p.op("vector", lambda v: v.tensor_scalar(S2b[:], Sst[:], d128c[:], None, ALU.mult))
            p.op("tensor", lambda t, os_=os_: t.matmul(PC[0:64, 0:64], gktm[:, os_], gvtm[:, os_], start=True, stop=True))
            p.op("vector", lambda v: v.tensor_tensor(Sst[:], S2b[:], PC[0:64, 0:64], ALU.add))
        for ch in range(16):
            cs = slice(ch * 128, (ch + 1) * 128)
            os_ = slice(ch * 64, (ch + 1) * 64)
            p.op("tensor", lambda t, os_=os_: t.transpose(PTb[0:64, :], otm[:, os_], idb[:]))
            p.op("vector", lambda v, cs=cs: v.tensor_copy(ofmb[:, cs], PTb[0:64, :]))
        for tc in range(4):
            csl = slice(b * T + tc * 512, b * T + (tc + 1) * 512)
            lsl = slice(tc * 512, (tc + 1) * 512)
            for oc in range(8):
                p.op("tensor", lambda t, oc=oc, lsl=lsl: t.matmul(
                    PA[:], wo1[:, oc * 128:(oc + 1) * 128], ofmb[:, lsl], start=True, stop=True))
                p.op("vector", lambda v, oc=oc: v.tensor_scalar(s1t[:], PA[:], bo[:, oc:oc + 1], None, ALU.add))
                p.dma("sync", S1F[oc * 128:(oc + 1) * 128, csl], s1t[:])
    es.close()
def emit_s9(nc, p, DR):
    XN2B = DR["XN2B"]
    X2F = DR["X2F"]
    S1F = DR["S1F"]
    W1T = DR["W1T"]
    W2T = DR["W2T"]
    B1 = DR["B1"]
    B2 = DR["B2"]
    WGT = DR["WGT"]
    BGM = DR["BGM"]
    IOTA8 = DR["IOTA8"]
    IOTA8P = DR["IOTA8P"]
    SEL = DR["SEL"]
    identf = DR["identf"]
    OUT = DR["OUT"]
    YACC = DR["YACC"]

    es = contextlib.ExitStack()
    sb = lambda n, s, d: es.enter_context(nc.sbuf_tensor('s9_' + n, s, d))
    ps = lambda n, s, d: es.enter_context(nc.psum_tensor('s9_' + n, s, d))
    wgt = [sb(f"wgt{i}", [128, 8], BF16) for i in range(8)]
    bgm = sb("bgm", [8, 1], F32)
    iota8 = sb("iota8", [128, 8], F32)
    iota8p = sb("iota8p", [128, 8], F32)
    sel = sb("sel", [8, 8 * 128], F32)
    idf = sb("idf", [128, 128], F32)
    xnb = [sb(f"xnb{i}", [128, 512], BF16) for i in range(8)]
    lgf = sb("lgf", [8, 512], F32)
    lgt = sb("lgt", [128, 8], F32)
    m1 = sb("m1", [128, 1], F32)
    d1 = sb("d1", [128, 8], F32)
    ind1p = sb("ind1p", [128, 8], F32)
    penal = sb("penal", [128, 8], F32)
    idx = sb("idx", [128, 1], F32)
    ind1 = sb("ind1", [128, 8], F32)
    lg2 = sb("lg2", [128, 8], F32)
    m2 = sb("m2", [128, 1], F32)
    ind2 = sb("ind2", [128, 8], F32)
    wdiff = sb("wdiff", [128, 1], F32)
    w1c = sb("w1c", [128, 1], F32)
    w2c = sb("w2c", [128, 1], F32)
    comb = sb("comb", [128, 8], F32)
    combfm = sb("combfm", [8, TT], F32)
    w1tb = [sb(f"w1tb{i}", [128, HID], BF16) for i in range(8)]
    w2tb = [sb(f"w2tb{i}", [128, D], BF16) for i in range(16)]
    b1c = sb("b1c", [128, 16], F32)
    b2c = sb("b2c", [128, 8], F32)
    hba = sb("hba", [128, 16 * 512], BF16)
    cbet = sb("cbet", [128, 512], F32)
    yt = sb("yt", [128, 512], F32)
    ytw = sb("ytw", [128, 512], F32)
    ya = sb("ya", [128, 512], F32)
    g1t = sb("g1t", [128, 512], F32)
    s1t = sb("s1t", [128, 512], F32)
    x2t = sb("x2t", [128, 512], F32)
    x3t = sb("x3t", [128, 512], F32)
    ot = sb("ot", [128, 128], F32)
    P1 = ps("P1", [128, 512], F32)
    P8 = ps("P8", [8, 512], F32)
    Pcb = ps("Pcb", [128, 512], F32)
    PT = ps("PT", [128, 128], F32)

    for i in range(8):
        p.dma("gpsimd", wgt[i][:], WGT[i * 128:(i + 1) * 128, :])
    for src, dst in [(BGM, bgm), (IOTA8, iota8), (IOTA8P, iota8p), (SEL, sel), (identf, idf)]:
        p.dma("sync", dst[:], src)
    for tc in range(8):
        csl = slice(tc * 512, (tc + 1) * 512)
        for i in range(8):
            p.dma("sync", xnb[i][:], XN2B[i * 128:(i + 1) * 128, csl])
        for kc in range(8):
            p.op("tensor", lambda t, kc=kc: t.matmul(
                P8[:], wgt[kc][:], xnb[kc][:], start=(kc == 0), stop=(kc == 7)))
        p.op("vector", lambda v: v.tensor_scalar(lgf[:], P8[:], bgm[:], None, ALU.add))
        for q in range(4):
            tt = tc * 4 + q
            p.op("tensor", lambda t, q=q: t.transpose(PT[:, 0:8], lgf[:, q * 128:(q + 1) * 128], idf[0:8, 0:8]))
            p.op("vector", lambda v: v.tensor_copy(lgt[:], PT[:, 0:8]))
            p.op("vector", lambda v: v.tensor_reduce(m1[:], lgt[:], AX.X, ALU.max))
            p.op("vector", lambda v: v.tensor_scalar(d1[:], lgt[:], m1[:], None, ALU.subtract))
            p.op("vector", lambda v: v.tensor_scalar(ind1p[:], d1[:], 0.0, None, ALU.is_ge))
            p.op("vector", lambda v: v.scalar_tensor_tensor(penal[:], ind1p[:], -1000.0, iota8p[:], ALU.mult, ALU.add))
            p.op("vector", lambda v: v.tensor_reduce(idx[:], penal[:], AX.X, ALU.min))
            p.op("vector", lambda v: v.tensor_scalar(ind1[:], iota8[:], idx[:], None, ALU.subtract))
            p.op("scalar", lambda s: s.activation(ind1[:], ind1[:], AF.Abs))
            p.op("scalar", lambda s: s.activation(ind1[:], ind1[:], AF.Relu, scale=-1.0, bias=1.0))
            p.op("vector", lambda v: v.scalar_tensor_tensor(lg2[:], ind1[:], -1e30, lgt[:], ALU.mult, ALU.add))
            p.op("vector", lambda v: v.tensor_reduce(m2[:], lg2[:], AX.X, ALU.max))
            p.op("vector", lambda v: v.tensor_scalar(d1[:], lg2[:], m2[:], None, ALU.subtract))
            p.op("vector", lambda v: v.tensor_scalar(ind1p[:], d1[:], 0.0, None, ALU.is_ge))
            p.op("vector", lambda v: v.scalar_tensor_tensor(penal[:], ind1p[:], -1000.0, iota8p[:], ALU.mult, ALU.add))
            p.op("vector", lambda v: v.tensor_reduce(idx[:], penal[:], AX.X, ALU.min))
            p.op("vector", lambda v: v.tensor_scalar(ind2[:], iota8[:], idx[:], None, ALU.subtract))
            p.op("scalar", lambda s: s.activation(ind2[:], ind2[:], AF.Abs))
            p.op("scalar", lambda s: s.activation(ind2[:], ind2[:], AF.Relu, scale=-1.0, bias=1.0))
            p.op("vector", lambda v: v.tensor_tensor(wdiff[:], m1[:], m2[:], ALU.subtract))
            p.op("scalar", lambda s: s.activation(w1c[:], wdiff[:], AF.Sigmoid))
            p.op("vector", lambda v: v.tensor_scalar(w2c[:], w1c[:], -1.0, 1.0, ALU.mult, ALU.add))
            p.op("vector", lambda v: v.tensor_scalar(comb[:], ind1[:], w1c[:], None, ALU.mult))
            p.op("vector", lambda v: v.scalar_tensor_tensor(comb[:], ind2[:], w2c[:], comb[:], ALU.mult, ALU.add))
            p.op("tensor", lambda t: t.transpose(PT[0:8, :], comb[:], idf[:]))
            p.op("vector", lambda v, tt=tt: v.tensor_copy(combfm[:, tt * 128:(tt + 1) * 128], PT[0:8, :]))
    for e in range(NE):
        for i in range(8):
            p.dma("gpsimd", w1tb[i][:], W1T[e, i * 128:(i + 1) * 128, :])
        for i in range(16):
            p.dma("gpsimd", w2tb[i][:], W2T[e, i * 128:(i + 1) * 128, :])
        for i in range(16):
            p.dma("sync", b1c[:, i:i + 1], B1[e, i])
        for i in range(8):
            p.dma("sync", b2c[:, i:i + 1], B2[e, i])
        for tc in range(8):
            csl = slice(tc * 512, (tc + 1) * 512)
            for i in range(8):
                p.dma("sync", xnb[i][:], XN2B[i * 128:(i + 1) * 128, csl])
            p.op("tensor", lambda t, e=e, csl=csl: t.matmul(
                Pcb[:], sel[:, e * 128:(e + 1) * 128], combfm[:, csl], start=True, stop=True))
            p.op("vector", lambda v: v.tensor_copy(cbet[:], Pcb[:]))
            for hc in range(16):
                for kc in range(8):
                    p.op("tensor", lambda t, hc=hc, kc=kc: t.matmul(
                        P1[:], w1tb[kc][:, hc * 128:(hc + 1) * 128], xnb[kc][:],
                        start=(kc == 0), stop=(kc == 7)))
                p.op("scalar", lambda s, hc=hc: s.activation(
                    hba[:, hc * 512:(hc + 1) * 512], P1[:], AF.Relu, bias=b1c[:, hc:hc + 1]))
            for oc in range(8):
                for hc in range(16):
                    p.op("tensor", lambda t, oc=oc, hc=hc: t.matmul(
                        P1[:], w2tb[hc][:, oc * 128:(oc + 1) * 128], hba[:, hc * 512:(hc + 1) * 512],
                        start=(hc == 0), stop=(hc == 15)))
                p.op("vector", lambda v, oc=oc: v.tensor_scalar(yt[:], P1[:], b2c[:, oc:oc + 1], None, ALU.add))
                p.op("vector", lambda v: v.tensor_tensor(ytw[:], yt[:], cbet[:], ALU.mult))
                if e == 0:
                    p.dma("gpsimd", YACC[oc * 128:(oc + 1) * 128, csl], ytw[:])
                else:
                    p.dma("gpsimd", YACC[oc * 128:(oc + 1) * 128, csl], ytw[:], accum_op=ALU.add)
    for tc in range(8):
        csl = slice(tc * 512, (tc + 1) * 512)
        for oc in range(8):
            p.dma("sync", ya[:], YACC[oc * 128:(oc + 1) * 128, csl])
            p.op("scalar", lambda s: s.activation(g1t[:], ya[:], AF.Silu))
            p.dma("sync", s1t[:], S1F[oc * 128:(oc + 1) * 128, csl])
            p.dma("sync", x2t[:], X2F[oc * 128:(oc + 1) * 128, csl])
            p.op("vector", lambda v: v.tensor_tensor(g1t[:], g1t[:], s1t[:], ALU.mult))
            p.op("vector", lambda v: v.tensor_tensor(x3t[:], g1t[:], x2t[:], ALU.add))
            for q in range(4):
                p.op("tensor", lambda t, q=q: t.transpose(PT[:], x3t[:, q * 128:(q + 1) * 128], idf[:]))
                p.op("vector", lambda v: v.tensor_copy(ot[:], PT[:]))
                p.dma("sync", OUT[tc * 512 + q * 128:tc * 512 + (q + 1) * 128, oc * 128:(oc + 1) * 128], ot[:])
    es.close()




def declare_drams(nc, decls):
    D = {}
    for name, kind in decls:
        shape, dtype = DRAM_SPECS[name]
        D[name] = nc.dram_tensor(name, shape, dtype, kind=kind).ap()
    return D


def _finish2(nc, p):
    with contextlib.ExitStack() as es:
        sems = _make_sems(nc, p, es)
        with nc.Block() as block:
            p.finalize(block, sems)


EMITS = {}


def _mk_stage_builder(sname):
    def build():
        nc = new_nc()
        D = declare_drams(nc, STAGE_DECLS[sname])
        p = Prog(nc)
        EMITS[sname](nc, p, D)
        _finish2(nc, p)
        return nc
    build.__name__ = "build_" + sname
    return build


FUSED_INPUT_NAMES = None


def fused_decls():
    produced = set()
    consumed = set()
    order = []
    for sname in ["s1", "s2", "s3", "s4", "s5", "s6", "s7", "s8", "s9"]:
        for name, kind in STAGE_DECLS[sname]:
            if name not in produced and name not in consumed and name not in [n for n, _ in order]:
                order.append((name, None))
            if kind == "ExternalOutput":
                produced.add(name)
            elif kind == "ExternalInput":
                consumed.add(name)
    decls = []
    for name, _ in order:
        if name == "OUT":
            decls.append((name, "ExternalOutput"))
        elif name in produced:
            decls.append((name, "Internal"))
        elif name in consumed:
            decls.append((name, "ExternalInput"))
        else:
            decls.append((name, "Internal"))
    return decls


def build_fused():
    nc = new_nc()
    D = declare_drams(nc, fused_decls())
    p = Prog(nc)
    for sname in ["s1", "s2", "s3", "s4", "s5", "s6", "s7", "s8", "s9"]:
        EMITS[sname](nc, p, D)
    _finish2(nc, p)
    return nc


EMITS.update(s1=emit_s1, s2=emit_s2, s3=emit_s3, s4=emit_s4, s5=emit_s5, s6=emit_s6, s7=emit_s7, s8=emit_s8, s9=emit_s9)
build_s1 = _mk_stage_builder('s1')
build_s2 = _mk_stage_builder('s2')
build_s3 = _mk_stage_builder('s3')
build_s4 = _mk_stage_builder('s4')
build_s5 = _mk_stage_builder('s5')
build_s6 = _mk_stage_builder('s6')
build_s7 = _mk_stage_builder('s7')
build_s8 = _mk_stage_builder('s8')
build_s9 = _mk_stage_builder('s9')


def _f32(a):
    return np.ascontiguousarray(a, dtype=np.float32)


def make_consts():
    c = {}
    c["identf"] = np.eye(128, dtype=np.float32)
    c["identb"] = np.eye(128, dtype=np.float32).astype(BF)
    m = np.zeros((128, 4 * 512), dtype=np.float32)
    kl = np.arange(128)[:, None]
    ql = np.arange(512)[None, :]
    for rr in range(4):
        m[:, rr * 512:(rr + 1) * 512] = (ql >= kl + 128 * rr)
    c["MASKC"] = m.astype(BF)
    c["ONESF"] = np.ones((1, 128), dtype=np.float32)
    c["IOTAM"] = (np.arange(128)[None, :] - np.arange(128)[:, None]).astype(np.float32)
    c["TRIU"] = (np.arange(128)[None, :] >= np.arange(128)[:, None]).astype(np.float32)
    c["IOTP1"] = (np.arange(128)[:, None] + 1).astype(np.float32)
    c["IOT127"] = (127 - np.arange(128)[:, None]).astype(np.float32)
    c["C128"] = np.full((1, 1), 128.0, dtype=np.float32)
    c["IOTA8"] = np.broadcast_to(np.arange(8, dtype=np.float32), (128, 8)).copy()
    c["IOTA8P"] = c["IOTA8"] + 1000.0
    sel = np.zeros((8, 8 * 128), dtype=np.float32)
    for e in range(8):
        sel[e, e * 128:(e + 1) * 128] = 1.0
    c["SEL"] = sel
    return c


FUSED = True


def host_inputs(inputs):
    cst = make_consts()
    m = dict(cst)
    m["WQKV"] = _f32(inputs["L0_attn_wqkv"])
    m["BQ"] = _f32(inputs["L0_attn_bqkv"]).reshape(9, 128, 1)
    m["WAO"] = _f32(inputs["L0_attn_w_attn_out"])
    m["BAO"] = _f32(inputs["L0_attn_b_attn_out"]).reshape(8, 128, 1)
    m["WIN"] = _f32(inputs["L0_attn_w_in"])
    m["CW"] = _f32(inputs["L0_attn_conv_w"]).reshape(16, 128, 4)
    m["CB"] = _f32(inputs["L0_attn_conv_b"]).reshape(16, 128, 1)
    m["WXP"] = _f32(inputs["L0_attn_w_xproj"])
    m["WDT"] = _f32(inputs["L0_attn_w_dt"])
    m["BDT"] = _f32(inputs["L0_attn_b_dt"]).reshape(16, 128, 1)
    m["ALOG"] = _f32(inputs["L0_attn_A_log"]).reshape(16, 128, DS)
    m["DPV"] = _f32(inputs["L0_attn_Dp"]).reshape(16, 128, 1)
    m["WOUT"] = _f32(inputs["L0_attn_w_out"])
    m["WF"] = _f32(np.asarray(inputs["L0_ffn_W"]).T)
    m["FB"] = _f32(inputs["L0_ffn_b"]).reshape(8, 128, 1)
    m["VR"] = _f32(inputs["L0_ffn_V_r"])
    m["CT"] = _f32(np.asarray(inputs["L0_ffn_C"]).T)
    m["WQKVG"] = _f32(np.concatenate([inputs["L1_attn_Wq"], inputs["L1_attn_Wk"],
                                      inputs["L1_attn_Wv"], inputs["L1_attn_Wg"]], axis=1))
    m["WO1"] = _f32(inputs["L1_attn_Wo"])
    m["BO"] = _f32(inputs["L1_attn_bo"]).reshape(8, 128, 1)
    m["BG1"] = _f32(inputs["L1_attn_bg"]).reshape(1, 1)
    m["DCP"] = _f32(inputs["L1_attn_decay_param"]).reshape(1, 1)
    m["W1T"] = _f32(np.transpose(np.asarray(inputs["L1_ffn_W1"]), (0, 2, 1)))
    m["W2T"] = _f32(np.transpose(np.asarray(inputs["L1_ffn_W2"]), (0, 2, 1)))
    m["B1"] = _f32(inputs["L1_ffn_b1"]).reshape(NE, 16, 128, 1)
    m["B2"] = _f32(inputs["L1_ffn_b2"]).reshape(NE, 8, 128, 1)
    m["WGT"] = _f32(np.asarray(inputs["L1_ffn_Wg"]).T)
    m["BGM"] = _f32(inputs["L1_ffn_bg"]).reshape(NE, 1)
    return m


def kernel(**inputs):
    x = _f32(inputs["L0_attn_x"])
    shared_map = host_inputs(inputs)
    in_names = {n for n, k in fused_decls() if k == "ExternalInput"}
    base = {k: v for k, v in shared_map.items() if k in in_names}
    maps = []
    for c in range(NC):
        m = dict(base)
        m["XTM"] = np.ascontiguousarray(x[B * c:B * (c + 1)].reshape(TT, D))
        maps.append(m)
    res = _run(build_fused, maps)
    out = np.empty((16, 2048, 1024), dtype=np.float32)
    for c in range(NC):
        out[B * c:B * (c + 1)] = np.asarray(res[c]["OUT"], dtype=np.float32).reshape(B, T, D)
    return out


# version 08
# speedup vs baseline: 1.5941x; 1.1449x over previous
import contextlib
import numpy as np
import ml_dtypes
import concourse.bass as bass
import concourse.mybir as mybir
from concourse.bass_utils import run_bass_kernel_spmd

F32 = mybir.dt.float32
BF16 = mybir.dt.bfloat16
U8 = mybir.dt.uint8
AF = mybir.ActivationFunctionType
ALU = mybir.AluOpType
AX = mybir.AxisListType
BF = ml_dtypes.bfloat16

NC = 8
B = 2
T = 2048
TT = B * T
D = 1024
DI = 2048
NHEAD = 16
HD = 64
DS = 16
DTR = 64
NE = 8
HID = 2048


def _is_ap(x):
    return hasattr(x, "space") and hasattr(x, "offset") and hasattr(x, "tensor")


class _FakeInstr:
    def then_inc(self, *a, **k):
        return self


class _Recorder:
    def __init__(self):
        self.calls = []

    def __getattr__(self, name):
        calls = self.calls

        def m(*a, **k):
            calls.append((name, a, k))
            return _FakeInstr()
        return m


class Prog:
    ENGINES = ("sync", "vector", "scalar", "tensor", "gpsimd")
    MODE = "dep"
    PAIRS = ()

    def __init__(self, nc):
        self.nc = nc
        self.seq = {e: [] for e in self.ENGINES}
        self.cmp_n = {e: 0 for e in self.ENGINES}
        self.dma_n = {e: 0 for e in self.ENGINES}
        self.last_tok = {}
        self.last_psum = {}
        self.tmap = {}
        self.ivals = []
        self.parent = {}
        self.gstate = {}
        self.ngid = 0

    def _find(self, g):
        while self.parent[g] != g:
            self.parent[g] = self.parent[self.parent[g]]
            g = self.parent[g]
        return g

    def _union(self, a, b):
        ra, rb = self._find(a), self._find(b)
        if ra == rb:
            return ra
        self.parent[rb] = ra
        wa, rda = self.gstate[ra]
        wb, rdb = self.gstate.pop(rb)
        self.gstate[ra] = [wa | wb, rda | rdb]
        return ra

    def _new_gid(self):
        g = self.ngid
        self.ngid += 1
        self.parent[g] = g
        self.gstate[g] = [set(), set()]
        return g

    def _gid(self, ap):
        name = ap.name
        g = self.tmap.get(name)
        if g is not None:
            return self._find(g)
        sp = str(ap.space)
        if "DRAM" in sp:
            g = self._new_gid()
            self.tmap[name] = g
            return g
        t = ap.tensor
        ml = self.nc.lookup_mloc(t)
        shape = list(t.shape)
        per_part = 1
        for s in shape[1:]:
            per_part *= int(s)
        per_part *= mybir.dt.size(t.dtype)
        b0 = int(ml.addr)
        b1 = b0 + per_part
        p0 = int(t.base_partition)
        p1 = p0 + int(shape[0])
        if "PSUM" in sp:
            p0, p1 = 0, 128
            b0 = int(ml.bank) * 2048
            b1 = b0 + ((per_part + 2047) // 2048) * 2048
        g = None
        for (sp2, q0, q1, c0, c1, g2) in self.ivals:
            if sp2 == sp and q0 < p1 and p0 < q1 and c0 < b1 and b0 < c1:
                g = g2 if g is None else self._union(g, g2)
        if g is None:
            g = self._new_gid()
        else:
            g = self._find(g)
        self.ivals.append((sp, p0, p1, b0, b1, g))
        self.tmap[name] = g
        return g

    def _emit(self, eng, kind, fn, reads, writes):
        touches_psum = any("PSUM" in str(ap.space) for ap in reads + writes)
        waits = set()
        rg = [self._find(self._gid(ap)) for ap in reads]
        wg = [self._find(self._gid(ap)) for ap in writes]
        rg = [self._find(g) for g in rg]
        wg = [self._find(g) for g in wg]
        for g in rg:
            waits |= self.gstate[g][0]
        for g in wg:
            st = self.gstate[g]
            waits |= st[0]
            waits |= st[1]
        if kind == "dma":
            self.dma_n[eng] += 1
            tok = (eng, "dma", self.dma_n[eng])
            w = 16 if eng == "gpsimd" else 32
            if tok[2] > w:
                waits.add((eng, "dma", tok[2] - w))
        else:
            self.cmp_n[eng] += 1
            tok = (eng, "cmp", self.cmp_n[eng])
        if self.MODE == "serial" or (self.MODE == "dmaserial" and kind == "dma"):
            waits |= set(self.last_tok.values())
        if self.MODE == "engserial" and kind == "cmp":
            waits |= {t for (k, e), t in self.last_tok.items() if k == "cmp"}
        if self.MODE == "pair" and kind == "cmp":
            for a, b in self.PAIRS:
                other = b if eng == a else (a if eng == b else None)
                if other is not None:
                    t = self.last_tok.get(("cmp", other))
                    if t is not None:
                        waits.add(t)
        if (self.MODE == "psumpair" and kind == "cmp"
                and eng in ("vector", "scalar") and touches_psum):
            other = "scalar" if eng == "vector" else "vector"
            t = self.last_psum.get(other)
            if t is not None:
                waits.add(t)
            self.last_psum[eng] = tok
        if self.MODE == "serial":
            self.last_tok = {("x", "x"): tok}
        else:
            self.last_tok[(kind, eng)] = tok
        waits.discard(tok)
        for g in rg:
            self.gstate[g][1].add(tok)
        for g in wg:
            self.gstate[g][0] = {tok}
            self.gstate[g][1] = set()
        self.seq[eng].append((kind, fn, waits))

    def op(self, eng, fn):
        rec = _Recorder()
        fn(rec)
        reads, writes = [], []
        for (meth, a, k) in rec.calls:
            aps = [x for x in a if _is_ap(x)]
            kaps = {n: x for n, x in k.items() if _is_ap(x)}
            if "out" in kaps:
                out = kaps.pop("out")
            elif aps:
                out = aps.pop(0)
            else:
                out = None
            if out is not None:
                writes.append(out)
            reads.extend(aps)
            reads.extend(kaps.values())
            if meth == "matmul" and out is not None and k.get("start") is False:
                reads.append(out)
        writes = writes + [ap for ap in reads if "PSUM" in str(ap.space)]
        self._emit(eng, "cmp", fn, reads, writes)

    def dma(self, eng, out, in_, **kw):
        def fn(E, out=out, in_=in_, kw=kw):
            return E.dma_start(out=out, in_=in_, **kw)
        reads = [in_]
        if kw.get("accum_op") is not None:
            reads.append(out)
        self._emit(eng, "dma", fn, reads, [out])

    def finalize(self, block, sems):
        finals = {}
        for e in self.ENGINES:
            if self.cmp_n[e]:
                finals[("cmp", e)] = self.cmp_n[e]
            if self.dma_n[e]:
                finals[("dma", e)] = 16 * self.dma_n[e]
        for eng in self.ENGINES:
            lst = self.seq[eng]

            def make_section(lst=lst, eng=eng):
                def section(E):
                    hw = {}
                    for (kind, fn, waits) in lst:
                        need = {}
                        for (teng, tkind, tidx) in waits:
                            if (tkind == "cmp" and teng == "tensor"
                                    and eng == "tensor" and kind == "cmp"):
                                continue
                            key = (tkind, teng)
                            val = tidx if tkind == "cmp" else 16 * tidx
                            if hw.get(key, 0) >= val:
                                continue
                            if need.get(key, 0) < val:
                                need[key] = val
                        for key, val in need.items():
                            E.wait_ge(sems[key], val)
                            hw[key] = val
                        r = fn(E)
                        r.then_inc(sems[(kind, eng)], 16 if kind == "dma" else 1)
                    for key, tot in finals.items():
                        if hw.get(key, 0) < tot:
                            E.wait_ge(sems[key], tot)
                return section
            getattr(block, eng)(make_section())


def new_nc():
    return bass.Bass("TRN2", target_bir_lowering=False)


def _run(build_fn, per_core_maps, cache={}):
    key = build_fn.__name__
    if key not in cache:
        cache[key] = build_fn()
    nc = cache[key]
    res = run_bass_kernel_spmd(nc, per_core_maps, core_ids=list(range(len(per_core_maps))))
    return res.results


def _make_sems(nc, p, es):
    sems = {}
    for eng in Prog.ENGINES:
        if p.cmp_n[eng]:
            sems[("cmp", eng)] = es.enter_context(nc.semaphore("c_" + eng))
        if p.dma_n[eng]:
            sems[("dma", eng)] = es.enter_context(nc.semaphore("d_" + eng))
    return sems


DRAM_SPECS = {
    "XTM": ([TT, D], F32),
    "identf": ([128, 128], F32),
    "XF": ([D, TT], F32),
    "XNF": ([D, TT], F32),
    "XNB": ([D, TT], BF16),
    "WQKV": ([D, 1152], F32),
    "BQ": ([9, 128, 1], F32),
    "identb": ([128, 128], BF16),
    "QH": ([B, 64, NHEAD * T], BF16),
    "KB": ([B, 64, T], BF16),
    "VA": ([B, 128, 16 * 65], BF16),
    "WAO": ([D, D], F32),
    "BAO": ([8, 128, 1], F32),
    "MASKC": ([128, 4 * 512], BF16),
    "ONESF": ([1, 128], F32),
    "X1F": ([D, TT], F32),
    "X1B": ([D, TT], BF16),
    "WIN": ([D, 2 * DI], F32),
    "CW": ([16, 128, 4], F32),
    "CB": ([16, 128, 1], F32),
    "WXP": ([DI, 96], F32),
    "WDT": ([DTR, DI], F32),
    "BDT": ([16, 128, 1], F32),
    "UB": ([DI, TT], BF16),
    "UF": ([DI, TT], F32),
    "SRES": ([DI, TT], F32),
    "DLT": ([DI, TT], F32),
    "BM": ([B, DS, T], F32),
    "CM": ([B, DS, T], F32),
    "U0D": ([DI, TT], F32),
    "XDB": ([DTR, TT], BF16),
    "ALOG": ([128, 16 * DS], F32),
    "DPV": ([128, 16], F32),
    "SEL16": ([16, 16 * 128], F32),
    "WOUT": ([DI, D], F32),
    "S0F": ([D, TT], F32),
    "WF": ([D, D], F32),
    "FB": ([8, 128, 1], F32),
    "VR": ([D, 32], F32),
    "CT": ([32, D], F32),
    "X2F": ([D, TT], F32),
    "X2B": ([D, TT], BF16),
    "XN2B": ([D, TT], BF16),
    "WQKVG": ([D, 193], F32),
    "WO1": ([64, D], F32),
    "BO": ([8, 128, 1], F32),
    "BG1": ([1, 1], F32),
    "DCP": ([1, 1], F32),
    "IOTAM": ([128, 128], F32),
    "TRIU": ([128, 128], F32),
    "IOTP1": ([128, 1], F32),
    "IOT127": ([128, 1], F32),
    "C128": ([1, 1], F32),
    "S1F": ([D, TT], F32),
    "W1T": ([NE, D, HID], F32),
    "W2T": ([NE, HID, D], F32),
    "B1": ([128, NE * 16], F32),
    "B2": ([128, NE * 8], F32),
    "WGT": ([D, NE], F32),
    "BGM": ([NE, 1], F32),
    "IOTA8": ([128, 8], F32),
    "IOTA8P": ([128, 8], F32),
    "SEL": ([8, 8 * 128], F32),
    "OUT": ([TT, D], F32),
    "YACC": ([D, TT], F32),
}

STAGE_DECLS = {
    "s1": [('XTM', 'ExternalInput'), ('identf', 'ExternalInput'), ('XF', 'ExternalOutput'), ('XNF', 'ExternalOutput'), ('XNB', 'ExternalOutput')],
    "s2": [('XNB', 'ExternalInput'), ('WQKV', 'ExternalInput'), ('BQ', 'ExternalInput'), ('identb', 'ExternalInput'), ('QH', 'ExternalOutput'), ('KB', 'ExternalOutput'), ('VA', 'ExternalOutput')],
    "s3": [('QH', 'ExternalInput'), ('KB', 'ExternalInput'), ('VA', 'ExternalInput'), ('WAO', 'ExternalInput'), ('BAO', 'ExternalInput'), ('XNF', 'ExternalInput'), ('MASKC', 'ExternalInput'), ('ONESF', 'ExternalInput'), ('X1F', 'ExternalOutput'), ('X1B', 'ExternalOutput')],
    "s4": [('X1B', 'ExternalInput'), ('WIN', 'ExternalInput'), ('CW', 'ExternalInput'), ('CB', 'ExternalInput'), ('WXP', 'ExternalInput'), ('WDT', 'ExternalInput'), ('BDT', 'ExternalInput'), ('UB', 'ExternalOutput'), ('UF', 'ExternalOutput'), ('SRES', 'ExternalOutput'), ('DLT', 'ExternalOutput'), ('BM', 'ExternalOutput'), ('CM', 'ExternalOutput'), ('U0D', 'Internal'), ('XDB', 'Internal')],
    "s5": [('DLT', 'ExternalInput'), ('UF', 'ExternalInput'), ('SRES', 'ExternalInput'), ('BM', 'ExternalInput'), ('CM', 'ExternalInput'), ('ALOG', 'ExternalInput'), ('DPV', 'ExternalInput'), ('WOUT', 'ExternalInput'), ('X1F', 'ExternalInput'), ('SEL16', 'ExternalInput'), ('S0F', 'ExternalOutput')],
    "s6": [('XNB', 'ExternalInput'), ('WF', 'ExternalInput'), ('FB', 'ExternalInput'), ('VR', 'ExternalInput'), ('CT', 'ExternalInput'), ('S0F', 'ExternalInput'), ('XF', 'ExternalInput'), ('X2F', 'ExternalOutput'), ('X2B', 'ExternalOutput')],
    "s7": [('X2F', 'ExternalInput'), ('identf', 'ExternalInput'), ('XN2B', 'ExternalOutput')],
    "s8": [('XN2B', 'ExternalInput'), ('WQKVG', 'ExternalInput'), ('WO1', 'ExternalInput'), ('BO', 'ExternalInput'), ('BG1', 'ExternalInput'), ('DCP', 'ExternalInput'), ('IOTAM', 'ExternalInput'), ('TRIU', 'ExternalInput'), ('IOTP1', 'ExternalInput'), ('IOT127', 'ExternalInput'), ('C128', 'ExternalInput'), ('ONESF', 'ExternalInput'), ('identb', 'ExternalInput'), ('S1F', 'ExternalOutput')],
    "s9": [('XN2B', 'ExternalInput'), ('X2F', 'ExternalInput'), ('S1F', 'ExternalInput'), ('W1T', 'ExternalInput'), ('W2T', 'ExternalInput'), ('B1', 'ExternalInput'), ('B2', 'ExternalInput'), ('WGT', 'ExternalInput'), ('BGM', 'ExternalInput'), ('IOTA8', 'ExternalInput'), ('IOTA8P', 'ExternalInput'), ('SEL', 'ExternalInput'), ('identf', 'ExternalInput'), ('OUT', 'ExternalOutput'), ('YACC', 'Internal')],
}

def emit_s1(nc, p, DR):
    XTM = DR["XTM"]
    identf = DR["identf"]
    XF = DR["XF"]
    XNF = DR["XNF"]
    XNB = DR["XNB"]

    es = contextlib.ExitStack()
    sb = lambda n, s, d: es.enter_context(nc.sbuf_tensor('s1_' + n, s, d))
    ps = lambda n, s, d: es.enter_context(nc.psum_tensor('s1_' + n, s, d))
    xt = sb("xt", [128, D], F32)
    xsq = sb("xsq", [128, D], F32)
    xn = sb("xn", [128, D], F32)
    idf = sb("idf", [128, 128], F32)
    s1 = sb("s1", [128, 1], F32)
    s2 = sb("s2", [128, 1], F32)
    mean = sb("mean", [128, 1], F32)
    ex2 = sb("ex2", [128, 1], F32)
    msq = sb("msq", [128, 1], F32)
    var = sb("var", [128, 1], F32)
    rstd = sb("rstd", [128, 1], F32)
    epsc = sb("epsc", [128, 1], F32)
    mr = sb("mr", [128, 1], F32)
    ev = sb("ev", [128, 128], F32)
    evb = sb("evb", [128, 128], BF16)
    pT = ps("pT", [128, 128], F32)

    p.op("vector", lambda v: v.memset(epsc[:], 1e-5))
    p.dma("sync", idf[:], identf)
    for tt in range(TT // 128):
        p.dma("sync", xt[:], XTM[tt * 128:(tt + 1) * 128, :])
        p.op("vector", lambda v: v.tensor_reduce(s1[:], xt[:], AX.X, ALU.add))
        p.op("scalar", lambda s: s.activation(xsq[:], xt[:], AF.Square))
        p.op("vector", lambda v: v.tensor_reduce(s2[:], xsq[:], AX.X, ALU.add))
        p.op("vector", lambda v: v.tensor_scalar(mean[:], s1[:], 1.0 / D, None, ALU.mult))
        p.op("vector", lambda v: v.tensor_scalar(ex2[:], s2[:], 1.0 / D, None, ALU.mult))
        p.op("scalar", lambda s: s.activation(msq[:], mean[:], AF.Square))
        p.op("vector", lambda v: v.tensor_tensor(var[:], ex2[:], msq[:], ALU.subtract))
        p.op("scalar", lambda s: s.activation(var[:], var[:], AF.Ln, bias=epsc[:]))
        p.op("scalar", lambda s: s.activation(rstd[:], var[:], AF.Exp, scale=-0.5))
        p.op("vector", lambda v: v.tensor_tensor(mr[:], mean[:], rstd[:], ALU.mult))
        p.op("vector", lambda v: v.tensor_scalar(xn[:], xt[:], rstd[:], mr[:], ALU.mult, ALU.subtract))
        for dc in range(D // 128):
            csl = slice(dc * 128, (dc + 1) * 128)
            p.op("tensor", lambda t, csl=csl: t.transpose(pT[:], xn[:, csl], idf[:]))
            p.op("vector", lambda v: v.tensor_copy(ev[:], pT[:]))
            p.op("scalar", lambda s: s.copy(evb[:], pT[:]))
            p.dma("sync", XNF[dc * 128:(dc + 1) * 128, tt * 128:(tt + 1) * 128], ev[:])
            p.dma("sync", XNB[dc * 128:(dc + 1) * 128, tt * 128:(tt + 1) * 128], evb[:])
            p.op("tensor", lambda t, csl=csl: t.transpose(pT[:], xt[:, csl], idf[:]))
            p.op("vector", lambda v: v.tensor_copy(ev[:], pT[:]))
            p.dma("sync", XF[dc * 128:(dc + 1) * 128, tt * 128:(tt + 1) * 128], ev[:])
    es.close()
def emit_s2(nc, p, DR):
    XNB = DR["XNB"]
    WQKV = DR["WQKV"]
    BQ = DR["BQ"]
    identb = DR["identb"]
    QH = DR["QH"]
    KB = DR["KB"]
    VA = DR["VA"]

    es = contextlib.ExitStack()
    sb = lambda n, s, d: es.enter_context(nc.sbuf_tensor('s2_' + n, s, d))
    ps = lambda n, s, d: es.enter_context(nc.psum_tensor('s2_' + n, s, d))
    wq = [sb(f"wq{i}", [128, 1152], BF16) for i in range(8)]
    bq = sb("bq", [128, 9], F32)
    xnb = [sb(f"xnb{i}", [128, 512], BF16) for i in range(8)]
    idb = sb("idb", [128, 128], BF16)
    qtmp = sb("qtmp", [64, 512], BF16)
    ktmp = sb("ktmp", [64, 512], BF16)
    vtmp = sb("vtmp", [64, 512], BF16)
    vat = sb("vat", [128, 64], BF16)
    onec = sb("onec", [128, 1], BF16)
    P1 = ps("P1", [128, 512], F32)
    PT = ps("PT", [128, 128], BF16)

    p.dma("sync", idb[:], identb)
    p.op("vector", lambda v: v.memset(onec[:], 1.0))
    for i in range(8):
        p.dma("gpsimd", wq[i][:], WQKV[i * 128:(i + 1) * 128, :])
    for i in range(9):
        p.dma("sync", bq[:, i:i + 1], BQ[i])
    for b in range(B):
        for kt in range(16):
            p.dma("sync", VA[b, :, kt * 65 + 64:kt * 65 + 65], onec[:], allow_slow_non_contiguous=True)
    for tc in range(8):
        b, tloc = tc // 4, (tc % 4) * 512
        csl = slice(tc * 512, (tc + 1) * 512)
        for i in range(8):
            p.dma("sync", xnb[i][:], XNB[i * 128:(i + 1) * 128, csl])
        for oc in range(9):
            for kc in range(8):
                p.op("tensor", lambda t, oc=oc, kc=kc: t.matmul(
                    P1[:], wq[kc][:, oc * 128:(oc + 1) * 128], xnb[kc][:],
                    start=(kc == 0), stop=(kc == 7)))
            if oc < 8:
                for half in range(2):
                    h = 2 * oc + half
                    hs = slice(half * 64, half * 64 + 64)
                    p.op("vector", lambda v, hs=hs, oc=oc: v.tensor_scalar(
                        qtmp[:], P1[hs, :], bq[hs, oc:oc + 1], 0.125, ALU.add, ALU.mult))
                    p.dma("sync", QH[b, :, h * T + tloc:h * T + tloc + 512], qtmp[:])
            else:
                p.op("vector", lambda v: v.tensor_scalar(
                    ktmp[:], P1[0:64, :], bq[0:64, 8:9], None, ALU.add))
                p.dma("sync", KB[b, :, tloc:tloc + 512], ktmp[:])
                p.op("vector", lambda v: v.tensor_scalar(
                    vtmp[:], P1[64:128, :], bq[64:128, 8:9], None, ALU.add))
                for i in range(4):
                    kt = (tloc // 128) + i
                    p.op("tensor", lambda t, i=i: t.transpose(PT[:, 0:64], vtmp[:, i * 128:(i + 1) * 128], idb[0:64, 0:64]))
                    p.op("vector", lambda v: v.tensor_copy(vat[:], PT[:, 0:64]))
                    p.dma("sync", VA[b, :, kt * 65:kt * 65 + 64], vat[:])
    es.close()
def emit_s3(nc, p, DR):
    QH = DR["QH"]
    KB = DR["KB"]
    VA = DR["VA"]
    WAO = DR["WAO"]
    BAO = DR["BAO"]
    XNF = DR["XNF"]
    MASKC = DR["MASKC"]
    ONESF = DR["ONESF"]
    X1F = DR["X1F"]
    X1B = DR["X1B"]

    es = contextlib.ExitStack()
    sb = lambda n, s, d: es.enter_context(nc.sbuf_tensor('s3_' + n, s, d))
    ps = lambda n, s, d: es.enter_context(nc.psum_tensor('s3_' + n, s, d))
    qh = sb("qh", [64, NHEAD * T], BF16)
    kb = sb("kb", [64, T], BF16)
    va = sb("va", [128, 16 * 65], BF16)
    wao = [sb(f"wao{i}", [128, D], BF16) for i in range(8)]
    bao = sb("bao", [128, 8], F32)
    maskc = sb("maskc", [128, 4 * 512], BF16)
    onesf = sb("onesf", [1, 128], F32)
    onesb64 = sb("onesb64", [64, 1], BF16)
    sqb = sb("sqb", [64, 2048], BF16)
    sums = sb("sums", [1, 2048], F32)
    red = sb("red", [1, 1], F32)
    qmax2 = sb("qmax2", [1, 1], F32)
    kmax2 = sb("kmax2", [1, 1], F32)
    bmag = sb("bmag", [1, 1], F32)
    negb = sb("negb", [1, 1], F32)
    negbc = sb("negbc", [128, 1], F32)
    et = sb("et", [128, 512], BF16)
    et2 = sb("et2", [128, 512], BF16)
    rc = sb("rc", [1, 512], F32)
    rcb = sb("rcb", [64, 512], F32)
    ohx = sb("ohx", [64, 512], BF16)
    ofm = sb("ofm", [128, 8 * T], BF16)
    xattn = sb("xattn", [128, 512], F32)
    xnt = sb("xnt", [128, 512], F32)
    x1t = sb("x1t", [128, 512], F32)
    x1b = sb("x1b", [128, 512], BF16)
    PS = ps("PS", [128, 512], F32)
    PO = ps("PO", [65, 512], F32)
    PB = ps("PB", [64, 512], F32)
    PN = ps("PN", [1, 2048], F32)

    p.dma("sync", maskc[:], MASKC)
    p.dma("sync", onesf[:], ONESF)
    p.op("vector", lambda v: v.memset(onesb64[:], 1.0))
    for i in range(8):
        p.dma("gpsimd", wao[i][:], WAO[i * 128:(i + 1) * 128, :])
    for i in range(8):
        p.dma("sync", bao[:, i:i + 1], BAO[i])
    for b in range(B):
        p.dma("sync", qh[:], QH[b])
        p.dma("sync", kb[:], KB[b])
        p.dma("sync", va[:], VA[b])
        p.op("scalar", lambda s: s.activation(sqb[:], kb[:], AF.Square))
        for c in range(4):
            p.op("tensor", lambda t, c=c: t.matmul(PN[:, c * 512:(c + 1) * 512], onesb64[:], sqb[:, c * 512:(c + 1) * 512], start=True, stop=True))
        p.op("vector", lambda v: v.tensor_copy(sums[:], PN[:]))
        p.op("vector", lambda v: v.tensor_reduce(kmax2[:], sums[:], AX.X, ALU.max))
        p.op("vector", lambda v: v.memset(qmax2[:], 0.0))
        for blk in range(16):
            p.op("scalar", lambda s, blk=blk: s.activation(sqb[:], qh[:, blk * 2048:(blk + 1) * 2048], AF.Square))
            for c in range(4):
                p.op("tensor", lambda t, c=c: t.matmul(
                    PN[:, c * 512:(c + 1) * 512], onesb64[:],
                    sqb[:, c * 512:(c + 1) * 512], start=True, stop=True))
            p.op("vector", lambda v: v.tensor_copy(sums[:], PN[:]))
            p.op("vector", lambda v: v.tensor_reduce(red[:], sums[:], AX.X, ALU.max))
            p.op("vector", lambda v: v.tensor_tensor(qmax2[:], qmax2[:], red[:], ALU.max))
        p.op("vector", lambda v: v.tensor_tensor(bmag[:], qmax2[:], kmax2[:], ALU.mult))
        p.op("scalar", lambda s: s.activation(bmag[:], bmag[:], AF.Sqrt))
        p.op("vector", lambda v: v.tensor_scalar(negb[:], bmag[:], -0.1300, None, ALU.mult))
        p.op("tensor", lambda t: t.matmul(PS[:, 0:1], onesf[:], negb[:], start=True, stop=True))
        p.op("vector", lambda v: v.tensor_copy(negbc[:], PS[:, 0:1]))
        for h in range(NHEAD):
            for qc in range(4):
                q512 = qh[:, h * T + qc * 512: h * T + (qc + 1) * 512]
                nkt = 4 * qc + 4
                for kt in range(nkt):
                    p.op("tensor", lambda t, kt=kt, q512=q512: t.matmul(
                        PS[:], kb[:, kt * 128:(kt + 1) * 128], q512, start=True, stop=True))
                    p.op("scalar", lambda s: s.activation(et[:], PS[:], AF.Exp, bias=negbc[:]))
                    if kt >= 4 * qc:
                        rr = kt - 4 * qc
                        p.op("vector", lambda v, rr=rr: v.tensor_tensor(
                            et2[:], et[:], maskc[:, rr * 512:(rr + 1) * 512], ALU.mult))
                        src = et2
                    else:
                        src = et
                    p.op("tensor", lambda t, kt=kt, src=src, nkt=nkt: t.matmul(
                        PO[:], va[:, kt * 65:kt * 65 + 65], src[:],
                        start=(kt == 0), stop=(kt == nkt - 1)))
                p.op("vector", lambda v: v.reciprocal(rc[:], PO[64:65, :]))
                p.op("tensor", lambda t: t.matmul(PB[:], onesf[:, 0:64], rc[:], start=True, stop=True))
                p.op("vector", lambda v: v.tensor_copy(rcb[:], PB[:]))
                p.op("vector", lambda v: v.tensor_tensor(ohx[:], PO[0:64, :], rcb[:], ALU.mult))
                dc, half = h // 2, h % 2
                p.op("vector", lambda v, dc=dc, half=half, qc=qc: v.tensor_copy(
                    ofm[half * 64:half * 64 + 64, dc * T + qc * 512:dc * T + (qc + 1) * 512], ohx[:]))
        for oc in range(8):
            for tc4 in range(4):
                for dc in range(8):
                    p.op("tensor", lambda t, oc=oc, tc4=tc4, dc=dc: t.matmul(
                        PS[:], wao[dc][:, oc * 128:(oc + 1) * 128],
                        ofm[:, dc * T + tc4 * 512:dc * T + (tc4 + 1) * 512],
                        start=(dc == 0), stop=(dc == 7)))
                gcols = slice(b * T + tc4 * 512, b * T + (tc4 + 1) * 512)
                p.op("vector", lambda v, oc=oc: v.tensor_scalar(
                    xattn[:], PS[:], bao[:, oc:oc + 1], None, ALU.add))
                p.dma("sync", xnt[:], XNF[oc * 128:(oc + 1) * 128, gcols])
                p.op("vector", lambda v: v.tensor_tensor(x1t[:], xattn[:], xnt[:], ALU.add))
                p.op("vector", lambda v: v.tensor_copy(x1b[:], x1t[:]))
                p.dma("sync", X1F[oc * 128:(oc + 1) * 128, gcols], x1t[:])
                p.dma("sync", X1B[oc * 128:(oc + 1) * 128, gcols], x1b[:])
    es.close()
def emit_s4(nc, p, DR):
    X1B = DR["X1B"]
    WIN = DR["WIN"]
    CW = DR["CW"]
    CB = DR["CB"]
    WXP = DR["WXP"]
    WDT = DR["WDT"]
    BDT = DR["BDT"]
    UB = DR["UB"]
    UF = DR["UF"]
    SRES = DR["SRES"]
    DLT = DR["DLT"]
    BM = DR["BM"]
    CM = DR["CM"]
    U0D = DR["U0D"]
    XDB = DR["XDB"]

    es = contextlib.ExitStack()
    sb = lambda n, s, d: es.enter_context(nc.sbuf_tensor('s4_' + n, s, d))
    ps = lambda n, s, d: es.enter_context(nc.psum_tensor('s4_' + n, s, d))
    win = [sb(f"win{i}", [128, 2 * DI], BF16) for i in range(8)]
    x1b = [sb(f"x1b{i}", [128, 512], BF16) for i in range(8)]
    ev = sb("ev", [128, 512], F32)
    u0p = sb("u0p", [128, T + 3], F32)
    acc = sb("acc", [128, T], F32)
    acc2 = sb("acc2", [128, T], F32)
    cw = sb("cw", [128, 4], F32)
    cb = sb("cb", [128, 1], F32)
    uf = sb("uf", [128, T], F32)
    ub = sb("ub", [128, T], BF16)
    wxp = [sb(f"wxp{i}", [128, 96], BF16) for i in range(16)]
    ubt = [sb(f"ubt{i}", [128, 512], BF16) for i in range(16)]
    xdbt = sb("xdbt", [64, 512], BF16)
    bct = sb("bct", [32, 512], F32)
    wdt = sb("wdt", [64, DI], BF16)
    bdt = sb("bdt", [128, 16], F32)
    ta = sb("ta", [128, 512], F32)
    tl = sb("tl", [128, 512], F32)
    tr = sb("tr", [128, 512], F32)
    dl = sb("dl", [128, 512], F32)
    P1 = ps("P1", [128, 512], F32)
    P96 = ps("P96", [96, 512], F32)

    for i in range(8):
        p.dma("gpsimd", win[i][:], WIN[i * 128:(i + 1) * 128, :])
    for tc in range(8):
        csl = slice(tc * 512, (tc + 1) * 512)
        for i in range(8):
            p.dma("sync", x1b[i][:], X1B[i * 128:(i + 1) * 128, csl])
        for oc in range(32):
            for kc in range(8):
                p.op("tensor", lambda t, oc=oc, kc=kc: t.matmul(
                    P1[:], win[kc][:, oc * 128:(oc + 1) * 128], x1b[kc][:],
                    start=(kc == 0), stop=(kc == 7)))
            if oc < 16:
                p.op("vector", lambda v: v.tensor_copy(ev[:], P1[:]))
                p.dma("sync", U0D[oc * 128:(oc + 1) * 128, csl], ev[:])
            else:
                p.op("scalar", lambda s: s.activation(ev[:], P1[:], AF.Silu))
                p.dma("sync", SRES[(oc - 16) * 128:(oc - 15) * 128, csl], ev[:])
    for ct in range(16):
        p.dma("sync", cw[:], CW[ct])
        p.dma("sync", cb[:], CB[ct])
        for b in range(B):
            rs = slice(ct * 128, (ct + 1) * 128)
            cs = slice(b * T, (b + 1) * T)
            p.op("vector", lambda v: v.memset(u0p[:, 0:3], 0.0))
            p.dma("sync", u0p[:, 3:], U0D[rs, cs])
            p.op("vector", lambda v: v.tensor_scalar(acc[:], u0p[:, 3:3 + T], cw[:, 3:4], None, ALU.mult))
            p.op("vector", lambda v: v.scalar_tensor_tensor(acc2[:], u0p[:, 2:2 + T], cw[:, 2:3], acc[:], ALU.mult, ALU.add))
            p.op("vector", lambda v: v.scalar_tensor_tensor(acc[:], u0p[:, 1:1 + T], cw[:, 1:2], acc2[:], ALU.mult, ALU.add))
            p.op("vector", lambda v: v.scalar_tensor_tensor(acc2[:], u0p[:, 0:T], cw[:, 0:1], acc[:], ALU.mult, ALU.add))
            p.op("scalar", lambda s: s.activation(uf[:], acc2[:], AF.Silu, bias=cb[:]))
            p.op("vector", lambda v: v.tensor_copy(ub[:], uf[:]))
            p.dma("sync", UF[rs, cs], uf[:])
            p.dma("sync", UB[rs, cs], ub[:])
    for i in range(16):
        p.dma("gpsimd", wxp[i][:], WXP[i * 128:(i + 1) * 128, :])
    for tc in range(8):
        csl = slice(tc * 512, (tc + 1) * 512)
        for i in range(16):
            p.dma("sync", ubt[i][:], UB[i * 128:(i + 1) * 128, csl])
        for kc in range(16):
            p.op("tensor", lambda t, kc=kc: t.matmul(
                P96[:], wxp[kc][:], ubt[kc][:], start=(kc == 0), stop=(kc == 15)))
        p.op("vector", lambda v: v.tensor_copy(xdbt[:], P96[0:64, :]))
        p.dma("sync", XDB[:, csl], xdbt[:])
        p.op("vector", lambda v: v.tensor_copy(bct[:], P96[64:96, :]))
        b, tloc = tc // 4, (tc % 4) * 512
        p.dma("sync", BM[b, :, tloc:tloc + 512], bct[0:16, :])
        p.dma("sync", CM[b, :, tloc:tloc + 512], bct[16:32, :])
    p.dma("gpsimd", wdt[:], WDT)
    for i in range(16):
        p.dma("sync", bdt[:, i:i + 1], BDT[i])
    for tc in range(8):
        csl = slice(tc * 512, (tc + 1) * 512)
        p.dma("sync", xdbt[:], XDB[:, csl])
        for oc in range(16):
            p.op("tensor", lambda t, oc=oc: t.matmul(
                P1[:], wdt[:, oc * 128:(oc + 1) * 128], xdbt[:], start=True, stop=True))
            p.op("scalar", lambda s, oc=oc: s.activation(ta[:], P1[:], AF.Abs, bias=bdt[:, oc:oc + 1]))
            p.op("scalar", lambda s: s.activation(tl[:], ta[:], AF.Exp, scale=-1.0))
            p.op("scalar", lambda s: s.activation(tl[:], tl[:], AF.Ln, bias=1.0))
            p.op("scalar", lambda s, oc=oc: s.activation(tr[:], P1[:], AF.Relu, bias=bdt[:, oc:oc + 1]))
            p.op("vector", lambda v: v.tensor_tensor(dl[:], tl[:], tr[:], ALU.add))
            p.dma("sync", DLT[oc * 128:(oc + 1) * 128, csl], dl[:])
    es.close()
def emit_s5(nc, p, DR):
    DLT = DR["DLT"]
    UF = DR["UF"]
    SRES = DR["SRES"]
    BM = DR["BM"]
    CM = DR["CM"]
    ALOG = DR["ALOG"]
    DPV = DR["DPV"]
    WOUT = DR["WOUT"]
    X1F = DR["X1F"]
    ONESF = DR["ONESF"]
    S0F = DR["S0F"]

    SEL16 = DR["SEL16"]
    es = contextlib.ExitStack()
    sb = lambda n, s, d: es.enter_context(nc.sbuf_tensor('s5_' + n, s, d))
    ps = lambda n, s, d: es.enter_context(nc.psum_tensor('s5_' + n, s, d))
    anall = sb("anall", [128, 16 * DS], F32)
    dpv = sb("dpv", [128, 16], F32)
    wo = [sb(f"wo{i}", [128, D], BF16) for i in range(16)]
    sel16 = sb("sel16", [16, 16 * 128], F32)
    bm16 = sb("bm16", [16, T], F32)
    cm16 = sb("cm16", [16, T], F32)
    Bb = sb("Bb", [128, DS * 512], F32)
    Cb = sb("Cb", [128, DS * 512], F32)
    dltc = [sb(f"dltc{i}", [128, 512], F32) for i in range(2)]
    dudc = [sb(f"dudc{i}", [128, 512], F32) for i in range(2)]
    ufc = [sb(f"ufc{i}", [128, 512], F32) for i in range(2)]
    sresc = [sb(f"sresc{i}", [128, 512], F32) for i in range(2)]
    dA = [sb(f"dA{i}", [128, 512], F32) for i in range(2)]
    dbu = [sb(f"dbu{i}", [128, 512], F32) for i in range(2)]
    hbuf = [sb(f"hbuf{i}", [128, 512], F32) for i in range(2)]
    prod = [sb(f"prod{i}", [128, 512], F32) for i in range(2)]
    yc = sb("yc", [128, 512], F32)
    hstates = sb("hstates", [128, 16 * DS], F32)
    ygb = sb("ygb", [128, 16 * 512], BF16)
    x1t = [sb(f"x1t{i}", [128, 512], F32) for i in range(2)]
    s0t = [sb(f"s0t{i}", [128, 512], F32) for i in range(2)]
    PA = [ps(f"PA{i}", [128, 512], F32) for i in range(2)]
    PW = [ps(f"PW{i}", [128, 512], F32) for i in range(2)]

    p.dma("sync", anall[:], ALOG)
    p.op("scalar", lambda s: s.activation(anall[:], anall[:], AF.Exp))
    p.op("vector", lambda v: v.tensor_scalar(anall[:], anall[:], -1.0, None, ALU.mult))
    p.dma("sync", dpv[:], DPV)
    p.dma("sync", sel16[:], SEL16)
    for i in range(16):
        p.dma("gpsimd", wo[i][:], WOUT[i * 128:(i + 1) * 128, :])
    for b in range(B):
        p.dma("sync", bm16[:], BM[b])
        p.dma("sync", cm16[:], CM[b])
        p.op("vector", lambda v: v.memset(hstates[:], 0.0))
        for TC in range(4):
            tsl = slice(b * T + TC * 512, b * T + (TC + 1) * 512)
            lsl = slice(TC * 512, (TC + 1) * 512)
            for n in range(DS):
                r = n % 2
                p.op("tensor", lambda t, n=n, r=r: t.matmul(
                    PA[r][:], sel16[:, n * 128:(n + 1) * 128], bm16[:, lsl], start=True, stop=True))
                p.op("vector", lambda v, n=n, r=r: v.tensor_copy(Bb[:, n * 512:(n + 1) * 512], PA[r][:]))
                p.op("tensor", lambda t, n=n, r=r: t.matmul(
                    PW[r][:], sel16[:, n * 128:(n + 1) * 128], cm16[:, lsl], start=True, stop=True))
                p.op("vector", lambda v, n=n, r=r: v.tensor_copy(Cb[:, n * 512:(n + 1) * 512], PW[r][:]))
            for ct in range(16):
                rc = ct % 2
                rs = slice(ct * 128, (ct + 1) * 128)
                p.dma("sync", dltc[rc][:], DLT[rs, tsl])
                p.dma("sync", ufc[rc][:], UF[rs, tsl])
                p.dma("sync", sresc[rc][:], SRES[rs, tsl])
                p.op("vector", lambda v, rc=rc: v.tensor_tensor(dudc[rc][:], dltc[rc][:], ufc[rc][:], ALU.mult))
                for n in range(DS):
                    r = n % 2
                    nsl = slice(n * 512, (n + 1) * 512)
                    hs = slice(ct * DS + n, ct * DS + n + 1)
                    asl = slice(ct * DS + n, ct * DS + n + 1)
                    p.op("scalar", lambda s, r=r, rc=rc, asl=asl: s.activation(
                        dA[r][:], dltc[rc][:], AF.Exp, scale=anall[:, asl]))
                    p.op("vector", lambda v, r=r, rc=rc, nsl=nsl: v.tensor_tensor(
                        dbu[r][:], dudc[rc][:], Bb[:, nsl], ALU.mult))
                    p.op("vector", lambda v, r=r, hs=hs: v.tensor_tensor_scan(
                        hbuf[r][:], dA[r][:], dbu[r][:], hstates[:, hs], ALU.mult, ALU.add))
                    p.op("vector", lambda v, r=r, hs=hs: v.tensor_copy(hstates[:, hs], hbuf[r][:, 511:512]))
                    if n == 0:
                        p.op("vector", lambda v, r=r, nsl=nsl: v.tensor_tensor(yc[:], hbuf[r][:], Cb[:, nsl], ALU.mult))
                    else:
                        p.op("vector", lambda v, r=r, nsl=nsl: v.tensor_tensor(prod[r][:], hbuf[r][:], Cb[:, nsl], ALU.mult))
                        p.op("vector", lambda v, r=r: v.tensor_tensor(yc[:], yc[:], prod[r][:], ALU.add))
                p.op("vector", lambda v, ct=ct, rc=rc: v.scalar_tensor_tensor(
                    yc[:], ufc[rc][:], dpv[:, ct:ct + 1], yc[:], ALU.mult, ALU.add))
                p.op("vector", lambda v, ct=ct, rc=rc: v.tensor_tensor(
                    ygb[:, ct * 512:(ct + 1) * 512], yc[:], sresc[rc][:], ALU.mult))
            for oc in range(8):
                ro = oc % 2
                for ct in range(16):
                    p.op("tensor", lambda t, oc=oc, ct=ct, ro=ro: t.matmul(
                        PW[ro][:], wo[ct][:, oc * 128:(oc + 1) * 128],
                        ygb[:, ct * 512:(ct + 1) * 512], start=(ct == 0), stop=(ct == 15)))
                p.dma("sync", x1t[ro][:], X1F[oc * 128:(oc + 1) * 128, tsl])
                p.op("vector", lambda v, ro=ro: v.tensor_tensor(s0t[ro][:], PW[ro][:], x1t[ro][:], ALU.add))
                p.dma("sync", S0F[oc * 128:(oc + 1) * 128, tsl], s0t[ro][:])
    es.close()
def emit_s6(nc, p, DR):
    XNB = DR["XNB"]
    WF = DR["WF"]
    FB = DR["FB"]
    VR = DR["VR"]
    CT = DR["CT"]
    S0F = DR["S0F"]
    XF = DR["XF"]
    X2F = DR["X2F"]
    X2B = DR["X2B"]

    es = contextlib.ExitStack()
    sb = lambda n, s, d: es.enter_context(nc.sbuf_tensor('s6_' + n, s, d))
    ps = lambda n, s, d: es.enter_context(nc.psum_tensor('s6_' + n, s, d))
    wf = [sb(f"wf{i}", [128, D], BF16) for i in range(8)]
    vr = [sb(f"vr{i}", [128, 32], BF16) for i in range(8)]
    ct = sb("ct", [32, D], BF16)
    fb = sb("fb", [128, 8], F32)
    xnb = [sb(f"xnb{i}", [128, 512], BF16) for i in range(8)]
    t1b = sb("t1b", [32, 512], BF16)
    gt = sb("gt", [128, 512], F32)
    s0t = sb("s0t", [128, 512], F32)
    xft = sb("xft", [128, 512], F32)
    x2t = sb("x2t", [128, 512], F32)
    x2b = sb("x2b", [128, 512], BF16)
    P1 = ps("P1", [128, 512], F32)
    P32 = ps("P32", [32, 512], F32)

    for i in range(8):
        p.dma("gpsimd", wf[i][:], WF[i * 128:(i + 1) * 128, :])
        p.dma("gpsimd", vr[i][:], VR[i * 128:(i + 1) * 128, :])
        p.dma("sync", fb[:, i:i + 1], FB[i])
    p.dma("gpsimd", ct[:], CT)
    for tc in range(8):
        csl = slice(tc * 512, (tc + 1) * 512)
        for i in range(8):
            p.dma("sync", xnb[i][:], XNB[i * 128:(i + 1) * 128, csl])
        for kc in range(8):
            p.op("tensor", lambda t, kc=kc: t.matmul(
                P32[:], vr[kc][:], xnb[kc][:], start=(kc == 0), stop=(kc == 7)))
        p.op("vector", lambda v: v.tensor_copy(t1b[:], P32[:]))
        for oc in range(8):
            for kc in range(8):
                p.op("tensor", lambda t, oc=oc, kc=kc: t.matmul(
                    P1[:], wf[kc][:, oc * 128:(oc + 1) * 128], xnb[kc][:],
                    start=(kc == 0), stop=False))
            p.op("tensor", lambda t, oc=oc: t.matmul(
                P1[:], ct[:, oc * 128:(oc + 1) * 128], t1b[:], start=False, stop=True))
            p.op("scalar", lambda s, oc=oc: s.activation(gt[:], P1[:], AF.Silu, bias=fb[:, oc:oc + 1]))
            p.dma("sync", s0t[:], S0F[oc * 128:(oc + 1) * 128, csl])
            p.dma("sync", xft[:], XF[oc * 128:(oc + 1) * 128, csl])
            p.op("vector", lambda v: v.tensor_tensor(gt[:], gt[:], s0t[:], ALU.mult))
            p.op("vector", lambda v: v.tensor_tensor(x2t[:], gt[:], xft[:], ALU.add))
            p.op("vector", lambda v: v.tensor_copy(x2b[:], x2t[:]))
            p.dma("sync", X2F[oc * 128:(oc + 1) * 128, csl], x2t[:])
            p.dma("sync", X2B[oc * 128:(oc + 1) * 128, csl], x2b[:])
    es.close()
def emit_s7(nc, p, DR):
    X2F = DR["X2F"]
    identf = DR["identf"]
    XN2B = DR["XN2B"]

    es = contextlib.ExitStack()
    sb = lambda n, s, d: es.enter_context(nc.sbuf_tensor('s7_' + n, s, d))
    ps = lambda n, s, d: es.enter_context(nc.psum_tensor('s7_' + n, s, d))
    idf = sb("idf", [128, 128], F32)
    fmt = sb("fmt", [128, 128], F32)
    xtm = sb("xtm", [128, D], F32)
    xsq = sb("xsq", [128, D], F32)
    xn = sb("xn", [128, D], F32)
    s1 = sb("s1", [128, 1], F32)
    s2 = sb("s2", [128, 1], F32)
    mean = sb("mean", [128, 1], F32)
    ex2 = sb("ex2", [128, 1], F32)
    msq = sb("msq", [128, 1], F32)
    var = sb("var", [128, 1], F32)
    rstd = sb("rstd", [128, 1], F32)
    epsc = sb("epsc", [128, 1], F32)
    mr = sb("mr", [128, 1], F32)
    evb = sb("evb", [128, 128], BF16)
    pT = ps("pT", [128, 128], F32)

    p.op("vector", lambda v: v.memset(epsc[:], 1e-5))
    p.dma("sync", idf[:], identf)
    for tt in range(TT // 128):
        for dc in range(8):
            p.dma("sync", fmt[:], X2F[dc * 128:(dc + 1) * 128, tt * 128:(tt + 1) * 128])
            p.op("tensor", lambda t: t.transpose(pT[:], fmt[:], idf[:]))
            p.op("vector", lambda v, dc=dc: v.tensor_copy(xtm[:, dc * 128:(dc + 1) * 128], pT[:]))
        p.op("vector", lambda v: v.tensor_reduce(s1[:], xtm[:], AX.X, ALU.add))
        p.op("scalar", lambda s: s.activation(xsq[:], xtm[:], AF.Square))
        p.op("vector", lambda v: v.tensor_reduce(s2[:], xsq[:], AX.X, ALU.add))
        p.op("vector", lambda v: v.tensor_scalar(mean[:], s1[:], 1.0 / D, None, ALU.mult))
        p.op("vector", lambda v: v.tensor_scalar(ex2[:], s2[:], 1.0 / D, None, ALU.mult))
        p.op("scalar", lambda s: s.activation(msq[:], mean[:], AF.Square))
        p.op("vector", lambda v: v.tensor_tensor(var[:], ex2[:], msq[:], ALU.subtract))
        p.op("scalar", lambda s: s.activation(var[:], var[:], AF.Ln, bias=epsc[:]))
        p.op("scalar", lambda s: s.activation(rstd[:], var[:], AF.Exp, scale=-0.5))
        p.op("vector", lambda v: v.tensor_tensor(mr[:], mean[:], rstd[:], ALU.mult))
        p.op("vector", lambda v: v.tensor_scalar(xn[:], xtm[:], rstd[:], mr[:], ALU.mult, ALU.subtract))
        for dc in range(8):
            p.op("tensor", lambda t, dc=dc: t.transpose(pT[:], xn[:, dc * 128:(dc + 1) * 128], idf[:]))
            p.op("scalar", lambda s: s.copy(evb[:], pT[:]))
            p.dma("sync", XN2B[dc * 128:(dc + 1) * 128, tt * 128:(tt + 1) * 128], evb[:])
    es.close()
def emit_s8(nc, p, DR):
    XN2B = DR["XN2B"]
    WQKVG = DR["WQKVG"]
    WO1 = DR["WO1"]
    BO = DR["BO"]
    BG1 = DR["BG1"]
    DCP = DR["DCP"]
    IOTAM = DR["IOTAM"]
    TRIU = DR["TRIU"]
    IOTP1 = DR["IOTP1"]
    IOT127 = DR["IOT127"]
    C128 = DR["C128"]
    ONESF = DR["ONESF"]
    identb = DR["identb"]
    S1F = DR["S1F"]

    es = contextlib.ExitStack()
    sb = lambda n, s, d: es.enter_context(nc.sbuf_tensor('s8_' + n, s, d))
    ps = lambda n, s, d: es.enter_context(nc.psum_tensor('s8_' + n, s, d))
    wqg = [sb(f"wqg{i}", [128, 193], BF16) for i in range(8)]
    wo1 = sb("wo1", [64, D], BF16)
    bo = sb("bo", [128, 8], F32)
    bg1 = sb("bg1", [1, 1], F32)
    dcp = sb("dcp", [1, 1], F32)
    iotam = sb("iotam", [128, 128], F32)
    triu = sb("triu", [128, 128], F32)
    iotp1 = sb("iotp1", [128, 1], F32)
    iot127 = sb("iot127", [128, 1], F32)
    c128 = sb("c128", [1, 1], F32)
    onesf = sb("onesf", [1, 128], F32)
    idb = sb("idb", [128, 128], BF16)
    dval = sb("dval", [1, 1], F32)
    ldv = sb("ldv", [1, 1], F32)
    ldcol = sb("ldcol", [128, 1], F32)
    Mb = sb("Mb", [128, 128], BF16)
    Mf = sb("Mf", [128, 128], F32)
    apow = sb("apow", [128, 1], F32)
    bpow = sb("bpow", [128, 1], F32)
    d128 = sb("d128", [1, 1], F32)
    d128c = sb("d128c", [64, 1], F32)
    xnb = [sb(f"xnb{i}", [128, 512], BF16) for i in range(8)]
    qf = sb("qf", [64, T], BF16)
    kf = sb("kf", [64, T], BF16)
    vf = sb("vf", [64, T], BF16)
    gr = sb("gr", [1, T], F32)
    gball = sb("gball", [64, T], F32)
    gkb = sb("gkb", [64, T], BF16)
    gvb = sb("gvb", [64, T], BF16)
    gktm = sb("gktm", [128, 16 * 64], BF16)
    gvtm = sb("gvtm", [128, 16 * 64], BF16)
    Sst = sb("Sst", [64, 64], F32)
    S2b = sb("S2b", [64, 64], F32)
    Sbf = sb("Sbf", [64, 64], BF16)
    o1 = sb("o1", [128, 64], F32)
    amd = sb("amd", [128, 128], BF16)
    otm = sb("otm", [128, 16 * 64], BF16)
    ofmb = sb("ofmb", [64, T], BF16)
    s1t = sb("s1t", [128, 512], F32)
    PA = ps("PA", [128, 512], F32)
    PB65 = ps("PB65", [65, 512], F32)
    PC = ps("PC", [128, 128], F32)
    PTb = ps("PTb", [128, 128], BF16)

    for i in range(8):
        p.dma("gpsimd", wqg[i][:], WQKVG[i * 128:(i + 1) * 128, :])
    p.dma("gpsimd", wo1[:], WO1)
    for i in range(8):
        p.dma("sync", bo[:, i:i + 1], BO[i])
    for src, dst in [(BG1, bg1), (DCP, dcp), (IOTAM, iotam), (TRIU, triu),
                     (IOTP1, iotp1), (IOT127, iot127), (C128, c128), (ONESF, onesf)]:
        p.dma("sync", dst[:], src)
    p.dma("gpsimd", idb[:], identb)
    p.op("scalar", lambda s: s.activation(dval[:], dcp[:], AF.Sigmoid))
    p.op("vector", lambda v: v.tensor_scalar(dval[:], dval[:], 0.099, 0.9, ALU.mult, ALU.add))
    p.op("scalar", lambda s: s.activation(ldv[:], dval[:], AF.Ln))
    p.op("tensor", lambda t: t.matmul(PC[:, 0:1], onesf[:], ldv[:], start=True, stop=True))
    p.op("vector", lambda v: v.tensor_copy(ldcol[:], PC[:, 0:1]))
    p.op("scalar", lambda s: s.activation(Mf[:], iotam[:], AF.Exp, scale=ldcol[:]))
    p.op("vector", lambda v: v.tensor_tensor(Mf[:], Mf[:], triu[:], ALU.mult))
    p.op("vector", lambda v: v.tensor_copy(Mb[:], Mf[:]))
    p.op("scalar", lambda s: s.activation(apow[:], iotp1[:], AF.Exp, scale=ldcol[:]))
    p.op("scalar", lambda s: s.activation(bpow[:], iot127[:], AF.Exp, scale=ldcol[:]))
    p.op("scalar", lambda s: s.activation(d128[:], c128[:], AF.Exp, scale=ldv[:]))
    p.op("tensor", lambda t: t.matmul(PC[0:64, 0:1], onesf[:, 0:64], d128[:], start=True, stop=True))
    p.op("vector", lambda v: v.tensor_copy(d128c[:], PC[0:64, 0:1]))
    for b in range(B):
        for tc in range(4):
            csl = slice(b * T + tc * 512, b * T + (tc + 1) * 512)
            lsl = slice(tc * 512, (tc + 1) * 512)
            for i in range(8):
                p.dma("sync", xnb[i][:], XN2B[i * 128:(i + 1) * 128, csl])
            for kc in range(8):
                p.op("tensor", lambda t, kc=kc: t.matmul(
                    PA[:], wqg[kc][:, 0:128], xnb[kc][:], start=(kc == 0), stop=(kc == 7)))
            for kc in range(8):
                p.op("tensor", lambda t, kc=kc: t.matmul(
                    PB65[:], wqg[kc][:, 128:193], xnb[kc][:], start=(kc == 0), stop=(kc == 7)))
            p.op("vector", lambda v, lsl=lsl: v.tensor_copy(qf[:, lsl], PA[0:64, :]))
            p.op("vector", lambda v, lsl=lsl: v.tensor_copy(kf[:, lsl], PA[64:128, :]))
            p.op("vector", lambda v, lsl=lsl: v.tensor_copy(vf[:, lsl], PB65[0:64, :]))
            p.op("scalar", lambda s, lsl=lsl: s.activation(gr[:, lsl], PB65[64:65, :], AF.Sigmoid, bias=bg1[:]))
        for tc in range(4):
            lsl = slice(tc * 512, (tc + 1) * 512)
            p.op("tensor", lambda t, lsl=lsl: t.matmul(PB65[0:64, :], onesf[:, 0:64], gr[:, lsl], start=True, stop=True))
            p.op("vector", lambda v, lsl=lsl: v.tensor_copy(gball[:, lsl], PB65[0:64, :]))
        p.op("vector", lambda v: v.tensor_tensor(gkb[:], kf[:], gball[:], ALU.mult))
        p.op("vector", lambda v: v.tensor_tensor(gvb[:], vf[:], gball[:], ALU.mult))
        for ch in range(16):
            cs = slice(ch * 128, (ch + 1) * 128)
            os_ = slice(ch * 64, (ch + 1) * 64)
            p.op("tensor", lambda t, cs=cs: t.transpose(PTb[:, 0:64], gkb[:, cs], idb[0:64, 0:64]))
            p.op("vector", lambda v, os_=os_: v.tensor_scalar(gktm[:, os_], PTb[:, 0:64], bpow[:], None, ALU.mult))
            p.op("tensor", lambda t, cs=cs: t.transpose(PTb[:, 0:64], gvb[:, cs], idb[0:64, 0:64]))
            p.op("vector", lambda v, os_=os_: v.tensor_copy(gvtm[:, os_], PTb[:, 0:64]))
        p.op("vector", lambda v: v.memset(Sst[:], 0.0))
        for ch in range(16):
            cs = slice(ch * 128, (ch + 1) * 128)
            os_ = slice(ch * 64, (ch + 1) * 64)
            p.op("vector", lambda v: v.tensor_copy(Sbf[:], Sst[:]))
            p.op("tensor", lambda t, cs=cs: t.matmul(PC[:, 0:64], qf[:, cs], Sbf[:], start=True, stop=True))
            p.op("vector", lambda v: v.tensor_scalar(o1[:], PC[:, 0:64], apow[:], None, ALU.mult))
            p.op("tensor", lambda t, cs=cs: t.matmul(PC[:], gkb[:, cs], qf[:, cs], start=True, stop=True))
            p.op("vector", lambda v: v.tensor_tensor(amd[:], PC[:], Mb[:], ALU.mult))
            p.op("tensor", lambda t, os_=os_: t.matmul(PA[:, 0:64], amd[:], gvtm[:, os_], start=True, stop=True))
            p.op("vector", lambda v, os_=os_: v.tensor_tensor(otm[:, os_], PA[:, 0:64], o1[:], ALU.add))
            p.op("vector", lambda v: v.tensor_scalar(S2b[:], Sst[:], d128c[:], None, ALU.mult))
            p.op("tensor", lambda t, os_=os_: t.matmul(PC[0:64, 0:64], gktm[:, os_], gvtm[:, os_], start=True, stop=True))
            p.op("vector", lambda v: v.tensor_tensor(Sst[:], S2b[:], PC[0:64, 0:64], ALU.add))
        for ch in range(16):
            cs = slice(ch * 128, (ch + 1) * 128)
            os_ = slice(ch * 64, (ch + 1) * 64)
            p.op("tensor", lambda t, os_=os_: t.transpose(PTb[0:64, :], otm[:, os_], idb[:]))
            p.op("vector", lambda v, cs=cs: v.tensor_copy(ofmb[:, cs], PTb[0:64, :]))
        for tc in range(4):
            csl = slice(b * T + tc * 512, b * T + (tc + 1) * 512)
            lsl = slice(tc * 512, (tc + 1) * 512)
            for oc in range(8):
                p.op("tensor", lambda t, oc=oc, lsl=lsl: t.matmul(
                    PA[:], wo1[:, oc * 128:(oc + 1) * 128], ofmb[:, lsl], start=True, stop=True))
                p.op("vector", lambda v, oc=oc: v.tensor_scalar(s1t[:], PA[:], bo[:, oc:oc + 1], None, ALU.add))
                p.dma("sync", S1F[oc * 128:(oc + 1) * 128, csl], s1t[:])
    es.close()
def emit_s9(nc, p, DR):
    XN2B = DR["XN2B"]
    X2F = DR["X2F"]
    S1F = DR["S1F"]
    W1T = DR["W1T"]
    W2T = DR["W2T"]
    B1 = DR["B1"]
    B2 = DR["B2"]
    WGT = DR["WGT"]
    BGM = DR["BGM"]
    IOTA8 = DR["IOTA8"]
    IOTA8P = DR["IOTA8P"]
    SEL = DR["SEL"]
    identf = DR["identf"]
    OUT = DR["OUT"]
    YACC = DR["YACC"]

    es = contextlib.ExitStack()
    sb = lambda n, s, d: es.enter_context(nc.sbuf_tensor('s9_' + n, s, d))
    ps = lambda n, s, d: es.enter_context(nc.psum_tensor('s9_' + n, s, d))
    wgt = [sb(f"wgt{i}", [128, 8], BF16) for i in range(8)]
    bgm = sb("bgm", [8, 1], F32)
    iota8 = sb("iota8", [128, 8], F32)
    iota8p = sb("iota8p", [128, 8], F32)
    sel = sb("sel", [8, 8 * 128], F32)
    idf = sb("idf", [128, 128], F32)
    xnball = [sb(f"xnball{i}", [128, 8 * 512], BF16) for i in range(2)]
    lgf = sb("lgf", [8, 512], F32)
    lgt = sb("lgt", [128, 8], F32)
    m1 = sb("m1", [128, 1], F32)
    d1 = sb("d1", [128, 8], F32)
    ind1p = sb("ind1p", [128, 8], F32)
    penal = sb("penal", [128, 8], F32)
    idx = sb("idx", [128, 1], F32)
    ind1 = sb("ind1", [128, 8], F32)
    lg2 = sb("lg2", [128, 8], F32)
    m2 = sb("m2", [128, 1], F32)
    ind2 = sb("ind2", [128, 8], F32)
    wdiff = sb("wdiff", [128, 1], F32)
    w1c = sb("w1c", [128, 1], F32)
    w2c = sb("w2c", [128, 1], F32)
    comb = sb("comb", [128, 8], F32)
    combfm = sb("combfm", [8, TT], F32)
    w1tb = [[sb(f"w1tb{r}_{i}", [128, HID], BF16) for i in range(8)] for r in range(2)]
    w2tb = [sb(f"w2tb{i}", [128, D], BF16) for i in range(16)]
    b1all = sb("b1all", [128, NE * 16], F32)
    b2all = sb("b2all", [128, NE * 8], F32)
    hba = [sb(f"hba{r}", [128, 16 * 512], BF16) for r in range(2)]
    cbet = [sb(f"cbet{r}", [128, 512], F32) for r in range(2)]
    yt = [sb(f"yt{r}", [128, 512], F32) for r in range(2)]
    ytw = [sb(f"ytw{r}", [128, 512], F32) for r in range(2)]
    ya = [sb(f"ya{r}", [128, 512], F32) for r in range(2)]
    g1t = [sb(f"g1t{r}", [128, 512], F32) for r in range(2)]
    s1t = [sb(f"s1t{r}", [128, 512], F32) for r in range(2)]
    x2t = [sb(f"x2t{r}", [128, 512], F32) for r in range(2)]
    x3t = [sb(f"x3t{r}", [128, 512], F32) for r in range(2)]
    ot = [sb(f"ot{r}", [128, 512], F32) for r in range(2)]
    P1 = [ps(f"P1{r}", [128, 512], F32) for r in range(2)]
    P8 = ps("P8", [8, 512], F32)
    Pcb = [ps(f"Pcb{r}", [128, 512], F32) for r in range(2)]
    PT = [ps(f"PT{r}", [128, 128], F32) for r in range(2)]

    def xn_load(r, csl):
        p.dma("sync", xnball[r][:].rearrange("p (c n) -> p c n", c=8),
              XN2B[:, csl].rearrange("(c p) n -> p c n", p=128))

    for i in range(8):
        p.dma("gpsimd", wgt[i][:], WGT[i * 128:(i + 1) * 128, :])
    for src, dst in [(BGM, bgm), (IOTA8, iota8), (IOTA8P, iota8p), (SEL, sel),
                     (identf, idf), (B1, b1all), (B2, b2all)]:
        p.dma("sync", dst[:], src)
    for tc in range(8):
        rt = tc % 2
        csl = slice(tc * 512, (tc + 1) * 512)
        xn_load(rt, csl)
        for kc in range(8):
            p.op("tensor", lambda t, kc=kc, rt=rt: t.matmul(
                P8[:], wgt[kc][:], xnball[rt][:, kc * 512:(kc + 1) * 512],
                start=(kc == 0), stop=(kc == 7)))
        p.op("vector", lambda v: v.tensor_scalar(lgf[:], P8[:], bgm[:], None, ALU.add))
        for q in range(4):
            tt = tc * 4 + q
            rq = q % 2
            p.op("tensor", lambda t, q=q, rq=rq: t.transpose(PT[rq][:, 0:8], lgf[:, q * 128:(q + 1) * 128], idf[0:8, 0:8]))
            p.op("vector", lambda v, rq=rq: v.tensor_copy(lgt[:], PT[rq][:, 0:8]))
            p.op("vector", lambda v: v.tensor_reduce(m1[:], lgt[:], AX.X, ALU.max))
            p.op("vector", lambda v: v.tensor_scalar(d1[:], lgt[:], m1[:], None, ALU.subtract))
            p.op("vector", lambda v: v.tensor_scalar(ind1p[:], d1[:], 0.0, None, ALU.is_ge))
            p.op("vector", lambda v: v.scalar_tensor_tensor(penal[:], ind1p[:], -1000.0, iota8p[:], ALU.mult, ALU.add))
            p.op("vector", lambda v: v.tensor_reduce(idx[:], penal[:], AX.X, ALU.min))
            p.op("vector", lambda v: v.tensor_scalar(ind1[:], iota8[:], idx[:], None, ALU.subtract))
            p.op("scalar", lambda s: s.activation(ind1[:], ind1[:], AF.Abs))
            p.op("scalar", lambda s: s.activation(ind1[:], ind1[:], AF.Relu, scale=-1.0, bias=1.0))
            p.op("vector", lambda v: v.scalar_tensor_tensor(lg2[:], ind1[:], -1e30, lgt[:], ALU.mult, ALU.add))
            p.op("vector", lambda v: v.tensor_reduce(m2[:], lg2[:], AX.X, ALU.max))
            p.op("vector", lambda v: v.tensor_scalar(d1[:], lg2[:], m2[:], None, ALU.subtract))
            p.op("vector", lambda v: v.tensor_scalar(ind1p[:], d1[:], 0.0, None, ALU.is_ge))
            p.op("vector", lambda v: v.scalar_tensor_tensor(penal[:], ind1p[:], -1000.0, iota8p[:], ALU.mult, ALU.add))
            p.op("vector", lambda v: v.tensor_reduce(idx[:], penal[:], AX.X, ALU.min))
            p.op("vector", lambda v: v.tensor_scalar(ind2[:], iota8[:], idx[:], None, ALU.subtract))
            p.op("scalar", lambda s: s.activation(ind2[:], ind2[:], AF.Abs))
            p.op("scalar", lambda s: s.activation(ind2[:], ind2[:], AF.Relu, scale=-1.0, bias=1.0))
            p.op("vector", lambda v: v.tensor_tensor(wdiff[:], m1[:], m2[:], ALU.subtract))
            p.op("scalar", lambda s: s.activation(w1c[:], wdiff[:], AF.Sigmoid))
            p.op("vector", lambda v: v.tensor_scalar(w2c[:], w1c[:], -1.0, 1.0, ALU.mult, ALU.add))
            p.op("vector", lambda v: v.tensor_scalar(comb[:], ind1[:], w1c[:], None, ALU.mult))
            p.op("vector", lambda v: v.scalar_tensor_tensor(comb[:], ind2[:], w2c[:], comb[:], ALU.mult, ALU.add))
            p.op("tensor", lambda t, rq=rq: t.transpose(PT[rq][0:8, :], comb[:], idf[:]))
            p.op("vector", lambda v, tt=tt, rq=rq: v.tensor_copy(combfm[:, tt * 128:(tt + 1) * 128], PT[rq][0:8, :]))
    for e in range(NE):
        we = e % 2
        for i in range(8):
            p.dma("gpsimd", w1tb[we][i][:], W1T[e, i * 128:(i + 1) * 128, :])
        for i in range(16):
            p.dma("gpsimd", w2tb[i][:], W2T[e, i * 128:(i + 1) * 128, :])
        for tc in range(8):
            it = e * 8 + tc
            rt = it % 2
            csl = slice(tc * 512, (tc + 1) * 512)
            xn_load(rt, csl)
            p.op("tensor", lambda t, e=e, csl=csl, rt=rt: t.matmul(
                Pcb[rt][:], sel[:, e * 128:(e + 1) * 128], combfm[:, csl], start=True, stop=True))
            p.op("vector", lambda v, rt=rt: v.tensor_copy(cbet[rt][:], Pcb[rt][:]))
            for hc in range(16):
                rh = hc % 2
                for kc in range(8):
                    p.op("tensor", lambda t, hc=hc, kc=kc, rh=rh, rt=rt, we=we: t.matmul(
                        P1[rh][:], w1tb[we][kc][:, hc * 128:(hc + 1) * 128],
                        xnball[rt][:, kc * 512:(kc + 1) * 512],
                        start=(kc == 0), stop=(kc == 7)))
                p.op("scalar", lambda s, hc=hc, rh=rh, rt=rt, e=e: s.activation(
                    hba[rt][:, hc * 512:(hc + 1) * 512], P1[rh][:], AF.Relu,
                    bias=b1all[:, e * 16 + hc:e * 16 + hc + 1]))
            for oc in range(8):
                ro = oc % 2
                for hc in range(16):
                    p.op("tensor", lambda t, oc=oc, hc=hc, ro=ro, rt=rt: t.matmul(
                        P1[ro][:], w2tb[hc][:, oc * 128:(oc + 1) * 128],
                        hba[rt][:, hc * 512:(hc + 1) * 512],
                        start=(hc == 0), stop=(hc == 15)))
                p.op("scalar", lambda s, oc=oc, ro=ro, e=e: s.activation(
                    yt[ro][:], P1[ro][:], AF.Identity,
                    bias=b2all[:, e * 8 + oc:e * 8 + oc + 1]))
                p.op("vector", lambda v, ro=ro, rt=rt: v.tensor_tensor(ytw[ro][:], yt[ro][:], cbet[rt][:], ALU.mult))
                if e == 0:
                    p.dma("gpsimd", YACC[oc * 128:(oc + 1) * 128, csl], ytw[ro][:])
                else:
                    p.dma("gpsimd", YACC[oc * 128:(oc + 1) * 128, csl], ytw[ro][:], accum_op=ALU.add)
    for tc in range(8):
        csl = slice(tc * 512, (tc + 1) * 512)
        for oc in range(8):
            ro = (tc * 8 + oc) % 2
            p.dma("sync", ya[ro][:], YACC[oc * 128:(oc + 1) * 128, csl])
            p.op("scalar", lambda s, ro=ro: s.activation(g1t[ro][:], ya[ro][:], AF.Silu))
            p.dma("sync", s1t[ro][:], S1F[oc * 128:(oc + 1) * 128, csl])
            p.dma("sync", x2t[ro][:], X2F[oc * 128:(oc + 1) * 128, csl])
            p.op("vector", lambda v, ro=ro: v.tensor_tensor(g1t[ro][:], g1t[ro][:], s1t[ro][:], ALU.mult))
            p.op("vector", lambda v, ro=ro: v.tensor_tensor(x3t[ro][:], g1t[ro][:], x2t[ro][:], ALU.add))
            for q in range(4):
                p.op("tensor", lambda t, q=q, ro=ro: t.transpose(PT[ro][:], x3t[ro][:, q * 128:(q + 1) * 128], idf[:]))
                p.op("scalar", lambda s, q=q, ro=ro: s.copy(ot[ro][:, q * 128:(q + 1) * 128], PT[ro][:]))
            p.dma("sync",
                  OUT[tc * 512:(tc + 1) * 512, oc * 128:(oc + 1) * 128].rearrange(
                      "(q p) n -> p q n", p=128),
                  ot[ro][:].rearrange("p (q n) -> p q n", q=4))
    es.close()




def declare_drams(nc, decls):
    D = {}
    for name, kind in decls:
        shape, dtype = DRAM_SPECS[name]
        D[name] = nc.dram_tensor(name, shape, dtype, kind=kind).ap()
    return D


def _finish2(nc, p):
    with contextlib.ExitStack() as es:
        sems = _make_sems(nc, p, es)
        with nc.Block() as block:
            p.finalize(block, sems)


EMITS = {}


def _mk_stage_builder(sname):
    def build():
        nc = new_nc()
        D = declare_drams(nc, STAGE_DECLS[sname])
        p = Prog(nc)
        EMITS[sname](nc, p, D)
        _finish2(nc, p)
        return nc
    build.__name__ = "build_" + sname
    return build


FUSED_INPUT_NAMES = None


def fused_decls():
    produced = set()
    consumed = set()
    order = []
    for sname in ["s1", "s2", "s3", "s4", "s5", "s6", "s7", "s8", "s9"]:
        for name, kind in STAGE_DECLS[sname]:
            if name not in produced and name not in consumed and name not in [n for n, _ in order]:
                order.append((name, None))
            if kind == "ExternalOutput":
                produced.add(name)
            elif kind == "ExternalInput":
                consumed.add(name)
    decls = []
    for name, _ in order:
        if name == "OUT":
            decls.append((name, "ExternalOutput"))
        elif name in produced:
            decls.append((name, "Internal"))
        elif name in consumed:
            decls.append((name, "ExternalInput"))
        else:
            decls.append((name, "Internal"))
    return decls


def build_fused():
    nc = new_nc()
    D = declare_drams(nc, fused_decls())
    p = Prog(nc)
    for sname in ["s1", "s2", "s3", "s4", "s5", "s6", "s7", "s8", "s9"]:
        EMITS[sname](nc, p, D)
    _finish2(nc, p)
    return nc


EMITS.update(s1=emit_s1, s2=emit_s2, s3=emit_s3, s4=emit_s4, s5=emit_s5, s6=emit_s6, s7=emit_s7, s8=emit_s8, s9=emit_s9)
build_s1 = _mk_stage_builder('s1')
build_s2 = _mk_stage_builder('s2')
build_s3 = _mk_stage_builder('s3')
build_s4 = _mk_stage_builder('s4')
build_s5 = _mk_stage_builder('s5')
build_s6 = _mk_stage_builder('s6')
build_s7 = _mk_stage_builder('s7')
build_s8 = _mk_stage_builder('s8')
build_s9 = _mk_stage_builder('s9')


def _f32(a):
    return np.ascontiguousarray(a, dtype=np.float32)


def make_consts():
    c = {}
    c["identf"] = np.eye(128, dtype=np.float32)
    c["identb"] = np.eye(128, dtype=np.float32).astype(BF)
    m = np.zeros((128, 4 * 512), dtype=np.float32)
    kl = np.arange(128)[:, None]
    ql = np.arange(512)[None, :]
    for rr in range(4):
        m[:, rr * 512:(rr + 1) * 512] = (ql >= kl + 128 * rr)
    c["MASKC"] = m.astype(BF)
    c["ONESF"] = np.ones((1, 128), dtype=np.float32)
    c["IOTAM"] = (np.arange(128)[None, :] - np.arange(128)[:, None]).astype(np.float32)
    c["TRIU"] = (np.arange(128)[None, :] >= np.arange(128)[:, None]).astype(np.float32)
    c["IOTP1"] = (np.arange(128)[:, None] + 1).astype(np.float32)
    c["IOT127"] = (127 - np.arange(128)[:, None]).astype(np.float32)
    c["C128"] = np.full((1, 1), 128.0, dtype=np.float32)
    c["IOTA8"] = np.broadcast_to(np.arange(8, dtype=np.float32), (128, 8)).copy()
    c["IOTA8P"] = c["IOTA8"] + 1000.0
    sel = np.zeros((8, 8 * 128), dtype=np.float32)
    for e in range(8):
        sel[e, e * 128:(e + 1) * 128] = 1.0
    c["SEL"] = sel
    sel16 = np.zeros((16, 16 * 128), dtype=np.float32)
    for n in range(16):
        sel16[n, n * 128:(n + 1) * 128] = 1.0
    c["SEL16"] = sel16
    return c


FUSED = True


def host_inputs(inputs):
    cst = make_consts()
    m = dict(cst)
    m["WQKV"] = _f32(inputs["L0_attn_wqkv"])
    m["BQ"] = _f32(inputs["L0_attn_bqkv"]).reshape(9, 128, 1)
    m["WAO"] = _f32(inputs["L0_attn_w_attn_out"])
    m["BAO"] = _f32(inputs["L0_attn_b_attn_out"]).reshape(8, 128, 1)
    m["WIN"] = _f32(inputs["L0_attn_w_in"])
    m["CW"] = _f32(inputs["L0_attn_conv_w"]).reshape(16, 128, 4)
    m["CB"] = _f32(inputs["L0_attn_conv_b"]).reshape(16, 128, 1)
    m["WXP"] = _f32(inputs["L0_attn_w_xproj"])
    m["WDT"] = _f32(inputs["L0_attn_w_dt"])
    m["BDT"] = _f32(inputs["L0_attn_b_dt"]).reshape(16, 128, 1)
    m["ALOG"] = np.ascontiguousarray(
        _f32(inputs["L0_attn_A_log"]).reshape(16, 128, DS).transpose(1, 0, 2).reshape(128, 16 * DS))
    m["DPV"] = np.ascontiguousarray(_f32(inputs["L0_attn_Dp"]).reshape(16, 128).T)
    m["WOUT"] = _f32(inputs["L0_attn_w_out"])
    m["WF"] = _f32(np.asarray(inputs["L0_ffn_W"]).T)
    m["FB"] = _f32(inputs["L0_ffn_b"]).reshape(8, 128, 1)
    m["VR"] = _f32(inputs["L0_ffn_V_r"])
    m["CT"] = _f32(np.asarray(inputs["L0_ffn_C"]).T)
    m["WQKVG"] = _f32(np.concatenate([inputs["L1_attn_Wq"], inputs["L1_attn_Wk"],
                                      inputs["L1_attn_Wv"], inputs["L1_attn_Wg"]], axis=1))
    m["WO1"] = _f32(inputs["L1_attn_Wo"])
    m["BO"] = _f32(inputs["L1_attn_bo"]).reshape(8, 128, 1)
    m["BG1"] = _f32(inputs["L1_attn_bg"]).reshape(1, 1)
    m["DCP"] = _f32(inputs["L1_attn_decay_param"]).reshape(1, 1)
    m["W1T"] = _f32(np.transpose(np.asarray(inputs["L1_ffn_W1"]), (0, 2, 1)))
    m["W2T"] = _f32(np.transpose(np.asarray(inputs["L1_ffn_W2"]), (0, 2, 1)))
    m["B1"] = np.ascontiguousarray(
        _f32(inputs["L1_ffn_b1"]).reshape(NE, 16, 128).transpose(2, 0, 1).reshape(128, NE * 16))
    m["B2"] = np.ascontiguousarray(
        _f32(inputs["L1_ffn_b2"]).reshape(NE, 8, 128).transpose(2, 0, 1).reshape(128, NE * 8))
    m["WGT"] = _f32(np.asarray(inputs["L1_ffn_Wg"]).T)
    m["BGM"] = _f32(inputs["L1_ffn_bg"]).reshape(NE, 1)
    return m


def kernel(**inputs):
    x = _f32(inputs["L0_attn_x"])
    shared_map = host_inputs(inputs)
    in_names = {n for n, k in fused_decls() if k == "ExternalInput"}
    base = {k: v for k, v in shared_map.items() if k in in_names}
    maps = []
    for c in range(NC):
        m = dict(base)
        m["XTM"] = np.ascontiguousarray(x[B * c:B * (c + 1)].reshape(TT, D))
        maps.append(m)
    res = _run(build_fused, maps)
    out = np.empty((16, 2048, 1024), dtype=np.float32)
    for c in range(NC):
        out[B * c:B * (c + 1)] = np.asarray(res[c]["OUT"], dtype=np.float32).reshape(B, T, D)
    return out


# version 09
# speedup vs baseline: 1.6052x; 1.0070x over previous
import contextlib
import os
import numpy as np
import ml_dtypes
import concourse.bass as bass
import concourse.mybir as mybir
from concourse.bass_utils import run_bass_kernel_spmd

F32 = mybir.dt.float32
BF16 = mybir.dt.bfloat16
U8 = mybir.dt.uint8
AF = mybir.ActivationFunctionType
ALU = mybir.AluOpType
AX = mybir.AxisListType
BF = ml_dtypes.bfloat16

NC = 8
B = 2
T = 2048
TT = B * T
D = 1024
DI = 2048
NHEAD = 16
HD = 64
DS = 16
DTR = 64
NE = 8
HID = 2048


def _is_ap(x):
    return hasattr(x, "space") and hasattr(x, "offset") and hasattr(x, "tensor")


class _FakeInstr:
    def then_inc(self, *a, **k):
        return self


class _Recorder:
    def __init__(self):
        self.calls = []

    def __getattr__(self, name):
        calls = self.calls

        def m(*a, **k):
            calls.append((name, a, k))
            return _FakeInstr()
        return m


class Prog:
    ENGINES = ("sync", "vector", "scalar", "tensor", "gpsimd")
    MODE = os.environ.get("KMODE", "dep")
    PAIRS = ()

    def __init__(self, nc):
        self.nc = nc
        self.seq = {e: [] for e in self.ENGINES}
        self.cmp_n = {e: 0 for e in self.ENGINES}
        self.dma_n = {e: 0 for e in self.ENGINES}
        self.last_tok = {}
        self.last_psum = {}
        self.tmap = {}
        self.ivals = []
        self.parent = {}
        self.gstate = {}
        self.ngid = 0

    def _find(self, g):
        while self.parent[g] != g:
            self.parent[g] = self.parent[self.parent[g]]
            g = self.parent[g]
        return g

    def _union(self, a, b):
        ra, rb = self._find(a), self._find(b)
        if ra == rb:
            return ra
        self.parent[rb] = ra
        wa, rda = self.gstate[ra]
        wb, rdb = self.gstate.pop(rb)
        self.gstate[ra] = [wa | wb, rda | rdb]
        return ra

    def _new_gid(self):
        g = self.ngid
        self.ngid += 1
        self.parent[g] = g
        self.gstate[g] = [set(), set()]
        return g

    def _gid(self, ap):
        name = ap.name
        g = self.tmap.get(name)
        if g is not None:
            return self._find(g)
        sp = str(ap.space)
        if "DRAM" in sp:
            g = self._new_gid()
            self.tmap[name] = g
            return g
        t = ap.tensor
        ml = self.nc.lookup_mloc(t)
        shape = list(t.shape)
        per_part = 1
        for s in shape[1:]:
            per_part *= int(s)
        per_part *= mybir.dt.size(t.dtype)
        b0 = int(ml.addr)
        b1 = b0 + per_part
        p0 = int(t.base_partition)
        p1 = p0 + int(shape[0])
        if "PSUM" in sp:
            p0, p1 = 0, 128
            b0 = int(ml.bank) * 2048
            b1 = b0 + ((per_part + 2047) // 2048) * 2048
        g = None
        for (sp2, q0, q1, c0, c1, g2) in self.ivals:
            if sp2 == sp and q0 < p1 and p0 < q1 and c0 < b1 and b0 < c1:
                g = g2 if g is None else self._union(g, g2)
        if g is None:
            g = self._new_gid()
        else:
            g = self._find(g)
        self.ivals.append((sp, p0, p1, b0, b1, g))
        self.tmap[name] = g
        return g

    def _emit(self, eng, kind, fn, reads, writes):
        touches_psum = any("PSUM" in str(ap.space) for ap in reads + writes)
        waits = set()
        rg = [self._find(self._gid(ap)) for ap in reads]
        wg = [self._find(self._gid(ap)) for ap in writes]
        rg = [self._find(g) for g in rg]
        wg = [self._find(g) for g in wg]
        for g in rg:
            waits |= self.gstate[g][0]
        for g in wg:
            st = self.gstate[g]
            waits |= st[0]
            waits |= st[1]
        if kind == "dma":
            self.dma_n[eng] += 1
            tok = (eng, "dma", self.dma_n[eng])
            w = 16 if eng == "gpsimd" else 32
            if tok[2] > w:
                waits.add((eng, "dma", tok[2] - w))
        else:
            self.cmp_n[eng] += 1
            tok = (eng, "cmp", self.cmp_n[eng])
        if self.MODE == "serial" or (self.MODE == "dmaserial" and kind == "dma"):
            waits |= set(self.last_tok.values())
        if self.MODE == "engserial" and kind == "cmp":
            waits |= {t for (k, e), t in self.last_tok.items() if k == "cmp"}
        if self.MODE == "pair" and kind == "cmp":
            for a, b in self.PAIRS:
                other = b if eng == a else (a if eng == b else None)
                if other is not None:
                    t = self.last_tok.get(("cmp", other))
                    if t is not None:
                        waits.add(t)
        if (self.MODE == "psumpair" and kind == "cmp"
                and eng in ("vector", "scalar") and touches_psum):
            other = "scalar" if eng == "vector" else "vector"
            t = self.last_psum.get(other)
            if t is not None:
                waits.add(t)
            self.last_psum[eng] = tok
        if self.MODE == "serial":
            self.last_tok = {("x", "x"): tok}
        else:
            self.last_tok[(kind, eng)] = tok
        waits.discard(tok)
        for g in rg:
            self.gstate[g][1].add(tok)
        for g in wg:
            self.gstate[g][0] = {tok}
            self.gstate[g][1] = set()
        self.seq[eng].append((kind, fn, waits))

    def op(self, eng, fn):
        rec = _Recorder()
        fn(rec)
        reads, writes = [], []
        for (meth, a, k) in rec.calls:
            aps = [x for x in a if _is_ap(x)]
            kaps = {n: x for n, x in k.items() if _is_ap(x)}
            if "out" in kaps:
                out = kaps.pop("out")
            elif aps:
                out = aps.pop(0)
            else:
                out = None
            if out is not None:
                writes.append(out)
            reads.extend(aps)
            reads.extend(kaps.values())
            if meth == "matmul" and out is not None and k.get("start") is False:
                reads.append(out)
        writes = writes + [ap for ap in reads if "PSUM" in str(ap.space)]
        self._emit(eng, "cmp", fn, reads, writes)

    def dma(self, eng, out, in_, **kw):
        def fn(E, out=out, in_=in_, kw=kw):
            return E.dma_start(out=out, in_=in_, **kw)
        reads = [in_]
        if kw.get("accum_op") is not None:
            reads.append(out)
        self._emit(eng, "dma", fn, reads, [out])

    LANES = {"sync": 8, "gpsimd": 4, "scalar": 2, "vector": 2, "tensor": 2}

    def _dma_key(self, eng, idx):
        lane = (idx - 1) % self.LANES[eng]
        use = (idx - 1) // self.LANES[eng] + 1
        return ("dma", eng, lane), 16 * use

    def finalize(self, block, sems):
        finals = {}
        for e in self.ENGINES:
            if self.cmp_n[e]:
                finals[("cmp", e)] = self.cmp_n[e]
            for i in range(self.dma_n[e]):
                key, val = self._dma_key(e, i + 1)
                finals[key] = max(finals.get(key, 0), val)
        for eng in self.ENGINES:
            lst = self.seq[eng]

            def make_section(lst=lst, eng=eng):
                def section(E):
                    hw = {}
                    ndma = 0
                    for (kind, fn, waits) in lst:
                        need = {}
                        for (teng, tkind, tidx) in waits:
                            if (tkind == "cmp" and teng == "tensor"
                                    and eng == "tensor" and kind == "cmp"):
                                continue
                            if tkind == "cmp":
                                key, val = ("cmp", teng), tidx
                            else:
                                key, val = self._dma_key(teng, tidx)
                            if hw.get(key, 0) >= val:
                                continue
                            if need.get(key, 0) < val:
                                need[key] = val
                        for key, val in need.items():
                            E.wait_ge(sems[key], val)
                            hw[key] = val
                        r = fn(E)
                        if kind == "dma":
                            ndma += 1
                            key, _ = self._dma_key(eng, ndma)
                            r.then_inc(sems[key], 16)
                        else:
                            r.then_inc(sems[("cmp", eng)], 1)
                    for key, tot in finals.items():
                        if hw.get(key, 0) < tot:
                            E.wait_ge(sems[key], tot)
                return section
            getattr(block, eng)(make_section())


def new_nc():
    return bass.Bass("TRN2", target_bir_lowering=False)


def _run(build_fn, per_core_maps, cache={}):
    key = build_fn.__name__
    if key not in cache:
        cache[key] = build_fn()
    nc = cache[key]
    res = run_bass_kernel_spmd(nc, per_core_maps, core_ids=list(range(len(per_core_maps))))
    return res.results


def _make_sems(nc, p, es):
    sems = {}
    for eng in Prog.ENGINES:
        if p.cmp_n[eng]:
            sems[("cmp", eng)] = es.enter_context(nc.semaphore("c_" + eng))
        nlanes = min(p.dma_n[eng], Prog.LANES[eng])
        for lane in range(nlanes):
            sems[("dma", eng, lane)] = es.enter_context(
                nc.semaphore(f"d_{eng}{lane}"))
    return sems


DRAM_SPECS = {
    "XTM": ([TT, D], F32),
    "identf": ([128, 128], F32),
    "XF": ([D, TT], F32),
    "XNF": ([D, TT], F32),
    "XNB": ([D, TT], BF16),
    "WQKV": ([D, 1152], F32),
    "BQ": ([9, 128, 1], F32),
    "identb": ([128, 128], BF16),
    "QH": ([B, 64, NHEAD * T], BF16),
    "KB": ([B, 64, T], BF16),
    "VA": ([B, 128, 16 * 65], BF16),
    "WAO": ([D, D], F32),
    "BAO": ([8, 128, 1], F32),
    "MASKC": ([128, 4 * 512], BF16),
    "ONESF": ([1, 128], F32),
    "X1F": ([D, TT], F32),
    "X1B": ([D, TT], BF16),
    "WIN": ([D, 2 * DI], F32),
    "CW": ([16, 128, 4], F32),
    "CB": ([16, 128, 1], F32),
    "WXP": ([DI, 96], F32),
    "WDT": ([DTR, DI], F32),
    "BDT": ([16, 128, 1], F32),
    "UB": ([DI, TT], BF16),
    "UF": ([DI, TT], F32),
    "SRES": ([DI, TT], F32),
    "DLT": ([DI, TT], F32),
    "BM": ([B, DS, T], F32),
    "CM": ([B, DS, T], F32),
    "U0D": ([DI, TT], F32),
    "XDB": ([DTR, TT], BF16),
    "ALOG": ([128, 16 * DS], F32),
    "DPV": ([128, 16], F32),
    "SEL16": ([16, 16 * 128], F32),
    "WOUT": ([DI, D], F32),
    "S0F": ([D, TT], F32),
    "WF": ([D, D], F32),
    "FB": ([8, 128, 1], F32),
    "VR": ([D, 32], F32),
    "CT": ([32, D], F32),
    "X2F": ([D, TT], F32),
    "X2B": ([D, TT], BF16),
    "XN2B": ([D, TT], BF16),
    "WQKVG": ([D, 193], F32),
    "WO1": ([64, D], F32),
    "BO": ([8, 128, 1], F32),
    "BG1": ([1, 1], F32),
    "DCP": ([1, 1], F32),
    "IOTAM": ([128, 128], F32),
    "TRIU": ([128, 128], F32),
    "IOTP1": ([128, 1], F32),
    "IOT127": ([128, 1], F32),
    "C128": ([1, 1], F32),
    "S1F": ([D, TT], F32),
    "W1T": ([NE, D, HID], F32),
    "W2T": ([NE, HID, D], F32),
    "B1": ([128, NE * 16], F32),
    "B2": ([128, NE * 8], F32),
    "WGT": ([D, NE], F32),
    "BGM": ([NE, 1], F32),
    "IOTA8": ([128, 8], F32),
    "IOTA8P": ([128, 8], F32),
    "SEL": ([8, 8 * 128], F32),
    "OUT": ([TT, D], F32),
    "YACC": ([D, TT], F32),
}

STAGE_DECLS = {
    "s1": [('XTM', 'ExternalInput'), ('identf', 'ExternalInput'), ('XF', 'ExternalOutput'), ('XNF', 'ExternalOutput'), ('XNB', 'ExternalOutput')],
    "s2": [('XNB', 'ExternalInput'), ('WQKV', 'ExternalInput'), ('BQ', 'ExternalInput'), ('identb', 'ExternalInput'), ('QH', 'ExternalOutput'), ('KB', 'ExternalOutput'), ('VA', 'ExternalOutput')],
    "s3": [('QH', 'ExternalInput'), ('KB', 'ExternalInput'), ('VA', 'ExternalInput'), ('WAO', 'ExternalInput'), ('BAO', 'ExternalInput'), ('XNF', 'ExternalInput'), ('MASKC', 'ExternalInput'), ('ONESF', 'ExternalInput'), ('X1F', 'ExternalOutput'), ('X1B', 'ExternalOutput')],
    "s4": [('X1B', 'ExternalInput'), ('WIN', 'ExternalInput'), ('CW', 'ExternalInput'), ('CB', 'ExternalInput'), ('WXP', 'ExternalInput'), ('WDT', 'ExternalInput'), ('BDT', 'ExternalInput'), ('UB', 'ExternalOutput'), ('UF', 'ExternalOutput'), ('SRES', 'ExternalOutput'), ('DLT', 'ExternalOutput'), ('BM', 'ExternalOutput'), ('CM', 'ExternalOutput'), ('U0D', 'Internal'), ('XDB', 'Internal')],
    "s5": [('DLT', 'ExternalInput'), ('UF', 'ExternalInput'), ('SRES', 'ExternalInput'), ('BM', 'ExternalInput'), ('CM', 'ExternalInput'), ('ALOG', 'ExternalInput'), ('DPV', 'ExternalInput'), ('WOUT', 'ExternalInput'), ('X1F', 'ExternalInput'), ('SEL16', 'ExternalInput'), ('S0F', 'ExternalOutput')],
    "s6": [('XNB', 'ExternalInput'), ('WF', 'ExternalInput'), ('FB', 'ExternalInput'), ('VR', 'ExternalInput'), ('CT', 'ExternalInput'), ('S0F', 'ExternalInput'), ('XF', 'ExternalInput'), ('X2F', 'ExternalOutput'), ('X2B', 'ExternalOutput')],
    "s7": [('X2F', 'ExternalInput'), ('identf', 'ExternalInput'), ('XN2B', 'ExternalOutput')],
    "s8": [('XN2B', 'ExternalInput'), ('WQKVG', 'ExternalInput'), ('WO1', 'ExternalInput'), ('BO', 'ExternalInput'), ('BG1', 'ExternalInput'), ('DCP', 'ExternalInput'), ('IOTAM', 'ExternalInput'), ('TRIU', 'ExternalInput'), ('IOTP1', 'ExternalInput'), ('IOT127', 'ExternalInput'), ('C128', 'ExternalInput'), ('ONESF', 'ExternalInput'), ('identb', 'ExternalInput'), ('S1F', 'ExternalOutput')],
    "s9": [('XN2B', 'ExternalInput'), ('X2F', 'ExternalInput'), ('S1F', 'ExternalInput'), ('W1T', 'ExternalInput'), ('W2T', 'ExternalInput'), ('B1', 'ExternalInput'), ('B2', 'ExternalInput'), ('WGT', 'ExternalInput'), ('BGM', 'ExternalInput'), ('IOTA8', 'ExternalInput'), ('IOTA8P', 'ExternalInput'), ('SEL', 'ExternalInput'), ('identf', 'ExternalInput'), ('OUT', 'ExternalOutput'), ('YACC', 'Internal')],
}

def emit_s1(nc, p, DR):
    XTM = DR["XTM"]
    identf = DR["identf"]
    XF = DR["XF"]
    XNF = DR["XNF"]
    XNB = DR["XNB"]

    es = contextlib.ExitStack()
    sb = lambda n, s, d: es.enter_context(nc.sbuf_tensor('s1_' + n, s, d))
    ps = lambda n, s, d: es.enter_context(nc.psum_tensor('s1_' + n, s, d))
    xt = sb("xt", [128, D], F32)
    xsq = sb("xsq", [128, D], F32)
    xn = sb("xn", [128, D], F32)
    idf = sb("idf", [128, 128], F32)
    s1 = sb("s1", [128, 1], F32)
    s2 = sb("s2", [128, 1], F32)
    mean = sb("mean", [128, 1], F32)
    ex2 = sb("ex2", [128, 1], F32)
    msq = sb("msq", [128, 1], F32)
    var = sb("var", [128, 1], F32)
    rstd = sb("rstd", [128, 1], F32)
    epsc = sb("epsc", [128, 1], F32)
    mr = sb("mr", [128, 1], F32)
    ev = sb("ev", [128, 128], F32)
    evb = sb("evb", [128, 128], BF16)
    pT = ps("pT", [128, 128], F32)

    p.op("vector", lambda v: v.memset(epsc[:], 1e-5))
    p.dma("sync", idf[:], identf)
    for tt in range(TT // 128):
        p.dma("sync", xt[:], XTM[tt * 128:(tt + 1) * 128, :])
        p.op("vector", lambda v: v.tensor_reduce(s1[:], xt[:], AX.X, ALU.add))
        p.op("scalar", lambda s: s.activation(xsq[:], xt[:], AF.Square))
        p.op("vector", lambda v: v.tensor_reduce(s2[:], xsq[:], AX.X, ALU.add))
        p.op("vector", lambda v: v.tensor_scalar(mean[:], s1[:], 1.0 / D, None, ALU.mult))
        p.op("vector", lambda v: v.tensor_scalar(ex2[:], s2[:], 1.0 / D, None, ALU.mult))
        p.op("scalar", lambda s: s.activation(msq[:], mean[:], AF.Square))
        p.op("vector", lambda v: v.tensor_tensor(var[:], ex2[:], msq[:], ALU.subtract))
        p.op("scalar", lambda s: s.activation(var[:], var[:], AF.Ln, bias=epsc[:]))
        p.op("scalar", lambda s: s.activation(rstd[:], var[:], AF.Exp, scale=-0.5))
        p.op("vector", lambda v: v.tensor_tensor(mr[:], mean[:], rstd[:], ALU.mult))
        p.op("vector", lambda v: v.tensor_scalar(xn[:], xt[:], rstd[:], mr[:], ALU.mult, ALU.subtract))
        for dc in range(D // 128):
            csl = slice(dc * 128, (dc + 1) * 128)
            p.op("tensor", lambda t, csl=csl: t.transpose(pT[:], xn[:, csl], idf[:]))
            p.op("vector", lambda v: v.tensor_copy(ev[:], pT[:]))
            p.op("scalar", lambda s: s.copy(evb[:], pT[:]))
            p.dma("sync", XNF[dc * 128:(dc + 1) * 128, tt * 128:(tt + 1) * 128], ev[:])
            p.dma("sync", XNB[dc * 128:(dc + 1) * 128, tt * 128:(tt + 1) * 128], evb[:])
            p.op("tensor", lambda t, csl=csl: t.transpose(pT[:], xt[:, csl], idf[:]))
            p.op("vector", lambda v: v.tensor_copy(ev[:], pT[:]))
            p.dma("sync", XF[dc * 128:(dc + 1) * 128, tt * 128:(tt + 1) * 128], ev[:])
    es.close()
def emit_s2(nc, p, DR):
    XNB = DR["XNB"]
    WQKV = DR["WQKV"]
    BQ = DR["BQ"]
    identb = DR["identb"]
    QH = DR["QH"]
    KB = DR["KB"]
    VA = DR["VA"]

    es = contextlib.ExitStack()
    sb = lambda n, s, d: es.enter_context(nc.sbuf_tensor('s2_' + n, s, d))
    ps = lambda n, s, d: es.enter_context(nc.psum_tensor('s2_' + n, s, d))
    wq = [sb(f"wq{i}", [128, 1152], BF16) for i in range(8)]
    bq = sb("bq", [128, 9], F32)
    xnb = [sb(f"xnb{i}", [128, 512], BF16) for i in range(8)]
    idb = sb("idb", [128, 128], BF16)
    qtmp = sb("qtmp", [64, 512], BF16)
    ktmp = sb("ktmp", [64, 512], BF16)
    vtmp = sb("vtmp", [64, 512], BF16)
    vat = sb("vat", [128, 64], BF16)
    onec = sb("onec", [128, 1], BF16)
    P1 = ps("P1", [128, 512], F32)
    PT = ps("PT", [128, 128], BF16)

    p.dma("sync", idb[:], identb)
    p.op("vector", lambda v: v.memset(onec[:], 1.0))
    for i in range(8):
        p.dma("gpsimd", wq[i][:], WQKV[i * 128:(i + 1) * 128, :])
    for i in range(9):
        p.dma("sync", bq[:, i:i + 1], BQ[i])
    for b in range(B):
        for kt in range(16):
            p.dma("sync", VA[b, :, kt * 65 + 64:kt * 65 + 65], onec[:], allow_slow_non_contiguous=True)
    for tc in range(8):
        b, tloc = tc // 4, (tc % 4) * 512
        csl = slice(tc * 512, (tc + 1) * 512)
        for i in range(8):
            p.dma("sync", xnb[i][:], XNB[i * 128:(i + 1) * 128, csl])
        for oc in range(9):
            for kc in range(8):
                p.op("tensor", lambda t, oc=oc, kc=kc: t.matmul(
                    P1[:], wq[kc][:, oc * 128:(oc + 1) * 128], xnb[kc][:],
                    start=(kc == 0), stop=(kc == 7)))
            if oc < 8:
                for half in range(2):
                    h = 2 * oc + half
                    hs = slice(half * 64, half * 64 + 64)
                    p.op("vector", lambda v, hs=hs, oc=oc: v.tensor_scalar(
                        qtmp[:], P1[hs, :], bq[hs, oc:oc + 1], 0.125, ALU.add, ALU.mult))
                    p.dma("sync", QH[b, :, h * T + tloc:h * T + tloc + 512], qtmp[:])
            else:
                p.op("vector", lambda v: v.tensor_scalar(
                    ktmp[:], P1[0:64, :], bq[0:64, 8:9], None, ALU.add))
                p.dma("sync", KB[b, :, tloc:tloc + 512], ktmp[:])
                p.op("vector", lambda v: v.tensor_scalar(
                    vtmp[:], P1[64:128, :], bq[64:128, 8:9], None, ALU.add))
                for i in range(4):
                    kt = (tloc // 128) + i
                    p.op("tensor", lambda t, i=i: t.transpose(PT[:, 0:64], vtmp[:, i * 128:(i + 1) * 128], idb[0:64, 0:64]))
                    p.op("vector", lambda v: v.tensor_copy(vat[:], PT[:, 0:64]))
                    p.dma("sync", VA[b, :, kt * 65:kt * 65 + 64], vat[:])
    es.close()
def emit_s3(nc, p, DR):
    QH = DR["QH"]
    KB = DR["KB"]
    VA = DR["VA"]
    WAO = DR["WAO"]
    BAO = DR["BAO"]
    XNF = DR["XNF"]
    MASKC = DR["MASKC"]
    ONESF = DR["ONESF"]
    X1F = DR["X1F"]
    X1B = DR["X1B"]

    es = contextlib.ExitStack()
    sb = lambda n, s, d: es.enter_context(nc.sbuf_tensor('s3_' + n, s, d))
    ps = lambda n, s, d: es.enter_context(nc.psum_tensor('s3_' + n, s, d))
    qh = sb("qh", [64, NHEAD * T], BF16)
    kb = sb("kb", [64, T], BF16)
    va = sb("va", [128, 16 * 65], BF16)
    wao = [sb(f"wao{i}", [128, D], BF16) for i in range(8)]
    bao = sb("bao", [128, 8], F32)
    maskc = sb("maskc", [128, 4 * 512], BF16)
    onesf = sb("onesf", [1, 128], F32)
    onesb64 = sb("onesb64", [64, 1], BF16)
    sqb = sb("sqb", [64, 2048], BF16)
    sums = sb("sums", [1, 2048], F32)
    red = sb("red", [1, 1], F32)
    qmax2 = sb("qmax2", [1, 1], F32)
    kmax2 = sb("kmax2", [1, 1], F32)
    bmag = sb("bmag", [1, 1], F32)
    negb = sb("negb", [1, 1], F32)
    negbc = sb("negbc", [128, 1], F32)
    et = sb("et", [128, 512], BF16)
    et2 = sb("et2", [128, 512], BF16)
    rc = sb("rc", [1, 512], F32)
    rcb = sb("rcb", [64, 512], F32)
    ohx = sb("ohx", [64, 512], BF16)
    ofm = sb("ofm", [128, 8 * T], BF16)
    xattn = sb("xattn", [128, 512], F32)
    xnt = sb("xnt", [128, 512], F32)
    x1t = sb("x1t", [128, 512], F32)
    x1b = sb("x1b", [128, 512], BF16)
    PS = ps("PS", [128, 512], F32)
    PO = ps("PO", [65, 512], F32)
    PB = ps("PB", [64, 512], F32)
    PN = ps("PN", [1, 2048], F32)

    p.dma("sync", maskc[:], MASKC)
    p.dma("sync", onesf[:], ONESF)
    p.op("vector", lambda v: v.memset(onesb64[:], 1.0))
    for i in range(8):
        p.dma("gpsimd", wao[i][:], WAO[i * 128:(i + 1) * 128, :])
    for i in range(8):
        p.dma("sync", bao[:, i:i + 1], BAO[i])
    for b in range(B):
        p.dma("sync", qh[:], QH[b])
        p.dma("sync", kb[:], KB[b])
        p.dma("sync", va[:], VA[b])
        p.op("scalar", lambda s: s.activation(sqb[:], kb[:], AF.Square))
        for c in range(4):
            p.op("tensor", lambda t, c=c: t.matmul(PN[:, c * 512:(c + 1) * 512], onesb64[:], sqb[:, c * 512:(c + 1) * 512], start=True, stop=True))
        p.op("vector", lambda v: v.tensor_copy(sums[:], PN[:]))
        p.op("vector", lambda v: v.tensor_reduce(kmax2[:], sums[:], AX.X, ALU.max))
        p.op("vector", lambda v: v.memset(qmax2[:], 0.0))
        for blk in range(16):
            p.op("scalar", lambda s, blk=blk: s.activation(sqb[:], qh[:, blk * 2048:(blk + 1) * 2048], AF.Square))
            for c in range(4):
                p.op("tensor", lambda t, c=c: t.matmul(
                    PN[:, c * 512:(c + 1) * 512], onesb64[:],
                    sqb[:, c * 512:(c + 1) * 512], start=True, stop=True))
            p.op("vector", lambda v: v.tensor_copy(sums[:], PN[:]))
            p.op("vector", lambda v: v.tensor_reduce(red[:], sums[:], AX.X, ALU.max))
            p.op("vector", lambda v: v.tensor_tensor(qmax2[:], qmax2[:], red[:], ALU.max))
        p.op("vector", lambda v: v.tensor_tensor(bmag[:], qmax2[:], kmax2[:], ALU.mult))
        p.op("scalar", lambda s: s.activation(bmag[:], bmag[:], AF.Sqrt))
        p.op("vector", lambda v: v.tensor_scalar(negb[:], bmag[:], -0.1300, None, ALU.mult))
        p.op("tensor", lambda t: t.matmul(PS[:, 0:1], onesf[:], negb[:], start=True, stop=True))
        p.op("vector", lambda v: v.tensor_copy(negbc[:], PS[:, 0:1]))
        for h in range(NHEAD):
            for qc in range(4):
                q512 = qh[:, h * T + qc * 512: h * T + (qc + 1) * 512]
                nkt = 4 * qc + 4
                for kt in range(nkt):
                    p.op("tensor", lambda t, kt=kt, q512=q512: t.matmul(
                        PS[:], kb[:, kt * 128:(kt + 1) * 128], q512, start=True, stop=True))
                    p.op("scalar", lambda s: s.activation(et[:], PS[:], AF.Exp, bias=negbc[:]))
                    if kt >= 4 * qc:
                        rr = kt - 4 * qc
                        p.op("vector", lambda v, rr=rr: v.tensor_tensor(
                            et2[:], et[:], maskc[:, rr * 512:(rr + 1) * 512], ALU.mult))
                        src = et2
                    else:
                        src = et
                    p.op("tensor", lambda t, kt=kt, src=src, nkt=nkt: t.matmul(
                        PO[:], va[:, kt * 65:kt * 65 + 65], src[:],
                        start=(kt == 0), stop=(kt == nkt - 1)))
                p.op("vector", lambda v: v.reciprocal(rc[:], PO[64:65, :]))
                p.op("tensor", lambda t: t.matmul(PB[:], onesf[:, 0:64], rc[:], start=True, stop=True))
                p.op("vector", lambda v: v.tensor_copy(rcb[:], PB[:]))
                p.op("vector", lambda v: v.tensor_tensor(ohx[:], PO[0:64, :], rcb[:], ALU.mult))
                dc, half = h // 2, h % 2
                p.op("vector", lambda v, dc=dc, half=half, qc=qc: v.tensor_copy(
                    ofm[half * 64:half * 64 + 64, dc * T + qc * 512:dc * T + (qc + 1) * 512], ohx[:]))
        for oc in range(8):
            for tc4 in range(4):
                for dc in range(8):
                    p.op("tensor", lambda t, oc=oc, tc4=tc4, dc=dc: t.matmul(
                        PS[:], wao[dc][:, oc * 128:(oc + 1) * 128],
                        ofm[:, dc * T + tc4 * 512:dc * T + (tc4 + 1) * 512],
                        start=(dc == 0), stop=(dc == 7)))
                gcols = slice(b * T + tc4 * 512, b * T + (tc4 + 1) * 512)
                p.op("vector", lambda v, oc=oc: v.tensor_scalar(
                    xattn[:], PS[:], bao[:, oc:oc + 1], None, ALU.add))
                p.dma("sync", xnt[:], XNF[oc * 128:(oc + 1) * 128, gcols])
                p.op("vector", lambda v: v.tensor_tensor(x1t[:], xattn[:], xnt[:], ALU.add))
                p.op("vector", lambda v: v.tensor_copy(x1b[:], x1t[:]))
                p.dma("sync", X1F[oc * 128:(oc + 1) * 128, gcols], x1t[:])
                p.dma("sync", X1B[oc * 128:(oc + 1) * 128, gcols], x1b[:])
    es.close()
def emit_s4(nc, p, DR):
    X1B = DR["X1B"]
    WIN = DR["WIN"]
    CW = DR["CW"]
    CB = DR["CB"]
    WXP = DR["WXP"]
    WDT = DR["WDT"]
    BDT = DR["BDT"]
    UB = DR["UB"]
    UF = DR["UF"]
    SRES = DR["SRES"]
    DLT = DR["DLT"]
    BM = DR["BM"]
    CM = DR["CM"]
    U0D = DR["U0D"]
    XDB = DR["XDB"]

    es = contextlib.ExitStack()
    sb = lambda n, s, d: es.enter_context(nc.sbuf_tensor('s4_' + n, s, d))
    ps = lambda n, s, d: es.enter_context(nc.psum_tensor('s4_' + n, s, d))
    win = [sb(f"win{i}", [128, 2 * DI], BF16) for i in range(8)]
    x1b = [sb(f"x1b{i}", [128, 512], BF16) for i in range(8)]
    ev = sb("ev", [128, 512], F32)
    u0p = sb("u0p", [128, T + 3], F32)
    acc = sb("acc", [128, T], F32)
    acc2 = sb("acc2", [128, T], F32)
    cw = sb("cw", [128, 4], F32)
    cb = sb("cb", [128, 1], F32)
    uf = sb("uf", [128, T], F32)
    ub = sb("ub", [128, T], BF16)
    wxp = [sb(f"wxp{i}", [128, 96], BF16) for i in range(16)]
    ubt = [sb(f"ubt{i}", [128, 512], BF16) for i in range(16)]
    xdbt = sb("xdbt", [64, 512], BF16)
    bct = sb("bct", [32, 512], F32)
    wdt = sb("wdt", [64, DI], BF16)
    bdt = sb("bdt", [128, 16], F32)
    ta = sb("ta", [128, 512], F32)
    tl = sb("tl", [128, 512], F32)
    tr = sb("tr", [128, 512], F32)
    dl = sb("dl", [128, 512], F32)
    P1 = ps("P1", [128, 512], F32)
    P96 = ps("P96", [96, 512], F32)

    for i in range(8):
        p.dma("gpsimd", win[i][:], WIN[i * 128:(i + 1) * 128, :])
    for tc in range(8):
        csl = slice(tc * 512, (tc + 1) * 512)
        for i in range(8):
            p.dma("sync", x1b[i][:], X1B[i * 128:(i + 1) * 128, csl])
        for oc in range(32):
            for kc in range(8):
                p.op("tensor", lambda t, oc=oc, kc=kc: t.matmul(
                    P1[:], win[kc][:, oc * 128:(oc + 1) * 128], x1b[kc][:],
                    start=(kc == 0), stop=(kc == 7)))
            if oc < 16:
                p.op("vector", lambda v: v.tensor_copy(ev[:], P1[:]))
                p.dma("sync", U0D[oc * 128:(oc + 1) * 128, csl], ev[:])
            else:
                p.op("scalar", lambda s: s.activation(ev[:], P1[:], AF.Silu))
                p.dma("sync", SRES[(oc - 16) * 128:(oc - 15) * 128, csl], ev[:])
    for ct in range(16):
        p.dma("sync", cw[:], CW[ct])
        p.dma("sync", cb[:], CB[ct])
        for b in range(B):
            rs = slice(ct * 128, (ct + 1) * 128)
            cs = slice(b * T, (b + 1) * T)
            p.op("vector", lambda v: v.memset(u0p[:, 0:3], 0.0))
            p.dma("sync", u0p[:, 3:], U0D[rs, cs])
            p.op("vector", lambda v: v.tensor_scalar(acc[:], u0p[:, 3:3 + T], cw[:, 3:4], None, ALU.mult))
            p.op("vector", lambda v: v.scalar_tensor_tensor(acc2[:], u0p[:, 2:2 + T], cw[:, 2:3], acc[:], ALU.mult, ALU.add))
            p.op("vector", lambda v: v.scalar_tensor_tensor(acc[:], u0p[:, 1:1 + T], cw[:, 1:2], acc2[:], ALU.mult, ALU.add))
            p.op("vector", lambda v: v.scalar_tensor_tensor(acc2[:], u0p[:, 0:T], cw[:, 0:1], acc[:], ALU.mult, ALU.add))
            p.op("scalar", lambda s: s.activation(uf[:], acc2[:], AF.Silu, bias=cb[:]))
            p.op("vector", lambda v: v.tensor_copy(ub[:], uf[:]))
            p.dma("sync", UF[rs, cs], uf[:])
            p.dma("sync", UB[rs, cs], ub[:])
    for i in range(16):
        p.dma("gpsimd", wxp[i][:], WXP[i * 128:(i + 1) * 128, :])
    for tc in range(8):
        csl = slice(tc * 512, (tc + 1) * 512)
        for i in range(16):
            p.dma("sync", ubt[i][:], UB[i * 128:(i + 1) * 128, csl])
        for kc in range(16):
            p.op("tensor", lambda t, kc=kc: t.matmul(
                P96[:], wxp[kc][:], ubt[kc][:], start=(kc == 0), stop=(kc == 15)))
        p.op("vector", lambda v: v.tensor_copy(xdbt[:], P96[0:64, :]))
        p.dma("sync", XDB[:, csl], xdbt[:])
        p.op("vector", lambda v: v.tensor_copy(bct[:], P96[64:96, :]))
        b, tloc = tc // 4, (tc % 4) * 512
        p.dma("sync", BM[b, :, tloc:tloc + 512], bct[0:16, :])
        p.dma("sync", CM[b, :, tloc:tloc + 512], bct[16:32, :])
    p.dma("gpsimd", wdt[:], WDT)
    for i in range(16):
        p.dma("sync", bdt[:, i:i + 1], BDT[i])
    for tc in range(8):
        csl = slice(tc * 512, (tc + 1) * 512)
        p.dma("sync", xdbt[:], XDB[:, csl])
        for oc in range(16):
            p.op("tensor", lambda t, oc=oc: t.matmul(
                P1[:], wdt[:, oc * 128:(oc + 1) * 128], xdbt[:], start=True, stop=True))
            p.op("scalar", lambda s, oc=oc: s.activation(ta[:], P1[:], AF.Abs, bias=bdt[:, oc:oc + 1]))
            p.op("scalar", lambda s: s.activation(tl[:], ta[:], AF.Exp, scale=-1.0))
            p.op("scalar", lambda s: s.activation(tl[:], tl[:], AF.Ln, bias=1.0))
            p.op("scalar", lambda s, oc=oc: s.activation(tr[:], P1[:], AF.Relu, bias=bdt[:, oc:oc + 1]))
            p.op("vector", lambda v: v.tensor_tensor(dl[:], tl[:], tr[:], ALU.add))
            p.dma("sync", DLT[oc * 128:(oc + 1) * 128, csl], dl[:])
    es.close()
def emit_s5(nc, p, DR):
    DLT = DR["DLT"]
    UF = DR["UF"]
    SRES = DR["SRES"]
    BM = DR["BM"]
    CM = DR["CM"]
    ALOG = DR["ALOG"]
    DPV = DR["DPV"]
    WOUT = DR["WOUT"]
    X1F = DR["X1F"]
    ONESF = DR["ONESF"]
    S0F = DR["S0F"]

    SEL16 = DR["SEL16"]
    es = contextlib.ExitStack()
    sb = lambda n, s, d: es.enter_context(nc.sbuf_tensor('s5_' + n, s, d))
    ps = lambda n, s, d: es.enter_context(nc.psum_tensor('s5_' + n, s, d))
    anall = sb("anall", [128, 16 * DS], F32)
    dpv = sb("dpv", [128, 16], F32)
    wo = [sb(f"wo{i}", [128, D], BF16) for i in range(16)]
    sel16 = sb("sel16", [16, 16 * 128], F32)
    bm16 = sb("bm16", [16, T], F32)
    cm16 = sb("cm16", [16, T], F32)
    Bb = sb("Bb", [128, DS * 512], F32)
    Cb = sb("Cb", [128, DS * 512], F32)
    dltc = [sb(f"dltc{i}", [128, 512], F32) for i in range(2)]
    dudc = [sb(f"dudc{i}", [128, 512], F32) for i in range(2)]
    ufc = [sb(f"ufc{i}", [128, 512], F32) for i in range(2)]
    sresc = [sb(f"sresc{i}", [128, 512], F32) for i in range(2)]
    dA = [sb(f"dA{i}", [128, 512], F32) for i in range(2)]
    dbu = [sb(f"dbu{i}", [128, 512], F32) for i in range(2)]
    hbuf = [sb(f"hbuf{i}", [128, 512], F32) for i in range(2)]
    prod = [sb(f"prod{i}", [128, 512], F32) for i in range(2)]
    yc = sb("yc", [128, 512], F32)
    hstates = sb("hstates", [128, 16 * DS], F32)
    ygb = sb("ygb", [128, 16 * 512], BF16)
    x1t = [sb(f"x1t{i}", [128, 512], F32) for i in range(2)]
    s0t = [sb(f"s0t{i}", [128, 512], F32) for i in range(2)]
    PA = [ps(f"PA{i}", [128, 512], F32) for i in range(2)]
    PW = [ps(f"PW{i}", [128, 512], F32) for i in range(2)]

    p.dma("sync", anall[:], ALOG)
    p.op("scalar", lambda s: s.activation(anall[:], anall[:], AF.Exp))
    p.op("vector", lambda v: v.tensor_scalar(anall[:], anall[:], -1.0, None, ALU.mult))
    p.dma("sync", dpv[:], DPV)
    p.dma("sync", sel16[:], SEL16)
    for i in range(16):
        p.dma("gpsimd", wo[i][:], WOUT[i * 128:(i + 1) * 128, :])
    for b in range(B):
        p.dma("sync", bm16[:], BM[b])
        p.dma("sync", cm16[:], CM[b])
        p.op("vector", lambda v: v.memset(hstates[:], 0.0))
        for TC in range(4):
            tsl = slice(b * T + TC * 512, b * T + (TC + 1) * 512)
            lsl = slice(TC * 512, (TC + 1) * 512)
            for n in range(DS):
                r = n % 2
                p.op("tensor", lambda t, n=n, r=r: t.matmul(
                    PA[r][:], sel16[:, n * 128:(n + 1) * 128], bm16[:, lsl], start=True, stop=True))
                p.op("vector", lambda v, n=n, r=r: v.tensor_copy(Bb[:, n * 512:(n + 1) * 512], PA[r][:]))
                p.op("tensor", lambda t, n=n, r=r: t.matmul(
                    PW[r][:], sel16[:, n * 128:(n + 1) * 128], cm16[:, lsl], start=True, stop=True))
                p.op("vector", lambda v, n=n, r=r: v.tensor_copy(Cb[:, n * 512:(n + 1) * 512], PW[r][:]))
            for ct in range(16):
                rc = ct % 2
                rs = slice(ct * 128, (ct + 1) * 128)
                p.dma("sync", dltc[rc][:], DLT[rs, tsl])
                p.dma("sync", ufc[rc][:], UF[rs, tsl])
                p.dma("sync", sresc[rc][:], SRES[rs, tsl])
                p.op("vector", lambda v, rc=rc: v.tensor_tensor(dudc[rc][:], dltc[rc][:], ufc[rc][:], ALU.mult))
                for n in range(DS):
                    r = n % 2
                    nsl = slice(n * 512, (n + 1) * 512)
                    hs = slice(ct * DS + n, ct * DS + n + 1)
                    asl = slice(ct * DS + n, ct * DS + n + 1)
                    p.op("scalar", lambda s, r=r, rc=rc, asl=asl: s.activation(
                        dA[r][:], dltc[rc][:], AF.Exp, scale=anall[:, asl]))
                    p.op("vector", lambda v, r=r, rc=rc, nsl=nsl: v.tensor_tensor(
                        dbu[r][:], dudc[rc][:], Bb[:, nsl], ALU.mult))
                    p.op("vector", lambda v, r=r, hs=hs: v.tensor_tensor_scan(
                        hbuf[r][:], dA[r][:], dbu[r][:], hstates[:, hs], ALU.mult, ALU.add))
                    p.op("vector", lambda v, r=r, hs=hs: v.tensor_copy(hstates[:, hs], hbuf[r][:, 511:512]))
                    if n == 0:
                        p.op("vector", lambda v, r=r, nsl=nsl: v.tensor_tensor(yc[:], hbuf[r][:], Cb[:, nsl], ALU.mult))
                    else:
                        p.op("vector", lambda v, r=r, nsl=nsl: v.tensor_tensor(prod[r][:], hbuf[r][:], Cb[:, nsl], ALU.mult))
                        p.op("vector", lambda v, r=r: v.tensor_tensor(yc[:], yc[:], prod[r][:], ALU.add))
                p.op("vector", lambda v, ct=ct, rc=rc: v.scalar_tensor_tensor(
                    yc[:], ufc[rc][:], dpv[:, ct:ct + 1], yc[:], ALU.mult, ALU.add))
                p.op("vector", lambda v, ct=ct, rc=rc: v.tensor_tensor(
                    ygb[:, ct * 512:(ct + 1) * 512], yc[:], sresc[rc][:], ALU.mult))
            for oc in range(8):
                ro = oc % 2
                for ct in range(16):
                    p.op("tensor", lambda t, oc=oc, ct=ct, ro=ro: t.matmul(
                        PW[ro][:], wo[ct][:, oc * 128:(oc + 1) * 128],
                        ygb[:, ct * 512:(ct + 1) * 512], start=(ct == 0), stop=(ct == 15)))
                p.dma("sync", x1t[ro][:], X1F[oc * 128:(oc + 1) * 128, tsl])
                p.op("vector", lambda v, ro=ro: v.tensor_tensor(s0t[ro][:], PW[ro][:], x1t[ro][:], ALU.add))
                p.dma("sync", S0F[oc * 128:(oc + 1) * 128, tsl], s0t[ro][:])
    es.close()
def emit_s6(nc, p, DR):
    XNB = DR["XNB"]
    WF = DR["WF"]
    FB = DR["FB"]
    VR = DR["VR"]
    CT = DR["CT"]
    S0F = DR["S0F"]
    XF = DR["XF"]
    X2F = DR["X2F"]
    X2B = DR["X2B"]

    es = contextlib.ExitStack()
    sb = lambda n, s, d: es.enter_context(nc.sbuf_tensor('s6_' + n, s, d))
    ps = lambda n, s, d: es.enter_context(nc.psum_tensor('s6_' + n, s, d))
    wf = [sb(f"wf{i}", [128, D], BF16) for i in range(8)]
    vr = [sb(f"vr{i}", [128, 32], BF16) for i in range(8)]
    ct = sb("ct", [32, D], BF16)
    fb = sb("fb", [128, 8], F32)
    xnb = [sb(f"xnb{i}", [128, 512], BF16) for i in range(8)]
    t1b = sb("t1b", [32, 512], BF16)
    gt = sb("gt", [128, 512], F32)
    s0t = sb("s0t", [128, 512], F32)
    xft = sb("xft", [128, 512], F32)
    x2t = sb("x2t", [128, 512], F32)
    x2b = sb("x2b", [128, 512], BF16)
    P1 = ps("P1", [128, 512], F32)
    P32 = ps("P32", [32, 512], F32)

    for i in range(8):
        p.dma("gpsimd", wf[i][:], WF[i * 128:(i + 1) * 128, :])
        p.dma("gpsimd", vr[i][:], VR[i * 128:(i + 1) * 128, :])
        p.dma("sync", fb[:, i:i + 1], FB[i])
    p.dma("gpsimd", ct[:], CT)
    for tc in range(8):
        csl = slice(tc * 512, (tc + 1) * 512)
        for i in range(8):
            p.dma("sync", xnb[i][:], XNB[i * 128:(i + 1) * 128, csl])
        for kc in range(8):
            p.op("tensor", lambda t, kc=kc: t.matmul(
                P32[:], vr[kc][:], xnb[kc][:], start=(kc == 0), stop=(kc == 7)))
        p.op("vector", lambda v: v.tensor_copy(t1b[:], P32[:]))
        for oc in range(8):
            for kc in range(8):
                p.op("tensor", lambda t, oc=oc, kc=kc: t.matmul(
                    P1[:], wf[kc][:, oc * 128:(oc + 1) * 128], xnb[kc][:],
                    start=(kc == 0), stop=False))
            p.op("tensor", lambda t, oc=oc: t.matmul(
                P1[:], ct[:, oc * 128:(oc + 1) * 128], t1b[:], start=False, stop=True))
            p.op("scalar", lambda s, oc=oc: s.activation(gt[:], P1[:], AF.Silu, bias=fb[:, oc:oc + 1]))
            p.dma("sync", s0t[:], S0F[oc * 128:(oc + 1) * 128, csl])
            p.dma("sync", xft[:], XF[oc * 128:(oc + 1) * 128, csl])
            p.op("vector", lambda v: v.tensor_tensor(gt[:], gt[:], s0t[:], ALU.mult))
            p.op("vector", lambda v: v.tensor_tensor(x2t[:], gt[:], xft[:], ALU.add))
            p.op("vector", lambda v: v.tensor_copy(x2b[:], x2t[:]))
            p.dma("sync", X2F[oc * 128:(oc + 1) * 128, csl], x2t[:])
            p.dma("sync", X2B[oc * 128:(oc + 1) * 128, csl], x2b[:])
    es.close()
def emit_s7(nc, p, DR):
    X2F = DR["X2F"]
    identf = DR["identf"]
    XN2B = DR["XN2B"]

    es = contextlib.ExitStack()
    sb = lambda n, s, d: es.enter_context(nc.sbuf_tensor('s7_' + n, s, d))
    ps = lambda n, s, d: es.enter_context(nc.psum_tensor('s7_' + n, s, d))
    idf = sb("idf", [128, 128], F32)
    fmt = sb("fmt", [128, 128], F32)
    xtm = sb("xtm", [128, D], F32)
    xsq = sb("xsq", [128, D], F32)
    xn = sb("xn", [128, D], F32)
    s1 = sb("s1", [128, 1], F32)
    s2 = sb("s2", [128, 1], F32)
    mean = sb("mean", [128, 1], F32)
    ex2 = sb("ex2", [128, 1], F32)
    msq = sb("msq", [128, 1], F32)
    var = sb("var", [128, 1], F32)
    rstd = sb("rstd", [128, 1], F32)
    epsc = sb("epsc", [128, 1], F32)
    mr = sb("mr", [128, 1], F32)
    evb = sb("evb", [128, 128], BF16)
    pT = ps("pT", [128, 128], F32)

    p.op("vector", lambda v: v.memset(epsc[:], 1e-5))
    p.dma("sync", idf[:], identf)
    for tt in range(TT // 128):
        for dc in range(8):
            p.dma("sync", fmt[:], X2F[dc * 128:(dc + 1) * 128, tt * 128:(tt + 1) * 128])
            p.op("tensor", lambda t: t.transpose(pT[:], fmt[:], idf[:]))
            p.op("vector", lambda v, dc=dc: v.tensor_copy(xtm[:, dc * 128:(dc + 1) * 128], pT[:]))
        p.op("vector", lambda v: v.tensor_reduce(s1[:], xtm[:], AX.X, ALU.add))
        p.op("scalar", lambda s: s.activation(xsq[:], xtm[:], AF.Square))
        p.op("vector", lambda v: v.tensor_reduce(s2[:], xsq[:], AX.X, ALU.add))
        p.op("vector", lambda v: v.tensor_scalar(mean[:], s1[:], 1.0 / D, None, ALU.mult))
        p.op("vector", lambda v: v.tensor_scalar(ex2[:], s2[:], 1.0 / D, None, ALU.mult))
        p.op("scalar", lambda s: s.activation(msq[:], mean[:], AF.Square))
        p.op("vector", lambda v: v.tensor_tensor(var[:], ex2[:], msq[:], ALU.subtract))
        p.op("scalar", lambda s: s.activation(var[:], var[:], AF.Ln, bias=epsc[:]))
        p.op("scalar", lambda s: s.activation(rstd[:], var[:], AF.Exp, scale=-0.5))
        p.op("vector", lambda v: v.tensor_tensor(mr[:], mean[:], rstd[:], ALU.mult))
        p.op("vector", lambda v: v.tensor_scalar(xn[:], xtm[:], rstd[:], mr[:], ALU.mult, ALU.subtract))
        for dc in range(8):
            p.op("tensor", lambda t, dc=dc: t.transpose(pT[:], xn[:, dc * 128:(dc + 1) * 128], idf[:]))
            p.op("scalar", lambda s: s.copy(evb[:], pT[:]))
            p.dma("sync", XN2B[dc * 128:(dc + 1) * 128, tt * 128:(tt + 1) * 128], evb[:])
    es.close()
def emit_s8(nc, p, DR):
    XN2B = DR["XN2B"]
    WQKVG = DR["WQKVG"]
    WO1 = DR["WO1"]
    BO = DR["BO"]
    BG1 = DR["BG1"]
    DCP = DR["DCP"]
    IOTAM = DR["IOTAM"]
    TRIU = DR["TRIU"]
    IOTP1 = DR["IOTP1"]
    IOT127 = DR["IOT127"]
    C128 = DR["C128"]
    ONESF = DR["ONESF"]
    identb = DR["identb"]
    S1F = DR["S1F"]

    es = contextlib.ExitStack()
    sb = lambda n, s, d: es.enter_context(nc.sbuf_tensor('s8_' + n, s, d))
    ps = lambda n, s, d: es.enter_context(nc.psum_tensor('s8_' + n, s, d))
    wqg = [sb(f"wqg{i}", [128, 193], BF16) for i in range(8)]
    wo1 = sb("wo1", [64, D], BF16)
    bo = sb("bo", [128, 8], F32)
    bg1 = sb("bg1", [1, 1], F32)
    dcp = sb("dcp", [1, 1], F32)
    iotam = sb("iotam", [128, 128], F32)
    triu = sb("triu", [128, 128], F32)
    iotp1 = sb("iotp1", [128, 1], F32)
    iot127 = sb("iot127", [128, 1], F32)
    c128 = sb("c128", [1, 1], F32)
    onesf = sb("onesf", [1, 128], F32)
    idb = sb("idb", [128, 128], BF16)
    dval = sb("dval", [1, 1], F32)
    ldv = sb("ldv", [1, 1], F32)
    ldcol = sb("ldcol", [128, 1], F32)
    Mb = sb("Mb", [128, 128], BF16)
    Mf = sb("Mf", [128, 128], F32)
    apow = sb("apow", [128, 1], F32)
    bpow = sb("bpow", [128, 1], F32)
    d128 = sb("d128", [1, 1], F32)
    d128c = sb("d128c", [64, 1], F32)
    xnb = [sb(f"xnb{i}", [128, 512], BF16) for i in range(8)]
    qf = sb("qf", [64, T], BF16)
    kf = sb("kf", [64, T], BF16)
    vf = sb("vf", [64, T], BF16)
    gr = sb("gr", [1, T], F32)
    gball = sb("gball", [64, T], F32)
    gkb = sb("gkb", [64, T], BF16)
    gvb = sb("gvb", [64, T], BF16)
    gktm = sb("gktm", [128, 16 * 64], BF16)
    gvtm = sb("gvtm", [128, 16 * 64], BF16)
    Sst = sb("Sst", [64, 64], F32)
    S2b = sb("S2b", [64, 64], F32)
    Sbf = sb("Sbf", [64, 64], BF16)
    o1 = sb("o1", [128, 64], F32)
    amd = sb("amd", [128, 128], BF16)
    otm = sb("otm", [128, 16 * 64], BF16)
    ofmb = sb("ofmb", [64, T], BF16)
    s1t = sb("s1t", [128, 512], F32)
    PA = ps("PA", [128, 512], F32)
    PB65 = ps("PB65", [65, 512], F32)
    PC = ps("PC", [128, 128], F32)
    PTb = ps("PTb", [128, 128], BF16)

    for i in range(8):
        p.dma("gpsimd", wqg[i][:], WQKVG[i * 128:(i + 1) * 128, :])
    p.dma("gpsimd", wo1[:], WO1)
    for i in range(8):
        p.dma("sync", bo[:, i:i + 1], BO[i])
    for src, dst in [(BG1, bg1), (DCP, dcp), (IOTAM, iotam), (TRIU, triu),
                     (IOTP1, iotp1), (IOT127, iot127), (C128, c128), (ONESF, onesf)]:
        p.dma("sync", dst[:], src)
    p.dma("gpsimd", idb[:], identb)
    p.op("scalar", lambda s: s.activation(dval[:], dcp[:], AF.Sigmoid))
    p.op("vector", lambda v: v.tensor_scalar(dval[:], dval[:], 0.099, 0.9, ALU.mult, ALU.add))
    p.op("scalar", lambda s: s.activation(ldv[:], dval[:], AF.Ln))
    p.op("tensor", lambda t: t.matmul(PC[:, 0:1], onesf[:], ldv[:], start=True, stop=True))
    p.op("vector", lambda v: v.tensor_copy(ldcol[:], PC[:, 0:1]))
    p.op("scalar", lambda s: s.activation(Mf[:], iotam[:], AF.Exp, scale=ldcol[:]))
    p.op("vector", lambda v: v.tensor_tensor(Mf[:], Mf[:], triu[:], ALU.mult))
    p.op("vector", lambda v: v.tensor_copy(Mb[:], Mf[:]))
    p.op("scalar", lambda s: s.activation(apow[:], iotp1[:], AF.Exp, scale=ldcol[:]))
    p.op("scalar", lambda s: s.activation(bpow[:], iot127[:], AF.Exp, scale=ldcol[:]))
    p.op("scalar", lambda s: s.activation(d128[:], c128[:], AF.Exp, scale=ldv[:]))
    p.op("tensor", lambda t: t.matmul(PC[0:64, 0:1], onesf[:, 0:64], d128[:], start=True, stop=True))
    p.op("vector", lambda v: v.tensor_copy(d128c[:], PC[0:64, 0:1]))
    for b in range(B):
        for tc in range(4):
            csl = slice(b * T + tc * 512, b * T + (tc + 1) * 512)
            lsl = slice(tc * 512, (tc + 1) * 512)
            for i in range(8):
                p.dma("sync", xnb[i][:], XN2B[i * 128:(i + 1) * 128, csl])
            for kc in range(8):
                p.op("tensor", lambda t, kc=kc: t.matmul(
                    PA[:], wqg[kc][:, 0:128], xnb[kc][:], start=(kc == 0), stop=(kc == 7)))
            for kc in range(8):
                p.op("tensor", lambda t, kc=kc: t.matmul(
                    PB65[:], wqg[kc][:, 128:193], xnb[kc][:], start=(kc == 0), stop=(kc == 7)))
            p.op("vector", lambda v, lsl=lsl: v.tensor_copy(qf[:, lsl], PA[0:64, :]))
            p.op("vector", lambda v, lsl=lsl: v.tensor_copy(kf[:, lsl], PA[64:128, :]))
            p.op("vector", lambda v, lsl=lsl: v.tensor_copy(vf[:, lsl], PB65[0:64, :]))
            p.op("scalar", lambda s, lsl=lsl: s.activation(gr[:, lsl], PB65[64:65, :], AF.Sigmoid, bias=bg1[:]))
        for tc in range(4):
            lsl = slice(tc * 512, (tc + 1) * 512)
            p.op("tensor", lambda t, lsl=lsl: t.matmul(PB65[0:64, :], onesf[:, 0:64], gr[:, lsl], start=True, stop=True))
            p.op("vector", lambda v, lsl=lsl: v.tensor_copy(gball[:, lsl], PB65[0:64, :]))
        p.op("vector", lambda v: v.tensor_tensor(gkb[:], kf[:], gball[:], ALU.mult))
        p.op("vector", lambda v: v.tensor_tensor(gvb[:], vf[:], gball[:], ALU.mult))
        for ch in range(16):
            cs = slice(ch * 128, (ch + 1) * 128)
            os_ = slice(ch * 64, (ch + 1) * 64)
            p.op("tensor", lambda t, cs=cs: t.transpose(PTb[:, 0:64], gkb[:, cs], idb[0:64, 0:64]))
            p.op("vector", lambda v, os_=os_: v.tensor_scalar(gktm[:, os_], PTb[:, 0:64], bpow[:], None, ALU.mult))
            p.op("tensor", lambda t, cs=cs: t.transpose(PTb[:, 0:64], gvb[:, cs], idb[0:64, 0:64]))
            p.op("vector", lambda v, os_=os_: v.tensor_copy(gvtm[:, os_], PTb[:, 0:64]))
        p.op("vector", lambda v: v.memset(Sst[:], 0.0))
        for ch in range(16):
            cs = slice(ch * 128, (ch + 1) * 128)
            os_ = slice(ch * 64, (ch + 1) * 64)
            p.op("vector", lambda v: v.tensor_copy(Sbf[:], Sst[:]))
            p.op("tensor", lambda t, cs=cs: t.matmul(PC[:, 0:64], qf[:, cs], Sbf[:], start=True, stop=True))
            p.op("vector", lambda v: v.tensor_scalar(o1[:], PC[:, 0:64], apow[:], None, ALU.mult))
            p.op("tensor", lambda t, cs=cs: t.matmul(PC[:], gkb[:, cs], qf[:, cs], start=True, stop=True))
            p.op("vector", lambda v: v.tensor_tensor(amd[:], PC[:], Mb[:], ALU.mult))
            p.op("tensor", lambda t, os_=os_: t.matmul(PA[:, 0:64], amd[:], gvtm[:, os_], start=True, stop=True))
            p.op("vector", lambda v, os_=os_: v.tensor_tensor(otm[:, os_], PA[:, 0:64], o1[:], ALU.add))
            p.op("vector", lambda v: v.tensor_scalar(S2b[:], Sst[:], d128c[:], None, ALU.mult))
            p.op("tensor", lambda t, os_=os_: t.matmul(PC[0:64, 0:64], gktm[:, os_], gvtm[:, os_], start=True, stop=True))
            p.op("vector", lambda v: v.tensor_tensor(Sst[:], S2b[:], PC[0:64, 0:64], ALU.add))
        for ch in range(16):
            cs = slice(ch * 128, (ch + 1) * 128)
            os_ = slice(ch * 64, (ch + 1) * 64)
            p.op("tensor", lambda t, os_=os_: t.transpose(PTb[0:64, :], otm[:, os_], idb[:]))
            p.op("vector", lambda v, cs=cs: v.tensor_copy(ofmb[:, cs], PTb[0:64, :]))
        for tc in range(4):
            csl = slice(b * T + tc * 512, b * T + (tc + 1) * 512)
            lsl = slice(tc * 512, (tc + 1) * 512)
            for oc in range(8):
                p.op("tensor", lambda t, oc=oc, lsl=lsl: t.matmul(
                    PA[:], wo1[:, oc * 128:(oc + 1) * 128], ofmb[:, lsl], start=True, stop=True))
                p.op("vector", lambda v, oc=oc: v.tensor_scalar(s1t[:], PA[:], bo[:, oc:oc + 1], None, ALU.add))
                p.dma("sync", S1F[oc * 128:(oc + 1) * 128, csl], s1t[:])
    es.close()
def emit_s9(nc, p, DR):
    XN2B = DR["XN2B"]
    X2F = DR["X2F"]
    S1F = DR["S1F"]
    W1T = DR["W1T"]
    W2T = DR["W2T"]
    B1 = DR["B1"]
    B2 = DR["B2"]
    WGT = DR["WGT"]
    BGM = DR["BGM"]
    IOTA8 = DR["IOTA8"]
    IOTA8P = DR["IOTA8P"]
    SEL = DR["SEL"]
    identf = DR["identf"]
    OUT = DR["OUT"]
    YACC = DR["YACC"]

    es = contextlib.ExitStack()
    sb = lambda n, s, d: es.enter_context(nc.sbuf_tensor('s9_' + n, s, d))
    ps = lambda n, s, d: es.enter_context(nc.psum_tensor('s9_' + n, s, d))
    wgt = [sb(f"wgt{i}", [128, 8], BF16) for i in range(8)]
    bgm = sb("bgm", [8, 1], F32)
    iota8 = sb("iota8", [128, 8], F32)
    iota8p = sb("iota8p", [128, 8], F32)
    sel = sb("sel", [8, 8 * 128], F32)
    idf = sb("idf", [128, 128], F32)
    xnball = [sb(f"xnball{i}", [128, 8 * 512], BF16) for i in range(2)]
    lgf = sb("lgf", [8, 512], F32)
    lgt = sb("lgt", [128, 8], F32)
    m1 = sb("m1", [128, 1], F32)
    d1 = sb("d1", [128, 8], F32)
    ind1p = sb("ind1p", [128, 8], F32)
    penal = sb("penal", [128, 8], F32)
    idx = sb("idx", [128, 1], F32)
    ind1 = sb("ind1", [128, 8], F32)
    lg2 = sb("lg2", [128, 8], F32)
    m2 = sb("m2", [128, 1], F32)
    ind2 = sb("ind2", [128, 8], F32)
    wdiff = sb("wdiff", [128, 1], F32)
    w1c = sb("w1c", [128, 1], F32)
    w2c = sb("w2c", [128, 1], F32)
    comb = sb("comb", [128, 8], F32)
    combfm = sb("combfm", [8, TT], F32)
    w1tb = [[sb(f"w1tb{r}_{i}", [128, HID], BF16) for i in range(8)] for r in range(2)]
    w2tb = [sb(f"w2tb{i}", [128, D], BF16) for i in range(16)]
    b1all = sb("b1all", [128, NE * 16], F32)
    b2all = sb("b2all", [128, NE * 8], F32)
    hba = [sb(f"hba{r}", [128, 16 * 512], BF16) for r in range(2)]
    cbet = [sb(f"cbet{r}", [128, 512], F32) for r in range(2)]
    yt = [sb(f"yt{r}", [128, 512], F32) for r in range(2)]
    ytw = [sb(f"ytw{r}", [128, 512], F32) for r in range(2)]
    ya = [sb(f"ya{r}", [128, 512], F32) for r in range(2)]
    g1t = [sb(f"g1t{r}", [128, 512], F32) for r in range(2)]
    s1t = [sb(f"s1t{r}", [128, 512], F32) for r in range(2)]
    x2t = [sb(f"x2t{r}", [128, 512], F32) for r in range(2)]
    x3t = [sb(f"x3t{r}", [128, 512], F32) for r in range(2)]
    ot = [sb(f"ot{r}", [128, 512], F32) for r in range(2)]
    P1 = [ps(f"P1{r}", [128, 512], F32) for r in range(2)]
    P8 = ps("P8", [8, 512], F32)
    Pcb = [ps(f"Pcb{r}", [128, 512], F32) for r in range(2)]
    PT = [ps(f"PT{r}", [128, 128], F32) for r in range(2)]

    def xn_load(r, csl):
        p.dma("sync", xnball[r][:].rearrange("p (c n) -> p c n", c=8),
              XN2B[:, csl].rearrange("(c p) n -> p c n", p=128))

    for i in range(8):
        p.dma("gpsimd", wgt[i][:], WGT[i * 128:(i + 1) * 128, :])
    for src, dst in [(BGM, bgm), (IOTA8, iota8), (IOTA8P, iota8p), (SEL, sel),
                     (identf, idf), (B1, b1all), (B2, b2all)]:
        p.dma("sync", dst[:], src)
    for tc in range(8):
        rt = tc % 2
        csl = slice(tc * 512, (tc + 1) * 512)
        xn_load(rt, csl)
        for kc in range(8):
            p.op("tensor", lambda t, kc=kc, rt=rt: t.matmul(
                P8[:], wgt[kc][:], xnball[rt][:, kc * 512:(kc + 1) * 512],
                start=(kc == 0), stop=(kc == 7)))
        p.op("vector", lambda v: v.tensor_scalar(lgf[:], P8[:], bgm[:], None, ALU.add))
        for q in range(4):
            tt = tc * 4 + q
            rq = q % 2
            p.op("tensor", lambda t, q=q, rq=rq: t.transpose(PT[rq][:, 0:8], lgf[:, q * 128:(q + 1) * 128], idf[0:8, 0:8]))
            p.op("vector", lambda v, rq=rq: v.tensor_copy(lgt[:], PT[rq][:, 0:8]))
            p.op("vector", lambda v: v.tensor_reduce(m1[:], lgt[:], AX.X, ALU.max))
            p.op("vector", lambda v: v.tensor_scalar(d1[:], lgt[:], m1[:], None, ALU.subtract))
            p.op("vector", lambda v: v.tensor_scalar(ind1p[:], d1[:], 0.0, None, ALU.is_ge))
            p.op("vector", lambda v: v.scalar_tensor_tensor(penal[:], ind1p[:], -1000.0, iota8p[:], ALU.mult, ALU.add))
            p.op("vector", lambda v: v.tensor_reduce(idx[:], penal[:], AX.X, ALU.min))
            p.op("vector", lambda v: v.tensor_scalar(ind1[:], iota8[:], idx[:], None, ALU.subtract))
            p.op("scalar", lambda s: s.activation(ind1[:], ind1[:], AF.Abs))
            p.op("scalar", lambda s: s.activation(ind1[:], ind1[:], AF.Relu, scale=-1.0, bias=1.0))
            p.op("vector", lambda v: v.scalar_tensor_tensor(lg2[:], ind1[:], -1e30, lgt[:], ALU.mult, ALU.add))
            p.op("vector", lambda v: v.tensor_reduce(m2[:], lg2[:], AX.X, ALU.max))
            p.op("vector", lambda v: v.tensor_scalar(d1[:], lg2[:], m2[:], None, ALU.subtract))
            p.op("vector", lambda v: v.tensor_scalar(ind1p[:], d1[:], 0.0, None, ALU.is_ge))
            p.op("vector", lambda v: v.scalar_tensor_tensor(penal[:], ind1p[:], -1000.0, iota8p[:], ALU.mult, ALU.add))
            p.op("vector", lambda v: v.tensor_reduce(idx[:], penal[:], AX.X, ALU.min))
            p.op("vector", lambda v: v.tensor_scalar(ind2[:], iota8[:], idx[:], None, ALU.subtract))
            p.op("scalar", lambda s: s.activation(ind2[:], ind2[:], AF.Abs))
            p.op("scalar", lambda s: s.activation(ind2[:], ind2[:], AF.Relu, scale=-1.0, bias=1.0))
            p.op("vector", lambda v: v.tensor_tensor(wdiff[:], m1[:], m2[:], ALU.subtract))
            p.op("scalar", lambda s: s.activation(w1c[:], wdiff[:], AF.Sigmoid))
            p.op("vector", lambda v: v.tensor_scalar(w2c[:], w1c[:], -1.0, 1.0, ALU.mult, ALU.add))
            p.op("vector", lambda v: v.tensor_scalar(comb[:], ind1[:], w1c[:], None, ALU.mult))
            p.op("vector", lambda v: v.scalar_tensor_tensor(comb[:], ind2[:], w2c[:], comb[:], ALU.mult, ALU.add))
            p.op("tensor", lambda t, rq=rq: t.transpose(PT[rq][0:8, :], comb[:], idf[:]))
            p.op("vector", lambda v, tt=tt, rq=rq: v.tensor_copy(combfm[:, tt * 128:(tt + 1) * 128], PT[rq][0:8, :]))
    for e in range(NE):
        we = e % 2
        for i in range(8):
            p.dma("gpsimd", w1tb[we][i][:], W1T[e, i * 128:(i + 1) * 128, :])
        for i in range(16):
            p.dma("gpsimd", w2tb[i][:], W2T[e, i * 128:(i + 1) * 128, :])
        for tc in range(8):
            it = e * 8 + tc
            rt = it % 2
            csl = slice(tc * 512, (tc + 1) * 512)
            xn_load(rt, csl)
            p.op("tensor", lambda t, e=e, csl=csl, rt=rt: t.matmul(
                Pcb[rt][:], sel[:, e * 128:(e + 1) * 128], combfm[:, csl], start=True, stop=True))
            p.op("vector", lambda v, rt=rt: v.tensor_copy(cbet[rt][:], Pcb[rt][:]))
            for hc in range(16):
                rh = hc % 2
                for kc in range(8):
                    p.op("tensor", lambda t, hc=hc, kc=kc, rh=rh, rt=rt, we=we: t.matmul(
                        P1[rh][:], w1tb[we][kc][:, hc * 128:(hc + 1) * 128],
                        xnball[rt][:, kc * 512:(kc + 1) * 512],
                        start=(kc == 0), stop=(kc == 7)))
                p.op("scalar", lambda s, hc=hc, rh=rh, rt=rt, e=e: s.activation(
                    hba[rt][:, hc * 512:(hc + 1) * 512], P1[rh][:], AF.Relu,
                    bias=b1all[:, e * 16 + hc:e * 16 + hc + 1]))
            for oc in range(8):
                ro = oc % 2
                for hc in range(16):
                    p.op("tensor", lambda t, oc=oc, hc=hc, ro=ro, rt=rt: t.matmul(
                        P1[ro][:], w2tb[hc][:, oc * 128:(oc + 1) * 128],
                        hba[rt][:, hc * 512:(hc + 1) * 512],
                        start=(hc == 0), stop=(hc == 15)))
                p.op("scalar", lambda s, oc=oc, ro=ro, e=e: s.activation(
                    yt[ro][:], P1[ro][:], AF.Identity,
                    bias=b2all[:, e * 8 + oc:e * 8 + oc + 1]))
                p.op("vector", lambda v, ro=ro, rt=rt: v.tensor_tensor(ytw[ro][:], yt[ro][:], cbet[rt][:], ALU.mult))
                if e == 0:
                    p.dma("gpsimd", YACC[oc * 128:(oc + 1) * 128, csl], ytw[ro][:])
                else:
                    p.dma("gpsimd", YACC[oc * 128:(oc + 1) * 128, csl], ytw[ro][:], accum_op=ALU.add)
    for tc in range(8):
        csl = slice(tc * 512, (tc + 1) * 512)
        for oc in range(8):
            ro = (tc * 8 + oc) % 2
            p.dma("sync", ya[ro][:], YACC[oc * 128:(oc + 1) * 128, csl])
            p.op("scalar", lambda s, ro=ro: s.activation(g1t[ro][:], ya[ro][:], AF.Silu))
            p.dma("sync", s1t[ro][:], S1F[oc * 128:(oc + 1) * 128, csl])
            p.dma("sync", x2t[ro][:], X2F[oc * 128:(oc + 1) * 128, csl])
            p.op("vector", lambda v, ro=ro: v.tensor_tensor(g1t[ro][:], g1t[ro][:], s1t[ro][:], ALU.mult))
            p.op("vector", lambda v, ro=ro: v.tensor_tensor(x3t[ro][:], g1t[ro][:], x2t[ro][:], ALU.add))
            for q in range(4):
                p.op("tensor", lambda t, q=q, ro=ro: t.transpose(PT[ro][:], x3t[ro][:, q * 128:(q + 1) * 128], idf[:]))
                p.op("scalar", lambda s, q=q, ro=ro: s.copy(ot[ro][:, q * 128:(q + 1) * 128], PT[ro][:]))
            p.dma("sync",
                  OUT[tc * 512:(tc + 1) * 512, oc * 128:(oc + 1) * 128].rearrange(
                      "(q p) n -> p q n", p=128),
                  ot[ro][:].rearrange("p (q n) -> p q n", q=4))
    es.close()




def declare_drams(nc, decls):
    D = {}
    for name, kind in decls:
        shape, dtype = DRAM_SPECS[name]
        D[name] = nc.dram_tensor(name, shape, dtype, kind=kind).ap()
    return D


def _finish2(nc, p):
    with contextlib.ExitStack() as es:
        sems = _make_sems(nc, p, es)
        with nc.Block() as block:
            p.finalize(block, sems)


EMITS = {}


def _mk_stage_builder(sname):
    def build():
        nc = new_nc()
        D = declare_drams(nc, STAGE_DECLS[sname])
        p = Prog(nc)
        EMITS[sname](nc, p, D)
        _finish2(nc, p)
        return nc
    build.__name__ = "build_" + sname
    return build


FUSED_INPUT_NAMES = None


def fused_decls():
    produced = set()
    consumed = set()
    order = []
    for sname in ["s1", "s2", "s3", "s4", "s5", "s6", "s7", "s8", "s9"]:
        for name, kind in STAGE_DECLS[sname]:
            if name not in produced and name not in consumed and name not in [n for n, _ in order]:
                order.append((name, None))
            if kind == "ExternalOutput":
                produced.add(name)
            elif kind == "ExternalInput":
                consumed.add(name)
    decls = []
    for name, _ in order:
        if name == "OUT":
            decls.append((name, "ExternalOutput"))
        elif name in produced:
            decls.append((name, "Internal"))
        elif name in consumed:
            decls.append((name, "ExternalInput"))
        else:
            decls.append((name, "Internal"))
    return decls


def build_fused():
    nc = new_nc()
    D = declare_drams(nc, fused_decls())
    p = Prog(nc)
    for sname in ["s1", "s2", "s3", "s4", "s5", "s6", "s7", "s8", "s9"]:
        EMITS[sname](nc, p, D)
    _finish2(nc, p)
    return nc


EMITS.update(s1=emit_s1, s2=emit_s2, s3=emit_s3, s4=emit_s4, s5=emit_s5, s6=emit_s6, s7=emit_s7, s8=emit_s8, s9=emit_s9)
build_s1 = _mk_stage_builder('s1')
build_s2 = _mk_stage_builder('s2')
build_s3 = _mk_stage_builder('s3')
build_s4 = _mk_stage_builder('s4')
build_s5 = _mk_stage_builder('s5')
build_s6 = _mk_stage_builder('s6')
build_s7 = _mk_stage_builder('s7')
build_s8 = _mk_stage_builder('s8')
build_s9 = _mk_stage_builder('s9')


def _f32(a):
    return np.ascontiguousarray(a, dtype=np.float32)


def make_consts():
    c = {}
    c["identf"] = np.eye(128, dtype=np.float32)
    c["identb"] = np.eye(128, dtype=np.float32).astype(BF)
    m = np.zeros((128, 4 * 512), dtype=np.float32)
    kl = np.arange(128)[:, None]
    ql = np.arange(512)[None, :]
    for rr in range(4):
        m[:, rr * 512:(rr + 1) * 512] = (ql >= kl + 128 * rr)
    c["MASKC"] = m.astype(BF)
    c["ONESF"] = np.ones((1, 128), dtype=np.float32)
    c["IOTAM"] = (np.arange(128)[None, :] - np.arange(128)[:, None]).astype(np.float32)
    c["TRIU"] = (np.arange(128)[None, :] >= np.arange(128)[:, None]).astype(np.float32)
    c["IOTP1"] = (np.arange(128)[:, None] + 1).astype(np.float32)
    c["IOT127"] = (127 - np.arange(128)[:, None]).astype(np.float32)
    c["C128"] = np.full((1, 1), 128.0, dtype=np.float32)
    c["IOTA8"] = np.broadcast_to(np.arange(8, dtype=np.float32), (128, 8)).copy()
    c["IOTA8P"] = c["IOTA8"] + 1000.0
    sel = np.zeros((8, 8 * 128), dtype=np.float32)
    for e in range(8):
        sel[e, e * 128:(e + 1) * 128] = 1.0
    c["SEL"] = sel
    sel16 = np.zeros((16, 16 * 128), dtype=np.float32)
    for n in range(16):
        sel16[n, n * 128:(n + 1) * 128] = 1.0
    c["SEL16"] = sel16
    return c


FUSED = True


def host_inputs(inputs):
    cst = make_consts()
    m = dict(cst)
    m["WQKV"] = _f32(inputs["L0_attn_wqkv"])
    m["BQ"] = _f32(inputs["L0_attn_bqkv"]).reshape(9, 128, 1)
    m["WAO"] = _f32(inputs["L0_attn_w_attn_out"])
    m["BAO"] = _f32(inputs["L0_attn_b_attn_out"]).reshape(8, 128, 1)
    m["WIN"] = _f32(inputs["L0_attn_w_in"])
    m["CW"] = _f32(inputs["L0_attn_conv_w"]).reshape(16, 128, 4)
    m["CB"] = _f32(inputs["L0_attn_conv_b"]).reshape(16, 128, 1)
    m["WXP"] = _f32(inputs["L0_attn_w_xproj"])
    m["WDT"] = _f32(inputs["L0_attn_w_dt"])
    m["BDT"] = _f32(inputs["L0_attn_b_dt"]).reshape(16, 128, 1)
    m["ALOG"] = np.ascontiguousarray(
        _f32(inputs["L0_attn_A_log"]).reshape(16, 128, DS).transpose(1, 0, 2).reshape(128, 16 * DS))
    m["DPV"] = np.ascontiguousarray(_f32(inputs["L0_attn_Dp"]).reshape(16, 128).T)
    m["WOUT"] = _f32(inputs["L0_attn_w_out"])
    m["WF"] = _f32(np.asarray(inputs["L0_ffn_W"]).T)
    m["FB"] = _f32(inputs["L0_ffn_b"]).reshape(8, 128, 1)
    m["VR"] = _f32(inputs["L0_ffn_V_r"])
    m["CT"] = _f32(np.asarray(inputs["L0_ffn_C"]).T)
    m["WQKVG"] = _f32(np.concatenate([inputs["L1_attn_Wq"], inputs["L1_attn_Wk"],
                                      inputs["L1_attn_Wv"], inputs["L1_attn_Wg"]], axis=1))
    m["WO1"] = _f32(inputs["L1_attn_Wo"])
    m["BO"] = _f32(inputs["L1_attn_bo"]).reshape(8, 128, 1)
    m["BG1"] = _f32(inputs["L1_attn_bg"]).reshape(1, 1)
    m["DCP"] = _f32(inputs["L1_attn_decay_param"]).reshape(1, 1)
    m["W1T"] = _f32(np.transpose(np.asarray(inputs["L1_ffn_W1"]), (0, 2, 1)))
    m["W2T"] = _f32(np.transpose(np.asarray(inputs["L1_ffn_W2"]), (0, 2, 1)))
    m["B1"] = np.ascontiguousarray(
        _f32(inputs["L1_ffn_b1"]).reshape(NE, 16, 128).transpose(2, 0, 1).reshape(128, NE * 16))
    m["B2"] = np.ascontiguousarray(
        _f32(inputs["L1_ffn_b2"]).reshape(NE, 8, 128).transpose(2, 0, 1).reshape(128, NE * 8))
    m["WGT"] = _f32(np.asarray(inputs["L1_ffn_Wg"]).T)
    m["BGM"] = _f32(inputs["L1_ffn_bg"]).reshape(NE, 1)
    return m


def kernel(**inputs):
    x = _f32(inputs["L0_attn_x"])
    shared_map = host_inputs(inputs)
    in_names = {n for n, k in fused_decls() if k == "ExternalInput"}
    base = {k: v for k, v in shared_map.items() if k in in_names}
    maps = []
    for c in range(NC):
        m = dict(base)
        m["XTM"] = np.ascontiguousarray(x[B * c:B * (c + 1)].reshape(TT, D))
        maps.append(m)
    res = _run(build_fused, maps)
    out = np.empty((16, 2048, 1024), dtype=np.float32)
    for c in range(NC):
        out[B * c:B * (c + 1)] = np.asarray(res[c]["OUT"], dtype=np.float32).reshape(B, T, D)
    return out
